# version 11
# baseline (speedup 1.0000x reference)
import os
import numpy as np
import ml_dtypes
import concourse.bass as bass
import concourse.mybir as mybir
from concourse.bass_utils import run_bass_kernel_spmd

F32 = mybir.dt.float32
BF16 = mybir.dt.bfloat16
AF = mybir.ActivationFunctionType
ALU = mybir.AluOpType
AX = mybir.AxisListType
AP = bass.AP

MAGIC = 12582912.0
EPS_LN = 1e-5
EPS_Q = 1e-5
ISQ128 = 0.08838834764831845
SQ2048 = 45.254833995939045
SQ8192 = 90.50966799187809

B, S, E = 4, 4096, 2048
H, D, G = 16, 128, 4
TO = 2048
FI = 8192
MASK_Z = 3968
MASK_W = MASK_Z + 2048

DEV_STAGE = int(os.environ.get("MK_DEV_STAGE", "99"))
NOEPI = os.environ.get("MK_NOEPI", "0") == "1"
NOBOUNCE = os.environ.get("MK_NOBOUNCE", "0") == "1"
NOMASK = os.environ.get("MK_NOMASK", "0") == "1"
NOEXP = os.environ.get("MK_NOEXP", "0") == "1"
NOEVH = os.environ.get("MK_NOEVH", "0") == "1"
ENGS = ("sync", "scalar", "vector", "tensor", "gpsimd")


def _hadamard(n):
    h = np.array([[1.0]], dtype=np.float32)
    while h.shape[0] < n:
        h = np.block([[h, h], [h, -h]])
    return h


class Ctr:
    def __init__(self, nc, name):
        self.h = nc.alloc_semaphore(name=name)
        self.v = 0


class Prog:

    def __init__(self, nc):
        self.nc = nc
        self.q = {k: [] for k in ENGS}

    def o(self, eng, build, waits=(), incs=()):
        wl = [(c.h, (c.v if v is None else v)) for (c, v) in waits]
        il = []
        for (c, n) in incs:
            c.v += n
            il.append((c.h, n))

        def thunk(e, wl=wl, il=il, build=build):
            for (h, v) in wl:
                if v > 0:
                    e.wait_ge(h, v)
            ins = build(e)
            for (h, n) in il:
                if ins is None:
                    ins = e.engine_nop() if hasattr(e, "engine_nop") else e.nop()
                ins.then_inc(h, n)
        self.q[eng].append(thunk)

    def flush(self, name):
        nc = self.nc
        with nc.Block(name) as blk:
            for eng in ENGS:
                th = self.q[eng]
                if th:
                    getattr(blk, eng)(lambda e, th=th: [t(e) for t in th])
        self.q = {k: [] for k in ENGS}


def build(nc):
    dbg = DEV_STAGE < 99
    xf = nc.dram_tensor("x_full", [S, E], F32, kind="ExternalInput")
    xo = nc.dram_tensor("x_own", [TO, E], F32, kind="ExternalInput")
    Wqh = nc.dram_tensor("Wq", [H, E, D], F32, kind="ExternalInput")
    Wkh = nc.dram_tensor("Wk", [G, E, D], F32, kind="ExternalInput")
    Wvh = nc.dram_tensor("Wv", [G, E, D], F32, kind="ExternalInput")
    bqh = nc.dram_tensor("bqT", [D, H], F32, kind="ExternalInput")
    bkh = nc.dram_tensor("bkT", [D, G], F32, kind="ExternalInput")
    bvh = nc.dram_tensor("bvT", [D, G], F32, kind="ExternalInput")
    wgh = nc.dram_tensor("wgT", [E, FI], F32, kind="ExternalInput")
    wuh = nc.dram_tensor("wuT", [E, FI], F32, kind="ExternalInput")
    wdh = nc.dram_tensor("wdT", [FI, E], F32, kind="ExternalInput")
    maskh = nc.dram_tensor("maskC", [128, MASK_W], BF16, kind="ExternalInput")
    cH128 = nc.dram_tensor("cH128", [128, 128], BF16, kind="ExternalInput")
    cH16p = nc.dram_tensor("cH16p", [128, 128], BF16, kind="ExternalInput")
    cH64p = nc.dram_tensor("cH64p", [128, 128], BF16, kind="ExternalInput")
    cIDf = nc.dram_tensor("cIDf", [128, 128], F32, kind="ExternalInput")
    cIDb = nc.dram_tensor("cIDb", [128, 128], BF16, kind="ExternalInput")
    conesb = nc.dram_tensor("conesb", [128, 1], BF16, kind="ExternalInput")

    out_h = nc.dram_tensor("out", [TO, E], F32, kind="ExternalOutput")
    ik = "ExternalOutput" if dbg else "Internal"
    x1h = nc.dram_tensor("x1h", [TO, E], F32, kind=ik)
    u1h = nc.dram_tensor("u1h", [16, 128, TO], BF16, kind="Internal")
    u2h = nc.dram_tensor("u2h", [128, 64, TO], BF16, kind=ik)
    wdqh = nc.dram_tensor("wdqh", [16, 64, 128, 128], BF16, kind="Internal")
    scrh = nc.dram_tensor("scrh", [8192], F32, kind="Internal")
    xq1d = nc.dram_tensor("xq1d", [128, 16 * TO], BF16, kind=ik)
    dbgc = nc.dram_tensor("dbgc", [128, 8], F32, kind=ik)
    dbgl = nc.dram_tensor("dbgl", [4, 16 * 512], F32, kind=ik)
    dbga = nc.dram_tensor("dbga", [128, 3 * 512], F32, kind=ik)
    dbgs = nc.dram_tensor("dbgs", [128, 2 * 512], BF16, kind=ik)
    dbgq = nc.dram_tensor("dbgq", [128, 4 * 512], BF16, kind=ik)
    dbgq2 = nc.dram_tensor("dbgq2", [128, 2 * 512], BF16, kind=ik)
    dbgw = nc.dram_tensor("dbgw", [128, 2 * 512], BF16, kind=ik)
    dbgg = nc.dram_tensor("dbgg", [128, 2 * 512], F32, kind=ik)
    dbgall = nc.dram_tensor("dbgall", [128, H * TO], BF16, kind=ik)

    import contextlib
    stck = contextlib.ExitStack()
    sb = lambda n, s, d: stck.enter_context(nc.sbuf_tensor(n, s, d))

    H128s = sb("H128s", [128, 128], BF16)
    H16ps = sb("H16ps", [128, 128], BF16)
    H64ps = sb("H64ps", [128, 128], BF16)
    IDfs = sb("IDfs", [128, 128], F32)
    IDbs = sb("IDbs", [128, 128], BF16)
    ones = sb("ones", [128, 1], BF16)
    bqs = sb("bqs", [128, H], F32)
    bks = sb("bks", [128, G], F32)
    bvs = sb("bvs", [128, G], F32)
    wcol = sb("wcol", [128, 6], F32)
    negB = sb("negB", [128, 1], F32)
    magicc = sb("magicc", [128, 1], F32)

    sems = {n: Ctr(nc, n) for n in
            ["ld", "wl", "sa", "sv", "sn", "tp", "tv", "mm", "ev", "qw",
             "at", "ae", "ax", "av", "ep", "er", "es", "ow", "d1", "d2",
             "d3", "d4", "f1", "f2", "f3", "f4", "f5", "f6", "g1", "g2",
             "w0", "w1", "w2", "w3", "x0", "x1", "y0", "y1", "o0", "o1",
             "u0", "u1"]}
    p = Prog(nc)
    ld = sems["ld"]

    for dst, src in ((H128s, cH128), (H16ps, cH16p), (H64ps, cH64p),
                     (IDfs, cIDf), (IDbs, cIDb), (ones, conesb),
                     (bqs, bqh), (bks, bkh), (bvs, bvh)):
        p.o("sync", lambda e, d=dst, s=src: e.dma_start(d[:, :], s[:, :]),
            incs=[(ld, 16)])
    p.o("vector", lambda e: e.memset(magicc[:, :], MAGIC))
    p.o("sync", lambda e: None, waits=[(ld, None)])
    p.flush("p0")

    with (
        nc.sbuf_tensor("p1buf", [128, 2, FI], F32) as p1buf,
        nc.sbuf_tensor("p1dmp", [128, FI], F32) as p1dmp,
        nc.sbuf_tensor("p1col", [128, 48], F32) as p1col,
        nc.sbuf_tensor("p1red", [128, 4], F32) as p1red,
        nc.sbuf_tensor("p1row", [128, 128], F32) as p1row,
        nc.sbuf_tensor("p1sc", [128, 8], F32) as p1sc,
        nc.psum_tensor("p1ps", [128, 128], F32) as p1ps,
    ):
        sA, sB = sems["sa"], sems["sv"]
        slabs = [(w, si, wi * 16 + si) for wi, w in enumerate((wgh, wuh, wdh))
                 for si in range(16)]
        for i, (wt, si, ci) in enumerate(slabs):
            r = AP(wt, si * 128 * FI, [[FI, 128], [1, FI]])
            xs = sems["x%d" % (i % 2)]
            w = [(sA, sA.v - 1)] if i >= 2 else []
            p.o("sync", lambda e, r=r, i=i: e.dma_start(
                p1buf[:, i % 2, :], r), waits=w, incs=[(xs, 16)])
            p.o("scalar", lambda e, i=i, ci=ci: e.activation(
                p1dmp[:, :], p1buf[:, i % 2, :], AF.Abs,
                accum_out=p1col[:, ci:ci + 1]), waits=[(xs, xs.v)], incs=[(sA, 1)])
        for wi in range(3):
            p.o("vector", lambda e, wi=wi: e.tensor_reduce(
                p1red[:, wi:wi + 1], p1col[:, wi * 16:(wi + 1) * 16], AX.X, ALU.add),
                waits=[(sA, None)], incs=[(sB, 1)])
        p.o("tensor", lambda e: e.transpose(p1ps[0:4, :], p1red[:, :], IDfs[:, :]),
            waits=[(sB, None)], incs=[(sB, 1)])
        def p1fin(e):
            e.tensor_copy(p1row[0:4, :], p1ps[0:4, :])
            e.drain()
            e.tensor_reduce(p1sc[0:3, 0:1], p1row[0:3, :], AX.X, ALU.add)
            e.drain()
            e.tensor_scalar(p1sc[0:3, 1:2], p1sc[0:3, 0:1], 1.0 / (E * FI), EPS_Q,
                            ALU.mult, ALU.max)
            e.drain()
            return e.reciprocal(p1sc[0:3, 2:3], p1sc[0:3, 1:2])
        p.o("vector", p1fin, waits=[(sB, None)], incs=[(sB, 1)])
        d1 = sems["d1"]
        p.o("sync", lambda e: e.dma_start(scrh[0:6], p1sc[0:3, 1:3]),
            waits=[(sB, None)], incs=[(d1, 16)])
        for j in range(6):
            p.o("sync", lambda e, j=j: e.dma_start(
                wcol[:, j:j + 1], AP(scrh, j, [[0, 128], [1, 1]])),
                waits=[(d1, 16)], incs=[(d1, 16)])
        if dbg:
            p.o("sync", lambda e: e.dma_start(dbgc[:, 0:6], wcol[:, :]),
                waits=[(d1, None)], incs=[(d1, 16)])
        p.o("sync", lambda e: None, waits=[(d1, None)])
        p.flush("p1")
    WSG, WIG, WSU, WIU, WSD, WID = range(6)

    if DEV_STAGE < 1:
        stck.close()
        return nc

    def emit_ln(xb_ap, xn_ap, stc, dump_ap, wait_ld, width=E):
        sA, sV, sN = sems["sa"], sems["sv"], sems["sn"]
        p.o("scalar", lambda e: e.activation(dump_ap, xb_ap, AF.Square,
                                             accum_out=stc[:, 1:2]), waits=wait_ld)
        p.o("scalar", lambda e: e.activation(dump_ap, xb_ap, AF.Copy,
                                             accum_out=stc[:, 0:1]), incs=[(sA, 1)])
        def stats(e):
            e.tensor_scalar(stc[:, 2:3], stc[:, 0:1], 1.0 / width, None, ALU.mult)
            e.tensor_scalar(stc[:, 3:4], stc[:, 1:2], 1.0 / width, None, ALU.mult)
            e.drain()
            e.tensor_tensor(stc[:, 4:5], stc[:, 2:3], stc[:, 2:3], ALU.mult)
            e.drain()
            e.tensor_tensor(stc[:, 5:6], stc[:, 3:4], stc[:, 4:5], ALU.subtract)
            e.drain()
            return e.tensor_scalar(stc[:, 5:6], stc[:, 5:6], EPS_LN, None, ALU.add)
        p.o("vector", stats, waits=[(sA, None)], incs=[(sV, 1)])
        p.o("scalar", lambda e: e.activation(stc[:, 6:7], stc[:, 5:6], AF.Sqrt),
            waits=[(sV, None)], incs=[(sA, 1)])
        def fin(e):
            e.reciprocal(stc[:, 7:8], stc[:, 6:7])
            e.drain()
            e.tensor_tensor(stc[:, 6:7], stc[:, 2:3], stc[:, 7:8], ALU.mult)
            e.drain()
            return e.tensor_scalar(stc[:, 6:7], stc[:, 6:7], -1.0, None, ALU.mult)
        p.o("vector", fin, waits=[(sA, None)], incs=[(sV, 1)])
        p.o("scalar", lambda e: e.activation(xn_ap, xb_ap, AF.Identity,
                                             bias=stc[:, 6:7], scale=stc[:, 7:8]),
            waits=[(sV, None)], incs=[(sN, 1)])

    astck = contextlib.ExitStack()
    asb = lambda n, s, d: astck.enter_context(nc.sbuf_tensor(n, s, d))
    qall = asb("qall", [128, H * TO], BF16)
    kall = asb("kall", [128, G * S], BF16)
    vTall = asb("vTall", [128, G * S], BF16)

    def emit_proj(name, src, n_tok, q_path):
        sN, sTp, sTv, sMM, sEv = (sems["sn"], sems["tp"], sems["tv"],
                                  sems["mm"], sems["ev"])
        sWl, sVt = sems["wl"], sems["qw"]
        ngrp = n_tok // 512
        nw = H if q_path else G
        with (
            nc.sbuf_tensor(name + "xb", [128, 2, E], F32) as xb,
            nc.sbuf_tensor(name + "dm", [128, E], F32) as dm,
            nc.sbuf_tensor(name + "st", [128, 8, 8], F32) as stt,
            nc.sbuf_tensor(name + "xn", [128, 2, E], BF16) as xn,
            nc.sbuf_tensor(name + "xf", [128, 16 * 512], BF16) as xnf,
            nc.sbuf_tensor(name + "wb", [128, 2, E], BF16) as wb,
            nc.sbuf_tensor(name + "vt", [128, 2, 512], BF16) as vtb,
            nc.psum_tensor(name + "tba", [128, 128], BF16) as ptba,
            nc.psum_tensor(name + "tbb", [128, 128], BF16) as ptbb,
            nc.psum_tensor(name + "t2a", [128, 128], BF16) as ptb2a,
            nc.psum_tensor(name + "t2b", [128, 128], BF16) as ptb2b,
            nc.psum_tensor(name + "pm", [128, 512], F32) as pm0,
            nc.psum_tensor(name + "pn", [128, 512], F32) as pm1,
            nc.psum_tensor(name + "pv", [128, 512], F32) as pv0,
            nc.psum_tensor(name + "pw", [128, 512], F32) as pv1,
        ):
            pms = (pm0, pm1)
            pvs = (pv0, pv1)
            ptbs = (ptba, ptbb)
            ptb2s = (ptb2a, ptb2b)
            nmark, tvmark, evmark, vtmark, wmark = {}, {}, {}, {}, {}
            tp0, tv0, vt0 = sTp.v, sTv.v, sVt.v
            xsrc = src[:, :].rearrange("(a p) f -> a p f", p=128)
            for g in range(ngrp):
                for s4 in range(4):
                    ti = g * 4 + s4
                    sl = ti % 2
                    xs = sems["x%d" % sl]
                    w = [(sN, nmark[ti - 2])] if ti - 2 in nmark else []
                    p.o("sync", lambda e, ti=ti, sl=sl: e.dma_start(
                        xb[:, sl, :], xsrc[ti, :, :]), waits=w, incs=[(xs, 16)])
                    emit_ln(xb[:, sl, :], xn[:, sl, :], stt[:, ti % 8, :],
                            dm[:, :], [(xs, xs.v)])
                    nmark[ti] = sems["sn"].v
                    for ec in range(16):
                        i = ti * 16 + ec
                        w = [(sems["sn"], nmark[ti])]
                        if i >= 2:
                            w.append((sTv, tv0 + i - 1))
                        p.o("tensor", lambda e, sl=sl, ec=ec, i=i: e.transpose(
                            ptbs[i % 2][:, :],
                            xn[:, sl, ec * 128:ec * 128 + 128], IDbs[:, :]),
                            waits=w, incs=[(sTp, 1)])
                        p.o("vector", lambda e, ec=ec, s4=s4, i=i: e.tensor_copy(
                            xnf[:, ec * 512 + s4 * 128: ec * 512 + s4 * 128 + 128],
                            ptbs[i % 2][:, :]),
                            waits=[(sTp, None)], incs=[(sTv, 1)])
                for hh in range(nw):
                    sl = hh % 2
                    pk = (g, hh - 2) if hh >= 2 else (g - 1, nw - 2 + hh)
                    ws = sems["w%d" % sl]
                    w = [(sMM, wmark[pk])] if pk in wmark else []
                    WH = Wqh if q_path else Wkh
                    swp = WH[:, :, :].rearrange("h (a p) d -> h a p d", p=128)[hh]
                    p.o("gpsimd", lambda e, sl=sl, sw=swp: e.dma_start(
                        wb[:, sl, :].rearrange("p (a d) -> p a d", a=16),
                        sw.transpose([1, 0, 2])), waits=w, incs=[(ws, 16)])
                    def mk_mms(ps_pair, slot, buf):
                        def mms(e):
                            ins = None
                            for ec in range(16):
                                ins = e.matmul(ps_pair[slot][:, :],
                                               buf[:, slot, ec * 128:ec * 128 + 128],
                                               xnf[:, ec * 512:ec * 512 + 512],
                                               start=(ec == 0), stop=(ec == 15))
                            return ins
                        return mms
                    prevkey = (g, hh - 2) if hh >= 2 else (g - 1, nw - 2 + hh)
                    w = [(ws, ws.v), (sTv, tv0 + (g + 1) * 64)]
                    if prevkey in evmark:
                        w.append((sEv, evmark[prevkey]))
                    p.o("tensor", mk_mms(pms, sl, wb), waits=w, incs=[(sMM, 1)])
                    kq_mm = sMM.v
                    if q_path:
                        wmark[(g, hh)] = kq_mm
                        p.o("scalar", lambda e, g=g, hh=hh, sl=sl: e.activation(
                            qall[:, hh * TO + g * 512: hh * TO + g * 512 + 512],
                            pms[sl][:, :], AF.Identity, bias=bqs[:, hh:hh + 1]),
                            waits=[(sMM, kq_mm)], incs=[(sEv, 1)])
                        evmark[(g, hh)] = sEv.v
                    else:
                        p.o("scalar", lambda e, g=g, hh=hh, sl=sl: e.activation(
                            kall[:, hh * S + g * 512: hh * S + g * 512 + 512],
                            pms[sl][:, :], AF.Identity, bias=bks[:, hh:hh + 1]),
                            waits=[(sMM, kq_mm)], incs=[(sEv, 1)])
                        svp = Wvh[:, :, :].rearrange("h (a p) d -> h a p d", p=128)[hh]
                        p.o("gpsimd", lambda e, sl=sl, sw=svp: e.dma_start(
                            wb[:, sl, :].rearrange("p (a d) -> p a d", a=16),
                            sw.transpose([1, 0, 2])),
                            waits=[(sMM, kq_mm)], incs=[(ws, 16)])
                        wv = [(ws, ws.v)]
                        if prevkey in evmark:
                            wv.append((sEv, evmark[prevkey]))
                        p.o("tensor", mk_mms(pvs, sl, wb), waits=wv, incs=[(sMM, 1)])
                        v_mm = sMM.v
                        wmark[(g, hh)] = v_mm
                        p.o("scalar", lambda e, hh=hh, sl=sl: e.activation(
                            vtb[:, sl, :], pvs[sl][:, :], AF.Identity,
                            bias=bvs[:, hh:hh + 1]),
                            waits=[(sMM, v_mm)], incs=[(sEv, 1)])
                        ev_v = sEv.v
                        evmark[(g, hh)] = ev_v
                        for s4 in range(4):
                            iv = (g * nw + hh) * 4 + s4
                            wv2 = [(sEv, ev_v)]
                            if iv >= 2:
                                wv2.append((sVt, vt0 + iv - 1))
                            p.o("tensor", lambda e, sl=sl, s4=s4, iv=iv: e.transpose(
                                ptb2s[iv % 2][:, :],
                                vtb[:, sl, s4 * 128:s4 * 128 + 128], IDbs[:, :]),
                                waits=wv2, incs=[(sTp, 1)])
                            p.o("vector", lambda e, hh=hh, g=g, s4=s4, iv=iv: e.tensor_copy(
                                vTall[:, hh * S + (g * 4 + s4) * 128:
                                      hh * S + (g * 4 + s4) * 128 + 128],
                                ptb2s[iv % 2][:, :]),
                                waits=[(sTp, None)], incs=[(sVt, 1)])
            p.o("sync", lambda e: None,
                waits=[(sEv, None), (sVt, None), (sTv, None)])
            p.flush(name)

    emit_proj("pq", xo, TO, True)
    if DEV_STAGE < 99:
        p.o("sync", lambda e: e.dma_start(dbgq2[:, 0:512],
                                          qall[:, 14 * TO:14 * TO + 512]),
            incs=[(sems["ow"], 16)])
        p.o("sync", lambda e: e.dma_start(dbgq2[:, 512:1024],
                                          qall[:, 13 * TO:13 * TO + 512]),
            incs=[(sems["ow"], 16)])
        p.o("sync", lambda e: None, waits=[(sems["ow"], None)])
        p.flush("dbgq2")
    if DEV_STAGE < 2:
        astck.close()
        stck.close()
        return nc
    emit_proj("pkv", xf, S, False)

    if os.environ.get("MK_STOP_PKV", "0") == "1":
        astck.close()
        stck.close()
        return nc
    with (
        nc.sbuf_tensor("pbsq", [128, 2, S], BF16) as sqb,
        nc.sbuf_tensor("pbn", [1, 64], F32) as pbn,
        nc.psum_tensor("pbps", [128, 512], F32) as bps,
    ):
        sA, sMM, sV = sems["sa"], sems["mm"], sems["sv"]
        NOPB = os.environ.get("MK_NOPB", "0") == "1"
        p.o("vector", lambda e: e.memset(pbn[:, :], 0.0), incs=[(sV, 1)])
        if NOPB:
            p.o("vector", lambda e: e.memset(pbn[0:1, 26:27], -18.6))
        for which, (buf, wid, nh) in enumerate(
                () if NOPB else ((qall, TO, H), (kall, S, G))):
            for hh in range(nh):
                col = hh if which == 0 else 16 + hh
                w = [(sMM, None)] if (which, hh) > (0, 1) else []
                p.o("scalar", lambda e, hh=hh, buf=buf, wid=wid: e.activation(
                    sqb[:, hh % 2, 0:wid], buf[:, hh * wid:(hh + 1) * wid], AF.Square),
                    waits=w, incs=[(sA, 1)])
                for sub in range(wid // 512):
                    p.o("tensor", lambda e, hh=hh, sub=sub: e.matmul(
                        bps[0:1, :], ones[:, :], sqb[:, hh % 2, sub * 512:sub * 512 + 512],
                        start=True, stop=True), waits=[(sA, None), (sV, None)],
                        incs=[(sMM, 1)])
                    def bmax(e, col=col):
                        e.tensor_reduce(pbn[0:1, 20:21], bps[0:1, :], AX.X, ALU.max)
                        e.drain()
                        return e.tensor_tensor(pbn[0:1, col:col + 1],
                                               pbn[0:1, col:col + 1],
                                               pbn[0:1, 20:21], ALU.max)
                    p.o("vector", bmax, waits=[(sMM, None)], incs=[(sV, 1)])
        if not NOPB:
            def bfin1(e):
                e.tensor_reduce(pbn[0:1, 21:22], pbn[0:1, 0:16], AX.X, ALU.max)
                return e.tensor_reduce(pbn[0:1, 22:23], pbn[0:1, 16:20], AX.X, ALU.max)
            p.o("vector", bfin1, waits=[(sV, None)], incs=[(sV, 1)])
            p.o("scalar", lambda e: e.activation(pbn[0:1, 23:25], pbn[0:1, 21:23], AF.Sqrt),
                waits=[(sV, None)], incs=[(sA, 1)])
            def bfin2(e):
                e.tensor_tensor(pbn[0:1, 25:26], pbn[0:1, 23:24], pbn[0:1, 24:25], ALU.mult)
                e.drain()
                return e.tensor_scalar(pbn[0:1, 26:27], pbn[0:1, 25:26],
                                       -ISQ128 * 1.05, None, ALU.mult)
            p.o("vector", bfin2, waits=[(sA, None)], incs=[(sV, 1)])
        d1 = sems["d1"]
        p.o("sync", lambda e: e.dma_start(scrh[8:9], pbn[0:1, 26:27]),
            waits=[(sV, None)], incs=[(d1, 16)])
        p.o("sync", lambda e: e.dma_start(negB[:, :], AP(scrh, 8, [[0, 128], [1, 1]])),
            waits=[(d1, None)], incs=[(d1, 16)])
        p.o("sync", lambda e: None, waits=[(d1, None)])
        p.flush("pb")

    if os.environ.get("MK_STOP_PB", "0") == "1":
        astck.close()
        stck.close()
        return nc
    NCH = 32
    with (
        nc.sbuf_tensor("masks", [128, MASK_W], BF16) as masks,
        nc.sbuf_tensor("paE", [128, 4, 512], BF16) as Eb,
        nc.sbuf_tensor("paao", [128, 8 * 512], F32) as ao,
        nc.sbuf_tensor("palr", [1, 16 * 512], F32) as lrows,
        nc.sbuf_tensor("palc", [128, 64], F32) as lcols,
        nc.sbuf_tensor("paxo", [128, 2, 1024], F32) as xot,
        nc.sbuf_tensor("pax1", [128, 2, 128], F32) as x1b,
        nc.psum_tensor("paS0", [128, 512], F32) as ps0,
        nc.psum_tensor("paS1", [128, 512], F32) as ps1,
        nc.psum_tensor("paA0", [128, 512], F32) as pa0,
        nc.psum_tensor("paA1", [128, 512], F32) as pa1,
        nc.psum_tensor("paL0", [128, 512], F32) as pl0,
        nc.psum_tensor("paL1", [128, 512], F32) as pl1,
        nc.psum_tensor("paT0", [128, 128], F32) as pt0,
        nc.psum_tensor("paT1", [128, 128], F32) as pt1,
    ):
        pss, pas, pls, pts = (ps0, ps1), (pa0, pa1), (pl0, pl1), (pt0, pt1)
        sAt, sAe, sAx, sAv = sems["at"], sems["ae"], sems["ax"], sems["av"]
        sEp, sEr, sEs, sOw = sems["ep"], sems["er"], sems["es"], sems["ow"]
        sTp = sems["tp"]
        d2, d3 = sems["d2"], sems["d3"]
        p.o("sync", lambda e: e.dma_start(masks[:, :], maskh[:, :]), incs=[(ld, 16)])
        if dbg:
            p.o("sync", lambda e: e.dma_start(
                dbgq[:, 1024:1536], kall[:, 3 * S:3 * S + 512]), incs=[(sOw, 16)])
            p.o("sync", lambda e: e.dma_start(
                dbgq[:, 1536:2048], vTall[:, 3 * S:3 * S + 512]), incs=[(sOw, 16)])
        p.o("vector", lambda e: None, waits=[(ld, None)])
        p.o("sync", lambda e: None, waits=[(ld, None)])
        expmark, avmark, epmark = {}, {}, {}
        eptq_end = {}
        xosrc = xo[:, :].rearrange("(a p) f -> a p f", p=128)
        ci = 0
        ei = 0
        xomark = {}
        for tq in range(4):
            for hf in range(2):
                for h8 in range(8):
                    h = hf * 8 + h8
                    g = h // 4
                    for c in range(NCH):
                        i = ci
                        ci += 1
                        sl = i % 2
                        el = i % 4
                        w = [(sAe, expmark[i - 2])] if i >= 2 else []
                        if os.environ.get("MK_NOSMM", "0") == "1":
                            p.o("tensor", lambda e: None, waits=w, incs=[(sAt, 1)])
                        else:
                            p.o("tensor", lambda e, g=g, c=c, tq=tq, h=h, sl=sl: e.matmul(
                                pss[sl][:, :],
                                kall[:, g * S + c * 128: g * S + c * 128 + 128],
                                qall[:, h * TO + tq * 512: h * TO + tq * 512 + 512],
                                start=True, stop=True), waits=w, incs=[(sAt, 1)])
                        w = [(sAt, sAt.v)]
                        if i >= 2:
                            w.append((sAv, avmark[i - 2]))
                        p.o("scalar", lambda e, sl=sl, el=el: e.activation(
                            Eb[:, el, :], pss[sl][:, :], AF.Exp,
                            bias=negB[:, 0:1], scale=ISQ128), waits=w, incs=[(sAe, 1)])
                        expmark[i] = sAe.v
                        off = tq * 512 - c * 128 + MASK_Z
                        p.o("vector", lambda e, el=el, off=off: e.tensor_tensor(
                            Eb[:, el, :], Eb[:, el, :], masks[:, off:off + 512],
                            ALU.mult), waits=[(sAe, sAe.v)], incs=[(sAx, 1)])
                        def avl(e, h=h, g=g, c=c, el=el):
                            e.matmul(pas[h % 2][:, :],
                                     vTall[:, g * S + c * 128: g * S + c * 128 + 128],
                                     Eb[:, el, :], start=(c == 0), stop=(c == NCH - 1))
                            return e.matmul(pls[h % 2][0:1, :], ones[:, :],
                                            Eb[:, el, :], start=(c == 0),
                                            stop=(c == NCH - 1))
                        w = [(sAx, sAx.v)]
                        if c == 0:
                            prev = (tq, h - 2) if h >= 2 else (tq - 1, 14 + h)
                            if prev in epmark:
                                w.append((sEp, epmark[prev]))
                        p.o("tensor", avl, waits=w, incs=[(sAv, 1)])
                        avmark[i] = sAv.v
                    def evh(e, h=h, h8=h8):
                        e.tensor_copy(ao[:, h8 * 512:(h8 + 1) * 512], pas[h % 2][:, :])
                        return e.tensor_copy(lrows[0:1, h * 512:(h + 1) * 512],
                                             pls[h % 2][0:1, :])
                    if NOEVH:
                        p.o("vector", lambda e: None, waits=[(sAv, None)], incs=[(sEp, 1)])
                    else:
                        p.o("vector", evh, waits=[(sAv, None)], incs=[(sEp, 1)])
                    epmark[(tq, h)] = sEp.v
                if not NOBOUNCE:
                    p.o("sync", lambda e, hf=hf: e.dma_start(
                        scrh[hf * 4096: hf * 4096 + 4096],
                        lrows[0:1, hf * 4096:(hf + 1) * 4096]),
                        waits=[(sEp, None)], incs=[(d2, 16)])
                    p.o("sync", lambda e, hf=hf: e.dma_start(
                        lcols[:, hf * 32:(hf + 1) * 32],
                        AP(scrh, hf * 4096, [[1, 128], [512, 8], [128, 4]]),
                        allow_slow_non_contiguous=True),
                        waits=[(d2, d2.v), (sEs, sEs.v)], incs=[(d2, 16)])
                p.o("vector", lambda e, hf=hf: e.reciprocal(
                    lcols[:, hf * 32:(hf + 1) * 32], lcols[:, hf * 32:(hf + 1) * 32]),
                    waits=[(d2, None)], incs=[(sEr, 1)])
                er_v = sEr.v
                for s4 in range(0 if NOEPI else 4):
                    si = (tq * 2 + hf) * 4 + s4
                    ys = sems["y%d" % (si % 2)]
                    w = [(sEs, xomark[si - 2])] if si - 2 in xomark else []
                    p.o("sync", lambda e, tq=tq, s4=s4, si=si, hf=hf: e.dma_start(
                        xot[:, si % 2, :],
                        xosrc[tq * 4 + s4, :, hf * 1024:(hf + 1) * 1024]),
                        waits=w, incs=[(ys, 16)])
                    d3_v = (ys, ys.v)
                    for h8 in range(8):
                        h = hf * 8 + h8
                        i = ei
                        ei += 1
                        w = [(sEp, epmark[(tq, h)])]
                        if i >= 2:
                            w.append((sEs, sEs.v - 1))
                        p.o("tensor", lambda e, h8=h8, s4=s4, i=i: e.transpose(
                            pts[i % 2][:, :],
                            ao[:, h8 * 512 + s4 * 128: h8 * 512 + s4 * 128 + 128],
                            IDfs[:, :]), waits=w, incs=[(sTp, 1)])
                        osm = sems["o%d" % (i % 2)]
                        wv = [(sTp, sTp.v), (sEr, er_v), d3_v]
                        if i >= 2:
                            wv.append((osm, osm.v))
                        def stt(e, h=h, s4=s4, i=i, hf=hf, h8=h8, si=si):
                            return e.scalar_tensor_tensor(
                                x1b[:, i % 2, :], pts[i % 2][:, :],
                                lcols[:, hf * 32 + h8 * 4 + s4:
                                      hf * 32 + h8 * 4 + s4 + 1],
                                xot[:, si % 2, h8 * 128:(h8 + 1) * 128],
                                ALU.mult, ALU.add)
                        p.o("vector", stt, waits=wv, incs=[(sEs, 1)])
                        p.o("sync", lambda e, tq=tq, s4=s4, h=h, i=i: e.dma_start(
                            x1h[tq * 512 + s4 * 128: tq * 512 + s4 * 128 + 128,
                                h * 128:(h + 1) * 128], x1b[:, i % 2, :]),
                            waits=[(sEs, sEs.v)], incs=[(osm, 16)])
                    xomark[si] = sEs.v
            if dbg:
                p.o("sync", lambda e, tq=tq: e.dma_start(
                    dbgl[tq:tq + 1, :], lrows[0:1, :]),
                    waits=[(sEp, None)], incs=[(sOw, 16)])
                if tq == 0:
                    p.o("sync", lambda e: e.dma_start(
                        dbga[:, :], ao[:, 5 * 512:8 * 512]),
                        waits=[(sEp, None)], incs=[(sOw, 16)])
            eptq_end[tq] = sEs.v
        p.o("sync", lambda e: None,
            waits=[(sOw, None), (sems["o0"], None), (sems["o1"], None)])
        p.flush("pa")
        if dbg:
            p.o("sync", lambda e: e.dma_start(dbgall[:, :], qall[:, :]),
                incs=[(sOw, 16)])
            p.o("sync", lambda e: None, waits=[(sOw, None)])
            p.flush("dbgpa")
    astck.close()

    if DEV_STAGE < 3:
        stck.close()
        return nc

    with (
        nc.sbuf_tensor("f1xb", [128, 2, E], F32) as xb,
        nc.sbuf_tensor("f1dm", [128, E], F32) as dm,
        nc.sbuf_tensor("f1st", [128, 8, 8], F32) as stt1,
        nc.sbuf_tensor("f1xn", [128, 2, E], BF16) as xn,
        nc.sbuf_tensor("f1xf", [128, 16 * 512], BF16) as xnf,
        nc.sbuf_tensor("f1ub", [128, 2, 512], BF16) as ub,
        nc.psum_tensor("f1tba", [128, 128], BF16) as ptba,
        nc.psum_tensor("f1tbb", [128, 128], BF16) as ptbb,
        nc.psum_tensor("f1p0", [128, 512], F32) as pu0,
        nc.psum_tensor("f1p1", [128, 512], F32) as pu1,
    ):
        pus = (pu0, pu1)
        ptbs = (ptba, ptbb)
        sN, sTp, sTv, sMM, sEv = (sems["sn"], sems["tp"], sems["tv"],
                                  sems["mm"], sems["ev"])
        d4 = sems["d4"]
        x1src = x1h[:, :].rearrange("(a p) f -> a p f", p=128)
        nmark, evm, dmam = {}, {}, {}
        tv0 = sTv.v
        for g in range(4):
            for s4 in range(4):
                ti = g * 4 + s4
                sl = ti % 2
                xs = sems["x%d" % sl]
                w = [(sN, nmark[ti - 2])] if ti - 2 in nmark else []
                p.o("sync", lambda e, ti=ti, sl=sl: e.dma_start(
                    xb[:, sl, :], x1src[ti, :, :]), waits=w, incs=[(xs, 16)])
                emit_ln(xb[:, sl, :], xn[:, sl, :], stt1[:, ti % 8, :], dm[:, :],
                        [(xs, xs.v)])
                nmark[ti] = sN.v
                for ec in range(16):
                    i = ti * 16 + ec
                    w = [(sN, nmark[ti])]
                    if i >= 2:
                        w.append((sTv, tv0 + i - 1))
                    p.o("tensor", lambda e, sl=sl, ec=ec, i=i: e.transpose(
                        ptbs[i % 2][:, :],
                        xn[:, sl, ec * 128:ec * 128 + 128], IDbs[:, :]),
                        waits=w, incs=[(sTp, 1)])
                    p.o("vector", lambda e, ec=ec, s4=s4, i=i: e.tensor_copy(
                        xnf[:, ec * 512 + s4 * 128: ec * 512 + s4 * 128 + 128],
                        ptbs[i % 2][:, :]),
                        waits=[(sTp, None)], incs=[(sTv, 1)])
            for tau in range(16):
                it = g * 16 + tau
                w = [(sTv, tv0 + (g + 1) * 64)]
                if it >= 2:
                    w.append((sEv, evm[it - 2]))
                p.o("tensor", lambda e, tau=tau, it=it: e.matmul(
                    pus[it % 2][:, :], H128s[:, :], xnf[:, tau * 512:tau * 512 + 512],
                    start=True, stop=True), waits=w, incs=[(sMM, 1)])
                us = sems["u%d" % (it % 2)]
                wv = [(sMM, sMM.v)]
                if it >= 2:
                    wv.append((us, us.v))
                p.o("scalar", lambda e, it=it: e.activation(
                    ub[:, it % 2, :], pus[it % 2][:, :], AF.Copy),
                    waits=wv, incs=[(sEv, 1)])
                evm[it] = sEv.v
                p.o("sync", lambda e, tau=tau, g=g, it=it: e.dma_start(
                    u1h[tau, :, g * 512:(g + 1) * 512], ub[:, it % 2, :]),
                    waits=[(sEv, sEv.v)], incs=[(us, 16)])
        p.o("sync", lambda e: None,
            waits=[(sems["u0"], None), (sems["u1"], None)])
        p.flush("pf1")

    fstck = contextlib.ExitStack()
    xq1 = fstck.enter_context(nc.sbuf_tensor("xq1", [128, 16 * TO], BF16))
    with (
        nc.sbuf_tensor("f2rb", [128, 16 * 512], BF16) as rb,
        nc.sbuf_tensor("f2z", [128, 16 * 512], F32) as z1,
        nc.sbuf_tensor("f2mx", [128, 512], F32) as mxj,
        nc.sbuf_tensor("f2mc", [128, 16], F32) as mc,
        nc.sbuf_tensor("f2qr", [128, 1024], F32) as qr,
        nc.sbuf_tensor("f2tq", [128, 512], F32) as tqb,
        nc.psum_tensor("f2z0", [128, 512], F32) as pz0,
        nc.psum_tensor("f2z1", [128, 512], F32) as pz1,
        nc.psum_tensor("f2t0", [128, 128], F32) as pt0,
        nc.psum_tensor("f2t1", [128, 128], F32) as pt1,
    ):
        pzs, pts = (pz0, pz1), (pt0, pt1)
        sMM, sEv, sV, sQ = sems["mm"], sems["ev"], sems["sv"], sems["f1"]
        sTp = sems["tp"]
        d4 = sems["d4"]
        for t5 in range(4):
            w = [(sMM, sMM.v)] if t5 > 0 else []
            for cc in range(8):
                p.o("sync", lambda e, t5=t5, cc=cc: e.dma_start(
                    rb[cc * 16:(cc + 1) * 16, :].rearrange("p (j t) -> p j t", j=16),
                    AP(u1h, t5 * 512 + cc * 16 * TO,
                       [[128 * TO, 16], [TO, 16], [1, 512]])),
                    waits=w, incs=[(ld, 16)])
            for jr in range(16):
                w = [(ld, None)]
                if t5 * 16 + jr >= 2:
                    w.append((sEv, sEv.v - 1))
                p.o("tensor", lambda e, jr=jr: e.matmul(
                    pzs[jr % 2][:, :], H16ps[:, :], rb[:, jr * 512:jr * 512 + 512],
                    start=True, stop=True), waits=w, incs=[(sMM, 1)])
                p.o("vector", lambda e, jr=jr: e.tensor_copy(
                    z1[:, jr * 512:jr * 512 + 512], pzs[jr % 2][:, :]),
                    waits=[(sMM, sMM.v)], incs=[(sEv, 1)])
            p.o("vector", lambda e: e.tensor_reduce(
                mxj[:, :], z1[:, :].rearrange("p (j t) -> p t j", j=16),
                AX.X, ALU.max, apply_absolute_value=True),
                waits=[(sEv, None)], incs=[(sV, 1)])
            for s4 in range(4):
                w = [(sV, sV.v)]
                p.o("tensor", lambda e, s4=s4: e.transpose(
                    pts[s4 % 2][:, :], mxj[:, s4 * 128:s4 * 128 + 128], IDfs[:, :]),
                    waits=w, incs=[(sTp, 1)])
                p.o("vector", lambda e, s4=s4: e.tensor_reduce(
                    mc[:, s4:s4 + 1], pts[s4 % 2][:, :], AX.X, ALU.max),
                    waits=[(sTp, None)], incs=[(sV, 1)])
            def smath(e):
                e.tensor_scalar(mc[:, 4:8], mc[:, 0:4], 1.0 / SQ2048, EPS_Q,
                                ALU.mult, ALU.max)
                e.drain()
                e.reciprocal(mc[:, 8:12], mc[:, 4:8])
                e.drain()
                e.tensor_scalar(mc[:, 8:12], mc[:, 8:12], 127.0 / SQ2048, None,
                                ALU.mult)
                return e.tensor_scalar(mc[:, 12:16], mc[:, 4:8], 1.0 / 127.0, None,
                                       ALU.mult)
            p.o("vector", smath, waits=[(sV, None)], incs=[(sV, 1)])
            p.o("sync", lambda e: e.dma_start(
                AP(scrh, 0, [[1, 128], [128, 8]]), mc[:, 8:16],
                allow_slow_non_contiguous=True),
                waits=[(sV, None), (d4, None)], incs=[(d4, 16)])
            p.o("sync", lambda e: e.dma_start(
                qr[:, :], AP(scrh, 0, [[0, 128], [1, 1024]])),
                waits=[(d4, d4.v), (sQ, sQ.v)], incs=[(d4, 16)])
            for jr in range(16):
                def quant(e, jr=jr, t5=t5):
                    e.tensor_tensor(tqb[:, :], z1[:, jr * 512:jr * 512 + 512],
                                    qr[:, 0:512], ALU.mult)
                    e.drain()
                    e.tensor_scalar(tqb[:, :], tqb[:, :], MAGIC, MAGIC,
                                    ALU.add, ALU.subtract)
                    e.drain()
                    return e.tensor_tensor(
                        xq1[:, jr * TO + t5 * 512: jr * TO + t5 * 512 + 512],
                        tqb[:, :], qr[:, 512:1024], ALU.mult)
                p.o("vector", quant, waits=[(d4, None)], incs=[(sQ, 1)])
        if dbg:
            p.o("sync", lambda e: e.dma_start(xq1d[:, :], xq1[:, :]),
                waits=[(sQ, None)], incs=[(d4, 16)])
        p.o("sync", lambda e: None, waits=[(d4, None), (sQ, None)])
        p.flush("pf2")

    if DEV_STAGE < 4:
        fstck.close()
        stck.close()
        return nc

    with (
        nc.sbuf_tensor("f3wf", [128, 4, 512], F32) as wf,
        nc.sbuf_tensor("f3w1", [128, 4, 512], F32) as wt1,
        nc.sbuf_tensor("f3w2", [128, 4, 512], F32) as wt2,
        nc.sbuf_tensor("f3gq", [128, 2, 16, 512], BF16) as wgq,
        nc.sbuf_tensor("f3uq", [128, 2, 16, 512], BF16) as wuq,
        nc.sbuf_tensor("f3sg", [128, 4, 512], F32) as sg,
        nc.sbuf_tensor("f3hb", [128, 4, 512], BF16) as hb,
        nc.sbuf_tensor("f3ub", [128, 2, 512], BF16) as ub2,
        nc.sbuf_tensor("f3df", [128, 2, 1024], F32) as dwf,
        nc.sbuf_tensor("f3d1", [128, 2, 1024], F32) as dw1,
        nc.sbuf_tensor("f3d2", [128, 2, 1024], F32) as dw2,
        nc.sbuf_tensor("f3dq", [128, 2, 1024], BF16) as dwq,
        nc.sbuf_tensor("f3dbg", [128, 512], F32) as dnfdbg,
        nc.psum_tensor("f3g0", [128, 512], F32) as pg0,
        nc.psum_tensor("f3g1", [128, 512], F32) as pg1,
        nc.psum_tensor("f3u0", [128, 512], F32) as pu0,
        nc.psum_tensor("f3u1", [128, 512], F32) as pu1,
        nc.psum_tensor("f3U0", [128, 512], F32) as pU0,
        nc.psum_tensor("f3U1", [128, 512], F32) as pU1,
    ):
        pgs, pus2, pUs = (pg0, pg1), (pu0, pu1), (pU0, pU1)
        sWl, sP1, sP2, sP3 = sems["wl"], sems["f2"], sems["f3"], sems["f4"]
        sMM, sSg, sHb, sUe = sems["mm"], sems["sa"], sems["f5"], sems["ev"]
        d4, dW, dS = sems["d4"], sems["g1"], sems["g2"]
        p3m = {}
        t1m, t2m, t3m = {}, {}, {}
        stashm = {}
        mm_ob = {}

        def emit_tern(src_ap, inv_col, dst_ap, nb, slot, kind, key, nslots,
                      extra_p3_waits):
            fbuf, b1, b2 = (wf, wt1, wt2) if nb == 512 else (dwf, dw1, dw2)
            ldc = sems[("w%d" if nb == 512 else "x%d") % slot]
            kprev = (kind, key - nslots)
            w = [(sP1, t1m[kprev])] if kprev in t1m else []
            if os.environ.get("MK_TERNSER", "0") == "1":
                w = [(sP3, sP3.v)]
            p.o("sync" if nb == 512 else "gpsimd",
                lambda e: e.dma_start(fbuf[:, slot, 0:nb], src_ap),
                waits=w, incs=[(ldc, 16)])
            w = [(ldc, ldc.v)]
            if kprev in t2m:
                w.append((sP2, t2m[kprev]))
            p.o("vector", lambda e: e.tensor_scalar(
                b1[:, slot, 0:nb], fbuf[:, slot, 0:nb], inv_col, MAGIC,
                ALU.mult, ALU.add), waits=w, incs=[(sP1, 1)])
            t1m[(kind, key)] = sP1.v
            w = [(sP1, sP1.v)]
            if kprev in t3m:
                w.append((sP3, t3m[kprev]))
            def p2p3(e, slot=slot, nb=nb, dst_ap=dst_ap):
                e.drain()
                e.tensor_scalar(b2[:, slot, 0:nb], b1[:, slot, 0:nb], MAGIC, 1.0,
                                ALU.subtract, ALU.min).then_inc(sP2.h, 1)
                e.drain()
                return e.tensor_scalar(dst_ap, b2[:, slot, 0:nb], -1.0, None,
                                       ALU.max)
            sP2.v += 1
            p.o("vector", p2p3, waits=w + extra_p3_waits, incs=[(sP3, 1)])
            t2m[(kind, key)] = sP2.v
            t3m[(kind, key)] = sP3.v
            return sP3.v

        dump_fence = [0]
        tern_fence = [0]
        def dump_tern0(v3):
            if not dbg:
                return
            p.o("sync", lambda e: e.dma_start(dbgg[:, 0:512], wf[:, 0, :]),
                waits=[(sP3, v3)], incs=[(d4, 16)])
            p.o("sync", lambda e: e.dma_start(dbgg[:, 512:1024], wt1[:, 0, :]),
                waits=[(sP3, v3)], incs=[(d4, 16)])
            p.o("gpsimd", lambda e: e.dma_start(dbgw[:, 0:512], wt2[:, 0, :]),
                waits=[(sP3, v3)], incs=[(d4, 16)])
            tern_fence[0] = d4.v
            p.o("vector", lambda e: None, waits=[(d4, d4.v)], incs=[(sP1, 1)])
            t1m[("g", 0)] = sP1.v
        def dump_tern(ob, last_tern):
            if not (dbg and ob == 0):
                return
            p.o("sync", lambda e: e.dma_start(dbgw[:, 0:512], wgq[:, 0, 0, :]),
                waits=[(sP3, last_tern)], incs=[(d4, 16)])
            p.o("sync", lambda e: e.dma_start(dbgw[:, 512:1024], wuq[:, 0, 0, :]),
                waits=[(sP3, last_tern)], incs=[(d4, 16)])
            dump_fence[0] = d4.v

        def dump_gu(ob, oc, tg, sl, mmv):
            if not (dbg and ob == 0 and oc == 0 and tg == 0):
                return
            p.o("scalar", lambda e, sl=sl: e.activation(
                dnfdbg[:, :], pgs[sl][:, :], AF.Copy),
                waits=[(sMM, mmv)], incs=[(sSg, 1)])
            p.o("sync", lambda e: e.dma_start(dbgg[:, 0:512], dnfdbg[:, :]),
                waits=[(sSg, sSg.v)], incs=[(d4, 16)])
            p.o("scalar", lambda e, sl=sl: e.activation(
                dnfdbg[:, :], pus2[sl][:, :], AF.Copy),
                waits=[(sSg, sSg.v), (d4, d4.v)], incs=[(sSg, 1)])
            p.o("sync", lambda e: e.dma_start(dbgg[:, 512:1024], dnfdbg[:, :]),
                waits=[(sSg, sSg.v)], incs=[(d4, 16)])

        def pf3_body():
          jjd = [0]
          lt = {}
          def emit_terns(ob):
            for jr in range(16):
                for wi, (wsrc, invc, dst) in enumerate(
                        ((wgh, wcol[:, WIG:WIG + 1], wgq),
                         (wuh, wcol[:, WIU:WIU + 1], wuq))):
                    key = (ob * 16 + jr) * 2 + wi
                    sap = AP(wsrc, jr * FI + ob * 512,
                             [[16 * FI, 8], [128 * FI, 16], [1, 512]])
                    ew = []
                    if ob > 1:
                        ew.append((sMM, mm_ob[ob - 2]))
                    emit_tern(sap, invc, dst[:, ob % 2, jr, :], 512, key % 4,
                              "g", key, 4, ew)
            lt[ob] = sems["f4"].v
            dump_tern(ob, lt[ob])
            for _ in range(8):
                jj = jjd[0]
                jjd[0] += 1
                jjj, hh2 = jj // 2, jj % 2
                sap = AP(wdh, jjj * E + hh2 * 1024,
                         [[64 * E, 2], [128 * E, 64], [1, 1024]])
                ew = [(dS, stashm[jj - 2])] if jj >= 2 else []
                v = emit_tern(sap, wcol[:, WID:WID + 1], dwq[:, jj % 2, :], 1024,
                              jj % 2, "d", jj, 2, ew)
                p.o("gpsimd", lambda e, jjj=jjj, hh2=hh2, jj=jj: e.dma_start(
                    AP(wdqh, jjj * 16384 + hh2 * 8 * 16384 * 64,
                       [[128, 128], [16384 * 64, 8], [1, 128]]),
                    dwq[:, jj % 2, :].rearrange("p (a b) -> p a b", a=8)),
                    waits=[(sP3, v)], incs=[(dS, 16)])
                stashm[jj] = dS.v
          def emit_mms(ob):
            last_tern = lt[ob]
            for oc in range(4):
                tau = ob * 4 + oc
                for tg in range(4):
                    it = tau * 4 + tg
                    sl = it % 2
                    fl = it % 4
                    def gmm(e, oc=oc, tg=tg, sl=sl, ob=ob, wq=wgq, ps=pgs):
                        ins = None
                        for jr in range(16):
                            ins = e.matmul(
                                ps[sl][:, :], wq[:, ob % 2, jr, oc * 128:oc * 128 + 128],
                                xq1[:, jr * TO + tg * 512: jr * TO + tg * 512 + 512],
                                start=(jr == 0), stop=(jr == 15))
                        return ins
                    w = [(sP3, last_tern)]
                    if it >= 2:
                        w.append((sSg, p3m[("sg", it - 2)]))
                    p.o("tensor", lambda e, gmm=gmm: gmm(e), waits=w, incs=[(sMM, 1)])
                    wu2 = []
                    if it >= 2:
                        wu2.append((sHb, p3m[("hb", it - 2)]))
                    p.o("tensor", lambda e, gmm=gmm: gmm(
                        e, wq=wuq, ps=pus2),
                        waits=wu2, incs=[(sMM, 1)])
                    mmv = sMM.v
                    dump_gu(ob, oc, tg, sl, mmv)
                    wsg2 = [(sMM, mmv)]
                    if it >= 4:
                        wsg2.append((sHb, p3m[("hb", it - 4)]))
                    p.o("scalar", lambda e, sl=sl, fl=fl: e.activation(
                        sg[:, fl, :], pgs[sl][:, :], AF.Silu,
                        scale=wcol[:, WSG:WSG + 1]), waits=wsg2, incs=[(sSg, 1)])
                    p3m[("sg", it)] = sSg.v
                    whb = [(sSg, sSg.v)]
                    if it >= 4:
                        whb.append((sMM, p3m[("ua", it - 4)]))
                    p.o("vector", lambda e, sl=sl, fl=fl: e.scalar_tensor_tensor(
                        hb[:, fl, :], pus2[sl][:, :], wcol[:, WSU:WSU + 1],
                        sg[:, fl, :], ALU.mult, ALU.mult), waits=whb, incs=[(sHb, 1)])
                    p3m[("hb", it)] = sHb.v
                    w = [(sHb, sHb.v)]
                    if it >= 2:
                        w.append((sUe, p3m[("ue", it - 2)]))
                    p.o("tensor", lambda e, sl=sl, fl=fl: e.matmul(
                        pUs[sl][:, :], H128s[:, :], hb[:, fl, :],
                        start=True, stop=True), waits=w, incs=[(sMM, 1)])
                    p3m[("ua", it)] = sMM.v
                    us = sems["u%d" % sl]
                    wue = [(sMM, sMM.v)]
                    if it >= 2:
                        wue.append((us, us.v))
                    p.o("scalar", lambda e, sl=sl: e.activation(
                        ub2[:, sl, :], pUs[sl][:, :], AF.Copy),
                        waits=wue, incs=[(sUe, 1)])
                    p3m[("ue", it)] = sUe.v
                    p.o("sync", lambda e, tau=tau, tg=tg, sl=sl: e.dma_start(
                        AP(u2h, tau * 64 * TO + tg * 512,
                           [[64 * 64 * TO, 2], [TO, 64], [1, 512]]),
                        ub2[:, sl, :].rearrange("p (g j) -> p g j", g=2)),
                        waits=[(sUe, sUe.v)], incs=[(us, 16)])
            mm_ob[ob] = sMM.v
          for ob in range(16):
            emit_terns(ob)
            if ob >= 1:
                emit_mms(ob - 1)
          emit_mms(15)
        pf3_body()
        p.o("sync", lambda e: None,
            waits=[(sems["u0"], None), (sems["u1"], None), (dS, None),
                   (d4, None)])
        p.flush("pf3")
    fstck.close()

    if DEV_STAGE < 5:
        stck.close()
        return nc

    with (
        nc.sbuf_tensor("pdrb", [128, 64 * 256], BF16) as rb2,
        nc.sbuf_tensor("pdmx", [128, 1024], F32) as mx2,
        nc.sbuf_tensor("pdmc", [128, 32], F32) as mc2,
        nc.sbuf_tensor("pdqr", [128, 1024], F32) as qr2,
        nc.sbuf_tensor("pdtq", [128, 512], F32) as qtb,
        nc.sbuf_tensor("pdxq", [128, 64 * 1024], BF16) as xq2,
        nc.sbuf_tensor("pdwd", [128, 64 * 128], BF16) as wdt,
        nc.sbuf_tensor("pddn", [128, 2, 512], F32) as dnf,
        nc.sbuf_tensor("pdx1", [128, 2, 128], F32) as x1t,
        nc.sbuf_tensor("pdob", [128, 2, 128], F32) as obf,
        nc.psum_tensor("pdz0", [128, 512], F32) as qz0,
        nc.psum_tensor("pdz1", [128, 512], F32) as qz1,
        nc.psum_tensor("pdt0", [128, 128], F32) as qt0,
        nc.psum_tensor("pdt1", [128, 128], F32) as qt1,
        nc.psum_tensor("pdd0", [128, 512], F32) as pd0,
        nc.psum_tensor("pdd1", [128, 512], F32) as pd1,
    ):
        qzs, qts, pds = (qz0, qz1), (qt0, qt1), (pd0, pd1)
        sMM, sEv, sV, sQ2 = sems["mm"], sems["ev"], sems["sv"], sems["f1"]
        sEs, sOw, sTp = sems["es"], sems["ow"], sems["tp"]
        d4, d5, dL = sems["d4"], sems["d2"], sems["g1"]
        tpm = {}
        for half in range(2):
            hbase = half * 1024
            for t4 in range(4):
                t0 = hbase + t4 * 256
                w = [(sMM, sMM.v)]
                p.o("sync", lambda e, t0=t0: e.dma_start(
                    rb2[:, :].rearrange("p (j t) -> p j t", j=64),
                    AP(u2h, t0, [[64 * TO, 128], [TO, 64], [1, 256]])),
                    waits=w, incs=[(ld, 16)])
                for ch in range(32):
                    w = [(ld, None)]
                    if t4 * 32 + ch >= 2:
                        w.append((sV, sV.v - 1))
                    if t4 == 0 and ch < 2:
                        w.append((sQ2, sQ2.v))
                    def mmz(e, ch=ch):
                        rhs = AP(rb2, ch * 8, [[64 * 256, 128], [1, 8], [256, 64]])
                        return e.matmul(qzs[ch % 2][:, :].rearrange(
                            "p (t j) -> p t j", t=8), H64ps[:, :], rhs,
                            start=True, stop=True)
                    p.o("tensor", mmz, waits=w, incs=[(sMM, 1)])
                    p.o("vector", lambda e, ch=ch, t4=t4: e.tensor_reduce(
                        mx2[:, t4 * 256 + ch * 8: t4 * 256 + ch * 8 + 8],
                        qzs[ch % 2][:, :].rearrange("p (t j) -> p t j", t=8),
                        AX.X, ALU.max, apply_absolute_value=True),
                        waits=[(sMM, sMM.v)], incs=[(sV, 1)])
            for s8 in range(8):
                p.o("tensor", lambda e, s8=s8: e.transpose(
                    qts[s8 % 2][:, :], mx2[:, s8 * 128:s8 * 128 + 128], IDfs[:, :]),
                    waits=[(sV, sV.v)], incs=[(sTp, 1)])
                p.o("vector", lambda e, s8=s8: e.tensor_reduce(
                    mc2[:, s8:s8 + 1], qts[s8 % 2][:, :], AX.X, ALU.max),
                    waits=[(sTp, None)], incs=[(sV, 1)])
            def smath2(e):
                e.tensor_scalar(mc2[:, 8:16], mc2[:, 0:8], 1.0 / SQ8192, EPS_Q,
                                ALU.mult, ALU.max)
                e.drain()
                e.reciprocal(mc2[:, 16:24], mc2[:, 8:16])
                e.drain()
                e.tensor_scalar(mc2[:, 16:24], mc2[:, 16:24], 127.0 / SQ8192, None,
                                ALU.mult)
                e.tensor_scalar(mc2[:, 24:32], mc2[:, 8:16], 1.0 / 127.0, None,
                                ALU.mult)
                e.drain()
                return e.tensor_tensor(
                    mc2[:, 24:32], mc2[:, 24:32],
                    AP(wcol, WSD, [[6, 128], [0, 8]]), ALU.mult)
            p.o("vector", smath2, waits=[(sV, None)], incs=[(sV, 1)])
            p.o("sync", lambda e: e.dma_start(
                AP(scrh, 0, [[1, 128], [128, 8]]), mc2[:, 16:24],
                allow_slow_non_contiguous=True),
                waits=[(sV, None), (d4, None)], incs=[(d4, 16)])
            p.o("sync", lambda e: e.dma_start(
                qr2[:, :], AP(scrh, 0, [[0, 128], [1, 1024]])),
                waits=[(d4, d4.v), (sQ2, sQ2.v)], incs=[(d4, 16)])
            for t4 in range(4):
                t0 = hbase + t4 * 256
                p.o("sync", lambda e, t0=t0: e.dma_start(
                    rb2[:, :].rearrange("p (j t) -> p j t", j=64),
                    AP(u2h, t0, [[64 * TO, 128], [TO, 64], [1, 256]])),
                    waits=[(sMM, sMM.v)], incs=[(ld, 16)])
                for ch in range(32):
                    w = [(ld, None), (d4, None)]
                    if t4 * 32 + ch >= 2:
                        w.append((sQ2, sQ2.v - 1))
                    def mmz2(e, ch=ch):
                        rhs = AP(rb2, ch * 8, [[64 * 256, 128], [1, 8], [256, 64]])
                        return e.matmul(qzs[ch % 2][:, :].rearrange(
                            "p (t j) -> p t j", t=8), H64ps[:, :], rhs,
                            start=True, stop=True)
                    p.o("tensor", mmz2, waits=w, incs=[(sMM, 1)])
                    def qnt(e, ch=ch, t4=t4):
                        tloc = t4 * 256 + ch * 8
                        qsl = AP(qr2, tloc, [[1024, 128], [1, 8], [0, 64]])
                        e.tensor_tensor(
                            qtb[:, :].rearrange("p (t j) -> p t j", t=8),
                            qzs[ch % 2][:, :].rearrange("p (t j) -> p t j", t=8),
                            qsl, ALU.mult)
                        e.drain()
                        return e.tensor_scalar(
                            AP(xq2, tloc, [[64 * 1024, 128], [1, 8], [1024, 64]]),
                            qtb[:, :].rearrange("p (t j) -> p t j", t=8),
                            MAGIC, MAGIC, ALU.add, ALU.subtract)
                    p.o("vector", qnt, waits=[(sMM, sMM.v)], incs=[(sQ2, 1)])
            for o2c in range(16):
                sl = 0
                w = [(sQ2, sQ2.v), (sMM, sMM.v)]
                p.o("sync", lambda e, o2c=o2c, sl=sl: e.dma_start(
                    wdt[:, :], AP(wdqh, o2c * 64 * 16384,
                                      [[128, 128], [16384, 64], [1, 128]])),
                    waits=w, incs=[(dL, 16)])
                for t5 in range(2):
                    it = (half * 16 + o2c) * 2 + t5
                    dsl = it % 2
                    def dmm(e, sl=sl, t5=t5, dsl=dsl):
                        ins = None
                        for j in range(64):
                            ins = e.matmul(
                                pds[dsl][:, :], wdt[:, j * 128:j * 128 + 128],
                                xq2[:, j * 1024 + t5 * 512: j * 1024 + t5 * 512 + 512],
                                start=(j == 0), stop=(j == 63))
                        return ins
                    w = [(dL, None)]
                    if it >= 2:
                        w.append((sEv, sEv.v - 1))
                    p.o("tensor", dmm, waits=w, incs=[(sMM, 1)])
                    wev = [(sMM, sMM.v)]
                    if it >= 2:
                        wev.append((sTp, tpm[it - 2]))
                    p.o("vector", lambda e, dsl=dsl: e.tensor_copy(
                        dnf[:, dsl, :], pds[dsl][:, :]), waits=wev, incs=[(sEv, 1)])
                    evv = sEv.v
                    for s4 in range(4):
                        tb = t5 * 4 + s4
                        i2 = it * 4 + s4
                        trow = hbase + tb * 128
                        ys = sems["y%d" % (i2 % 2)]
                        wx = [(sEs, sEs.v - 1)] if i2 >= 2 else []
                        p.o("sync", lambda e, trow=trow, o2c=o2c, i2=i2: e.dma_start(
                            x1t[:, i2 % 2, :],
                            x1h[trow:trow + 128, o2c * 128:o2c * 128 + 128]),
                            waits=wx, incs=[(ys, 16)])
                        wt = [(sEv, evv)]
                        if i2 >= 2:
                            wt.append((sEs, sEs.v - 1))
                        p.o("tensor", lambda e, dsl=dsl, s4=s4, i2=i2: e.transpose(
                            qts[i2 % 2][:, :],
                            dnf[:, dsl, s4 * 128:s4 * 128 + 128], IDfs[:, :]),
                            waits=wt, incs=[(sTp, 1)])
                        if s4 == 3:
                            tpm[it] = sTp.v
                        osm = sems["o%d" % (i2 % 2)]
                        ws = [(sTp, sTp.v), (ys, ys.v)]
                        if i2 >= 2:
                            ws.append((osm, osm.v))
                        def sttd(e, i2=i2, tb=tb):
                            return e.scalar_tensor_tensor(
                                obf[:, i2 % 2, :], qts[i2 % 2][:, :],
                                mc2[:, 24 + tb:24 + tb + 1],
                                x1t[:, i2 % 2, :], ALU.mult, ALU.add)
                        p.o("vector", sttd, waits=ws, incs=[(sEs, 1)])
                        p.o("sync", lambda e, trow=trow, o2c=o2c, i2=i2: e.dma_start(
                            out_h[trow:trow + 128, o2c * 128:o2c * 128 + 128],
                            obf[:, i2 % 2, :]), waits=[(sEs, sEs.v)],
                            incs=[(osm, 16)])
        p.o("sync", lambda e: None,
            waits=[(sems["o0"], None), (sems["o1"], None)])
        p.flush("pd")

    stck.close()
    return nc


_CACHE = {}


def _mk_inputs(inputs):
    bf = ml_dtypes.bfloat16
    x = np.asarray(inputs["L0_attn_x"], dtype=np.float32)
    cons = {
        "cH128": _hadamard(128).astype(bf),
        "cH16p": np.kron(np.eye(8, dtype=np.float32), _hadamard(16)).astype(bf),
        "cH64p": np.kron(np.eye(2, dtype=np.float32), _hadamard(64)).astype(bf),
        "cIDf": np.eye(128, dtype=np.float32),
        "cIDb": np.eye(128, dtype=np.float32).astype(bf),
        "conesb": np.ones((128, 1), dtype=np.float32).astype(bf),
        "Wq": np.ascontiguousarray(np.asarray(inputs["L0_attn_Wq"], dtype=np.float32)),
        "Wk": np.ascontiguousarray(np.asarray(inputs["L0_attn_Wk"], dtype=np.float32)),
        "Wv": np.ascontiguousarray(np.asarray(inputs["L0_attn_Wv"], dtype=np.float32)),
        "bqT": np.ascontiguousarray(np.asarray(inputs["L0_attn_bq"], dtype=np.float32).T),
        "bkT": np.ascontiguousarray(np.asarray(inputs["L0_attn_bk"], dtype=np.float32).T),
        "bvT": np.ascontiguousarray(np.asarray(inputs["L0_attn_bv"], dtype=np.float32).T),
        "wgT": np.ascontiguousarray(np.asarray(inputs["L0_ffn_w_gate"], dtype=np.float32).T),
        "wuT": np.ascontiguousarray(np.asarray(inputs["L0_ffn_w_up"], dtype=np.float32).T),
        "wdT": np.ascontiguousarray(np.asarray(inputs["L0_ffn_w_down"], dtype=np.float32).T),
    }
    masks = {}
    for qb in (0, TO):
        cc = np.arange(MASK_W)[None, :] - MASK_Z + qb
        ii = np.arange(128)[:, None]
        masks[qb] = (cc >= ii).astype(np.float32).astype(bf)
    in_maps = []
    for c in range(8):
        b, half = c // 2, c % 2
        m = dict(cons)
        m["x_full"] = np.ascontiguousarray(x[b])
        m["x_own"] = np.ascontiguousarray(x[b, half * TO:(half + 1) * TO])
        m["maskC"] = masks[half * TO]
        in_maps.append(m)
    return in_maps


def _get_nc():
    if "nc" not in _CACHE:
        nc = bass.Bass("TRN2", target_bir_lowering=False)
        build(nc)
        mybir.codegen_inst_isa_subclasses(nc)
        _CACHE["nc"] = nc
    return _CACHE["nc"]


def kernel(**inputs):
    nc = _get_nc()
    in_maps = _mk_inputs(inputs)
    res = run_bass_kernel_spmd(nc, in_maps, core_ids=list(range(8)))
    out = np.empty((B, S, E), dtype=np.float32)
    for c in range(8):
        b, half = c // 2, c % 2
        out[b, half * TO:(half + 1) * TO] = res.results[c]["out"]
    return out


# version 13 (final)
# speedup vs baseline: 1.3180x; 1.3180x over previous
import os
import numpy as np
import ml_dtypes
import concourse.bass as bass
import concourse.mybir as mybir
from concourse.bass_utils import run_bass_kernel_spmd

F32 = mybir.dt.float32
BF16 = mybir.dt.bfloat16
AF = mybir.ActivationFunctionType
ALU = mybir.AluOpType
AX = mybir.AxisListType
AP = bass.AP

MAGIC = 12582912.0
EPS_LN = 1e-5
EPS_Q = 1e-5
ISQ128 = 0.08838834764831845
SQ2048 = 45.254833995939045
SQ8192 = 90.50966799187809

B, S, E = 4, 4096, 2048
H, D, G = 16, 128, 4
TO = 2048
FI = 8192
MASK_Z = 3968
MASK_W = MASK_Z + 2048

DEV_STAGE = int(os.environ.get("MK_DEV_STAGE", "99"))
NOEPI = os.environ.get("MK_NOEPI", "0") == "1"
NOBOUNCE = os.environ.get("MK_NOBOUNCE", "0") == "1"
NOMASK = os.environ.get("MK_NOMASK", "0") == "1"
NOEXP = os.environ.get("MK_NOEXP", "0") == "1"
NOEVH = os.environ.get("MK_NOEVH", "0") == "1"
ENGS = ("sync", "scalar", "vector", "tensor", "gpsimd")


def _hadamard(n):
    h = np.array([[1.0]], dtype=np.float32)
    while h.shape[0] < n:
        h = np.block([[h, h], [h, -h]])
    return h


class Ctr:
    def __init__(self, nc, name):
        self.h = nc.alloc_semaphore(name=name)
        self.v = 0


class Prog:

    def __init__(self, nc):
        self.nc = nc
        self.q = {k: [] for k in ENGS}

    def o(self, eng, build, waits=(), incs=()):
        wl = [(c.h, (c.v if v is None else v)) for (c, v) in waits]
        il = []
        for (c, n) in incs:
            c.v += n
            il.append((c.h, n))

        def thunk(e, wl=wl, il=il, build=build):
            for (h, v) in wl:
                if v > 0:
                    e.wait_ge(h, v)
            ins = build(e)
            for (h, n) in il:
                if ins is None:
                    ins = e.engine_nop() if hasattr(e, "engine_nop") else e.nop()
                ins.then_inc(h, n)
        self.q[eng].append(thunk)

    def flush(self, name):
        nc = self.nc
        with nc.Block(name) as blk:
            for eng in ENGS:
                th = self.q[eng]
                if th:
                    getattr(blk, eng)(lambda e, th=th: [t(e) for t in th])
        self.q = {k: [] for k in ENGS}


def build(nc):
    dbg = DEV_STAGE < 99
    xf = nc.dram_tensor("x_full", [S, E], F32, kind="ExternalInput")
    xo = nc.dram_tensor("x_own", [TO, E], F32, kind="ExternalInput")
    Wqh = nc.dram_tensor("Wq", [H, E, D], F32, kind="ExternalInput")
    Wkh = nc.dram_tensor("Wk", [G, E, D], F32, kind="ExternalInput")
    Wvh = nc.dram_tensor("Wv", [G, E, D], F32, kind="ExternalInput")
    bqh = nc.dram_tensor("bqT", [D, H], F32, kind="ExternalInput")
    bkh = nc.dram_tensor("bkT", [D, G], F32, kind="ExternalInput")
    bvh = nc.dram_tensor("bvT", [D, G], F32, kind="ExternalInput")
    wgh = nc.dram_tensor("wgT", [E, FI], F32, kind="ExternalInput")
    wuh = nc.dram_tensor("wuT", [E, FI], F32, kind="ExternalInput")
    wdh = nc.dram_tensor("wdT", [FI, E], F32, kind="ExternalInput")
    maskh = nc.dram_tensor("maskC", [128, MASK_W], BF16, kind="ExternalInput")
    cH128 = nc.dram_tensor("cH128", [128, 128], BF16, kind="ExternalInput")
    cH16p = nc.dram_tensor("cH16p", [128, 128], BF16, kind="ExternalInput")
    cH64p = nc.dram_tensor("cH64p", [128, 128], BF16, kind="ExternalInput")
    cIDf = nc.dram_tensor("cIDf", [128, 128], F32, kind="ExternalInput")
    cIDb = nc.dram_tensor("cIDb", [128, 128], BF16, kind="ExternalInput")
    conesb = nc.dram_tensor("conesb", [128, 1], BF16, kind="ExternalInput")

    out_h = nc.dram_tensor("out", [TO, E], F32, kind="ExternalOutput")
    ik = "ExternalOutput" if dbg else "Internal"
    x1h = nc.dram_tensor("x1h", [TO, E], F32, kind=ik)
    u1h = nc.dram_tensor("u1h", [16, 128, TO], BF16, kind="Internal")
    u2h = nc.dram_tensor("u2h", [128, 64, TO], BF16, kind=ik)
    wdqh = nc.dram_tensor("wdqh", [16, 64, 128, 128], BF16, kind="Internal")
    scrh = nc.dram_tensor("scrh", [8192], F32, kind="Internal")
    xq1d = nc.dram_tensor("xq1d", [128, 16 * TO], BF16, kind=ik)
    dbgc = nc.dram_tensor("dbgc", [128, 8], F32, kind=ik)
    dbgl = nc.dram_tensor("dbgl", [4, 16 * 512], F32, kind=ik)
    dbga = nc.dram_tensor("dbga", [128, 3 * 512], F32, kind=ik)
    dbgs = nc.dram_tensor("dbgs", [128, 2 * 512], BF16, kind=ik)
    dbgq = nc.dram_tensor("dbgq", [128, 4 * 512], BF16, kind=ik)
    dbgq2 = nc.dram_tensor("dbgq2", [128, 2 * 512], BF16, kind=ik)
    dbgw = nc.dram_tensor("dbgw", [128, 2 * 512], BF16, kind=ik)
    dbgg = nc.dram_tensor("dbgg", [128, 2 * 512], F32, kind=ik)
    dbgall = nc.dram_tensor("dbgall", [128, H * TO], BF16, kind=ik)

    import contextlib
    stck = contextlib.ExitStack()
    sb = lambda n, s, d: stck.enter_context(nc.sbuf_tensor(n, s, d))

    H128s = sb("H128s", [128, 128], BF16)
    H16ps = sb("H16ps", [128, 128], BF16)
    H64ps = sb("H64ps", [128, 128], BF16)
    IDfs = sb("IDfs", [128, 128], F32)
    IDbs = sb("IDbs", [128, 128], BF16)
    ones = sb("ones", [128, 1], BF16)
    bqs = sb("bqs", [128, H], F32)
    bks = sb("bks", [128, G], F32)
    bvs = sb("bvs", [128, G], F32)
    wcol = sb("wcol", [128, 6], F32)
    negB = sb("negB", [128, 1], F32)
    magicc = sb("magicc", [128, 1], F32)

    sems = {n: Ctr(nc, n) for n in
            ["ld", "wl", "sa", "sv", "sn", "tp", "tv", "mm", "ev", "qw",
             "at", "ae", "ax", "av", "ep", "er", "es", "ow", "d1", "d2",
             "d3", "d4", "f1", "f2", "f3", "f4", "f5", "f6", "g1", "g2",
             "w0", "w1", "w2", "w3", "x0", "x1", "y0", "y1", "o0", "o1",
             "u0", "u1"]}
    p = Prog(nc)
    ld = sems["ld"]

    for dst, src in ((H128s, cH128), (H16ps, cH16p), (H64ps, cH64p),
                     (IDfs, cIDf), (IDbs, cIDb), (ones, conesb),
                     (bqs, bqh), (bks, bkh), (bvs, bvh)):
        p.o("sync", lambda e, d=dst, s=src: e.dma_start(d[:, :], s[:, :]),
            incs=[(ld, 16)])
    p.o("vector", lambda e: e.memset(magicc[:, :], MAGIC))
    p.o("sync", lambda e: None, waits=[(ld, None)])
    p.flush("p0")

    with (
        nc.sbuf_tensor("p1buf", [128, 2, FI], F32) as p1buf,
        nc.sbuf_tensor("p1dmp", [128, FI], F32) as p1dmp,
        nc.sbuf_tensor("p1col", [128, 48], F32) as p1col,
        nc.sbuf_tensor("p1red", [128, 4], F32) as p1red,
        nc.sbuf_tensor("p1row", [128, 128], F32) as p1row,
        nc.sbuf_tensor("p1sc", [128, 8], F32) as p1sc,
        nc.psum_tensor("p1ps", [128, 128], F32) as p1ps,
    ):
        sA, sB = sems["sa"], sems["sv"]
        slabs = [(w, si, wi * 16 + si) for wi, w in enumerate((wgh, wuh, wdh))
                 for si in range(16)]
        for i, (wt, si, ci) in enumerate(slabs):
            r = AP(wt, si * 128 * FI, [[FI, 128], [1, FI]])
            xs = sems["x%d" % (i % 2)]
            w = [(sA, sA.v - 1)] if i >= 2 else []
            p.o("sync", lambda e, r=r, i=i: e.dma_start(
                p1buf[:, i % 2, :], r), waits=w, incs=[(xs, 16)])
            p.o("scalar", lambda e, i=i, ci=ci: e.activation(
                p1dmp[:, :], p1buf[:, i % 2, :], AF.Abs,
                accum_out=p1col[:, ci:ci + 1]), waits=[(xs, xs.v)], incs=[(sA, 1)])
        for wi in range(3):
            p.o("vector", lambda e, wi=wi: e.tensor_reduce(
                p1red[:, wi:wi + 1], p1col[:, wi * 16:(wi + 1) * 16], AX.X, ALU.add),
                waits=[(sA, None)], incs=[(sB, 1)])
        p.o("tensor", lambda e: e.transpose(p1ps[0:4, :], p1red[:, :], IDfs[:, :]),
            waits=[(sB, None)], incs=[(sB, 1)])
        def p1fin(e):
            e.tensor_copy(p1row[0:4, :], p1ps[0:4, :])
            e.drain()
            e.tensor_reduce(p1sc[0:3, 0:1], p1row[0:3, :], AX.X, ALU.add)
            e.drain()
            e.tensor_scalar(p1sc[0:3, 1:2], p1sc[0:3, 0:1], 1.0 / (E * FI), EPS_Q,
                            ALU.mult, ALU.max)
            e.drain()
            return e.reciprocal(p1sc[0:3, 2:3], p1sc[0:3, 1:2])
        p.o("vector", p1fin, waits=[(sB, None)], incs=[(sB, 1)])
        d1 = sems["d1"]
        p.o("sync", lambda e: e.dma_start(scrh[0:6], p1sc[0:3, 1:3]),
            waits=[(sB, None)], incs=[(d1, 16)])
        for j in range(6):
            p.o("sync", lambda e, j=j: e.dma_start(
                wcol[:, j:j + 1], AP(scrh, j, [[0, 128], [1, 1]])),
                waits=[(d1, 16)], incs=[(d1, 16)])
        if dbg:
            p.o("sync", lambda e: e.dma_start(dbgc[:, 0:6], wcol[:, :]),
                waits=[(d1, None)], incs=[(d1, 16)])
        p.o("sync", lambda e: None, waits=[(d1, None)])
        p.flush("p1")
    WSG, WIG, WSU, WIU, WSD, WID = range(6)

    if DEV_STAGE < 1:
        stck.close()
        return nc

    def emit_ln(xb_ap, xn_ap, stc, dump_ap, wait_ld, width=E):
        sA, sV, sN = sems["sa"], sems["sv"], sems["sn"]
        p.o("scalar", lambda e: e.activation(dump_ap, xb_ap, AF.Square,
                                             accum_out=stc[:, 1:2]), waits=wait_ld)
        p.o("scalar", lambda e: e.activation(dump_ap, xb_ap, AF.Copy,
                                             accum_out=stc[:, 0:1]), incs=[(sA, 1)])
        def stats(e):
            e.tensor_scalar(stc[:, 2:3], stc[:, 0:1], 1.0 / width, None, ALU.mult)
            e.tensor_scalar(stc[:, 3:4], stc[:, 1:2], 1.0 / width, None, ALU.mult)
            e.drain()
            e.tensor_tensor(stc[:, 4:5], stc[:, 2:3], stc[:, 2:3], ALU.mult)
            e.drain()
            e.tensor_tensor(stc[:, 5:6], stc[:, 3:4], stc[:, 4:5], ALU.subtract)
            e.drain()
            return e.tensor_scalar(stc[:, 5:6], stc[:, 5:6], EPS_LN, None, ALU.add)
        p.o("vector", stats, waits=[(sA, None)], incs=[(sV, 1)])
        p.o("scalar", lambda e: e.activation(stc[:, 6:7], stc[:, 5:6], AF.Sqrt),
            waits=[(sV, None)], incs=[(sA, 1)])
        def fin(e):
            e.reciprocal(stc[:, 7:8], stc[:, 6:7])
            e.drain()
            e.tensor_tensor(stc[:, 6:7], stc[:, 2:3], stc[:, 7:8], ALU.mult)
            e.drain()
            return e.tensor_scalar(stc[:, 6:7], stc[:, 6:7], -1.0, None, ALU.mult)
        p.o("vector", fin, waits=[(sA, None)], incs=[(sV, 1)])
        p.o("scalar", lambda e: e.activation(xn_ap, xb_ap, AF.Identity,
                                             bias=stc[:, 6:7], scale=stc[:, 7:8]),
            waits=[(sV, None)], incs=[(sN, 1)])

    astck = contextlib.ExitStack()
    asb = lambda n, s, d: astck.enter_context(nc.sbuf_tensor(n, s, d))
    qall = asb("qall", [128, H * TO], BF16)
    kall = asb("kall", [128, G * S], BF16)
    vTall = asb("vTall", [128, G * S], BF16)

    def emit_proj(name, src, n_tok, q_path):
        sN, sTp, sTv, sMM, sEv = (sems["sn"], sems["tp"], sems["tv"],
                                  sems["mm"], sems["ev"])
        sWl, sVt = sems["wl"], sems["qw"]
        ngrp = n_tok // 512
        nw = H if q_path else G
        with (
            nc.sbuf_tensor(name + "xb", [128, 2, E], F32) as xb,
            nc.sbuf_tensor(name + "dm", [128, E], F32) as dm,
            nc.sbuf_tensor(name + "st", [128, 8, 8], F32) as stt,
            nc.sbuf_tensor(name + "xn", [128, 2, E], BF16) as xn,
            nc.sbuf_tensor(name + "xf", [128, 16 * 512], BF16) as xnf,
            nc.sbuf_tensor(name + "wb", [128, 2, E], BF16) as wb,
            nc.sbuf_tensor(name + "vt", [128, 2, 512], BF16) as vtb,
            nc.psum_tensor(name + "tba", [128, 128], BF16) as ptba,
            nc.psum_tensor(name + "tbb", [128, 128], BF16) as ptbb,
            nc.psum_tensor(name + "t2a", [128, 128], BF16) as ptb2a,
            nc.psum_tensor(name + "t2b", [128, 128], BF16) as ptb2b,
            nc.psum_tensor(name + "pm", [128, 512], F32) as pm0,
            nc.psum_tensor(name + "pn", [128, 512], F32) as pm1,
            nc.psum_tensor(name + "pv", [128, 512], F32) as pv0,
            nc.psum_tensor(name + "pw", [128, 512], F32) as pv1,
        ):
            pms = (pm0, pm1)
            pvs = (pv0, pv1)
            ptbs = (ptba, ptbb)
            ptb2s = (ptb2a, ptb2b)
            nmark, tvmark, evmark, vtmark, wmark = {}, {}, {}, {}, {}
            tp0, tv0, vt0 = sTp.v, sTv.v, sVt.v
            xsrc = src[:, :].rearrange("(a p) f -> a p f", p=128)
            for g in range(ngrp):
                for s4 in range(4):
                    ti = g * 4 + s4
                    sl = ti % 2
                    xs = sems["x%d" % sl]
                    w = [(sN, nmark[ti - 2])] if ti - 2 in nmark else []
                    p.o("sync", lambda e, ti=ti, sl=sl: e.dma_start(
                        xb[:, sl, :], xsrc[ti, :, :]), waits=w, incs=[(xs, 16)])
                    emit_ln(xb[:, sl, :], xn[:, sl, :], stt[:, ti % 8, :],
                            dm[:, :], [(xs, xs.v)])
                    nmark[ti] = sems["sn"].v
                    for ec in range(16):
                        i = ti * 16 + ec
                        w = [(sems["sn"], nmark[ti])]
                        if i >= 2:
                            w.append((sTv, tv0 + i - 1))
                        p.o("tensor", lambda e, sl=sl, ec=ec, i=i: e.transpose(
                            ptbs[i % 2][:, :],
                            xn[:, sl, ec * 128:ec * 128 + 128], IDbs[:, :]),
                            waits=w, incs=[(sTp, 1)])
                        p.o("vector", lambda e, ec=ec, s4=s4, i=i: e.tensor_copy(
                            xnf[:, ec * 512 + s4 * 128: ec * 512 + s4 * 128 + 128],
                            ptbs[i % 2][:, :]),
                            waits=[(sTp, None)], incs=[(sTv, 1)])
                for hh in range(nw):
                    sl = hh % 2
                    pk = (g, hh - 2) if hh >= 2 else (g - 1, nw - 2 + hh)
                    ws = sems["w%d" % sl]
                    w = [(sMM, wmark[pk])] if pk in wmark else []
                    WH = Wqh if q_path else Wkh
                    swp = WH[:, :, :].rearrange("h (a p) d -> h a p d", p=128)[hh]
                    p.o("gpsimd", lambda e, sl=sl, sw=swp: e.dma_start(
                        wb[:, sl, :].rearrange("p (a d) -> p a d", a=16),
                        sw.transpose([1, 0, 2])), waits=w, incs=[(ws, 16)])
                    def mk_mms(ps_pair, slot, buf):
                        def mms(e):
                            ins = None
                            for ec in range(16):
                                ins = e.matmul(ps_pair[slot][:, :],
                                               buf[:, slot, ec * 128:ec * 128 + 128],
                                               xnf[:, ec * 512:ec * 512 + 512],
                                               start=(ec == 0), stop=(ec == 15))
                            return ins
                        return mms
                    prevkey = (g, hh - 2) if hh >= 2 else (g - 1, nw - 2 + hh)
                    w = [(ws, ws.v), (sTv, tv0 + (g + 1) * 64)]
                    if prevkey in evmark:
                        w.append((sEv, evmark[prevkey]))
                    p.o("tensor", mk_mms(pms, sl, wb), waits=w, incs=[(sMM, 1)])
                    kq_mm = sMM.v
                    if q_path:
                        wmark[(g, hh)] = kq_mm
                        p.o("scalar", lambda e, g=g, hh=hh, sl=sl: e.activation(
                            qall[:, hh * TO + g * 512: hh * TO + g * 512 + 512],
                            pms[sl][:, :], AF.Identity, bias=bqs[:, hh:hh + 1]),
                            waits=[(sMM, kq_mm)], incs=[(sEv, 1)])
                        evmark[(g, hh)] = sEv.v
                    else:
                        p.o("scalar", lambda e, g=g, hh=hh, sl=sl: e.activation(
                            kall[:, hh * S + g * 512: hh * S + g * 512 + 512],
                            pms[sl][:, :], AF.Identity, bias=bks[:, hh:hh + 1]),
                            waits=[(sMM, kq_mm)], incs=[(sEv, 1)])
                        svp = Wvh[:, :, :].rearrange("h (a p) d -> h a p d", p=128)[hh]
                        p.o("gpsimd", lambda e, sl=sl, sw=svp: e.dma_start(
                            wb[:, sl, :].rearrange("p (a d) -> p a d", a=16),
                            sw.transpose([1, 0, 2])),
                            waits=[(sMM, kq_mm)], incs=[(ws, 16)])
                        wv = [(ws, ws.v)]
                        if prevkey in evmark:
                            wv.append((sEv, evmark[prevkey]))
                        p.o("tensor", mk_mms(pvs, sl, wb), waits=wv, incs=[(sMM, 1)])
                        v_mm = sMM.v
                        wmark[(g, hh)] = v_mm
                        p.o("scalar", lambda e, hh=hh, sl=sl: e.activation(
                            vtb[:, sl, :], pvs[sl][:, :], AF.Identity,
                            bias=bvs[:, hh:hh + 1]),
                            waits=[(sMM, v_mm)], incs=[(sEv, 1)])
                        ev_v = sEv.v
                        evmark[(g, hh)] = ev_v
                        for s4 in range(4):
                            iv = (g * nw + hh) * 4 + s4
                            wv2 = [(sEv, ev_v)]
                            if iv >= 2:
                                wv2.append((sVt, vt0 + iv - 1))
                            p.o("tensor", lambda e, sl=sl, s4=s4, iv=iv: e.transpose(
                                ptb2s[iv % 2][:, :],
                                vtb[:, sl, s4 * 128:s4 * 128 + 128], IDbs[:, :]),
                                waits=wv2, incs=[(sTp, 1)])
                            p.o("vector", lambda e, hh=hh, g=g, s4=s4, iv=iv: e.tensor_copy(
                                vTall[:, hh * S + (g * 4 + s4) * 128:
                                      hh * S + (g * 4 + s4) * 128 + 128],
                                ptb2s[iv % 2][:, :]),
                                waits=[(sTp, None)], incs=[(sVt, 1)])
            p.o("sync", lambda e: None,
                waits=[(sEv, None), (sVt, None), (sTv, None)])
            p.flush(name)

    emit_proj("pq", xo, TO, True)
    if DEV_STAGE < 99:
        p.o("sync", lambda e: e.dma_start(dbgq2[:, 0:512],
                                          qall[:, 14 * TO:14 * TO + 512]),
            incs=[(sems["ow"], 16)])
        p.o("sync", lambda e: e.dma_start(dbgq2[:, 512:1024],
                                          qall[:, 13 * TO:13 * TO + 512]),
            incs=[(sems["ow"], 16)])
        p.o("sync", lambda e: None, waits=[(sems["ow"], None)])
        p.flush("dbgq2")
    if DEV_STAGE < 2:
        astck.close()
        stck.close()
        return nc
    emit_proj("pkv", xf, S, False)

    if os.environ.get("MK_STOP_PKV", "0") == "1":
        astck.close()
        stck.close()
        return nc
    with (
        nc.sbuf_tensor("pbsq", [128, 2, S], BF16) as sqb,
        nc.sbuf_tensor("pbn", [1, 64], F32) as pbn,
        nc.psum_tensor("pbps", [128, 512], F32) as bps,
    ):
        sA, sMM, sV = sems["sa"], sems["mm"], sems["sv"]
        NOPB = os.environ.get("MK_NOPB", "0") == "1"
        p.o("vector", lambda e: e.memset(pbn[:, :], 0.0), incs=[(sV, 1)])
        if NOPB:
            p.o("vector", lambda e: e.memset(pbn[0:1, 26:27], -18.6))
        for which, (buf, wid, nh) in enumerate(
                () if NOPB else ((qall, TO, H), (kall, S, G))):
            for hh in range(nh):
                col = hh if which == 0 else 16 + hh
                w = [(sMM, None)] if (which, hh) > (0, 1) else []
                p.o("scalar", lambda e, hh=hh, buf=buf, wid=wid: e.activation(
                    sqb[:, hh % 2, 0:wid], buf[:, hh * wid:(hh + 1) * wid], AF.Square),
                    waits=w, incs=[(sA, 1)])
                for sub in range(wid // 512):
                    p.o("tensor", lambda e, hh=hh, sub=sub: e.matmul(
                        bps[0:1, :], ones[:, :], sqb[:, hh % 2, sub * 512:sub * 512 + 512],
                        start=True, stop=True), waits=[(sA, None), (sV, None)],
                        incs=[(sMM, 1)])
                    def bmax(e, col=col):
                        e.tensor_reduce(pbn[0:1, 20:21], bps[0:1, :], AX.X, ALU.max)
                        e.drain()
                        return e.tensor_tensor(pbn[0:1, col:col + 1],
                                               pbn[0:1, col:col + 1],
                                               pbn[0:1, 20:21], ALU.max)
                    p.o("vector", bmax, waits=[(sMM, None)], incs=[(sV, 1)])
        if not NOPB:
            def bfin1(e):
                e.tensor_reduce(pbn[0:1, 21:22], pbn[0:1, 0:16], AX.X, ALU.max)
                return e.tensor_reduce(pbn[0:1, 22:23], pbn[0:1, 16:20], AX.X, ALU.max)
            p.o("vector", bfin1, waits=[(sV, None)], incs=[(sV, 1)])
            p.o("scalar", lambda e: e.activation(pbn[0:1, 23:25], pbn[0:1, 21:23], AF.Sqrt),
                waits=[(sV, None)], incs=[(sA, 1)])
            def bfin2(e):
                e.tensor_tensor(pbn[0:1, 25:26], pbn[0:1, 23:24], pbn[0:1, 24:25], ALU.mult)
                e.drain()
                return e.tensor_scalar(pbn[0:1, 26:27], pbn[0:1, 25:26],
                                       -ISQ128 * 1.05, None, ALU.mult)
            p.o("vector", bfin2, waits=[(sA, None)], incs=[(sV, 1)])
        d1 = sems["d1"]
        p.o("sync", lambda e: e.dma_start(scrh[8:9], pbn[0:1, 26:27]),
            waits=[(sV, None)], incs=[(d1, 16)])
        p.o("sync", lambda e: e.dma_start(negB[:, :], AP(scrh, 8, [[0, 128], [1, 1]])),
            waits=[(d1, None)], incs=[(d1, 16)])
        p.o("sync", lambda e: None, waits=[(d1, None)])
        p.flush("pb")

    if os.environ.get("MK_STOP_PB", "0") == "1":
        astck.close()
        stck.close()
        return nc
    NCH = 32
    with (
        nc.sbuf_tensor("masks", [128, MASK_W], BF16) as masks,
        nc.sbuf_tensor("paE", [128, 4, 512], BF16) as Eb,
        nc.sbuf_tensor("paao", [128, 8 * 512], F32) as ao,
        nc.sbuf_tensor("palr", [1, 16 * 512], F32) as lrows,
        nc.sbuf_tensor("palc", [128, 64], F32) as lcols,
        nc.sbuf_tensor("paxo", [128, 2, 1024], F32) as xot,
        nc.sbuf_tensor("pax1", [128, 2, 128], F32) as x1b,
        nc.psum_tensor("paS0", [128, 512], F32) as ps0,
        nc.psum_tensor("paS1", [128, 512], F32) as ps1,
        nc.psum_tensor("paS2", [128, 512], F32) as ps2,
        nc.psum_tensor("paS3", [128, 512], F32) as ps3,
        nc.psum_tensor("paA0", [128, 512], F32) as pa0,
        nc.psum_tensor("paA1", [128, 512], F32) as pa1,
        nc.psum_tensor("paL0", [128, 512], F32) as pl0,
        nc.psum_tensor("paT0", [128, 128], F32) as pt0,
    ):
        pss, pas = (ps0, ps1, ps2, ps3), (pa0, pa1)
        sAt, sAe, sAx, sAv = sems["at"], sems["ae"], sems["ax"], sems["av"]
        sEp, sEr, sEs, sOw = sems["ep"], sems["er"], sems["es"], sems["ow"]
        sTp = sems["tp"]
        d2, d3 = sems["d2"], sems["d3"]
        p.o("sync", lambda e: e.dma_start(masks[:, :], maskh[:, :]), incs=[(ld, 16)])
        if dbg:
            p.o("sync", lambda e: e.dma_start(
                dbgq[:, 1024:1536], kall[:, 3 * S:3 * S + 512]), incs=[(sOw, 16)])
            p.o("sync", lambda e: e.dma_start(
                dbgq[:, 1536:2048], vTall[:, 3 * S:3 * S + 512]), incs=[(sOw, 16)])
        p.o("vector", lambda e: None, waits=[(ld, None)])
        p.o("sync", lambda e: None, waits=[(ld, None)])
        expmark, avmark, epmark, satm = {}, {}, {}, {}
        eptq_end = {}
        xosrc = xo[:, :].rearrange("(a p) f -> a p f", p=128)
        ci = 0
        ei = 0
        xomark = {}
        for tq in range(4):
            for hf in range(2):
                for h8 in range(8):
                    h = hf * 8 + h8
                    g = h // 4
                    def emit_smm(c, i, g=g, tq=tq, h=h):
                        sl = i % 4
                        w = [(sAe, expmark[i - 4])] if i >= 4 else []
                        if os.environ.get("MK_NOSMM", "0") == "1":
                            p.o("tensor", lambda e: None, waits=w, incs=[(sAt, 1)])
                        else:
                            p.o("tensor", lambda e, g=g, c=c, tq=tq, h=h, sl=sl: e.matmul(
                                pss[sl][:, :],
                                kall[:, g * S + c * 128: g * S + c * 128 + 128],
                                qall[:, h * TO + tq * 512: h * TO + tq * 512 + 512],
                                start=True, stop=True), waits=w, incs=[(sAt, 1)])
                        satm[i] = sAt.v
                    emit_smm(0, ci)
                    emit_smm(1, ci + 1)
                    for c in range(NCH):
                        i = ci
                        ci += 1
                        sl = i % 4
                        el = i % 4
                        w = [(sAt, satm[i])]
                        if i >= 2:
                            w.append((sAv, avmark[i - 2]))
                        p.o("scalar", lambda e, sl=sl, el=el: e.activation(
                            Eb[:, el, :], pss[sl][:, :], AF.Exp,
                            bias=negB[:, 0:1], scale=ISQ128), waits=w, incs=[(sAe, 1)])
                        expmark[i] = sAe.v
                        off = tq * 512 - c * 128 + MASK_Z
                        p.o("vector", lambda e, el=el, off=off: e.tensor_tensor(
                            Eb[:, el, :], Eb[:, el, :], masks[:, off:off + 512],
                            ALU.mult), waits=[(sAe, sAe.v)], incs=[(sAx, 1)])
                        if c + 2 < NCH:
                            emit_smm(c + 2, i + 2)
                        def avl(e, h=h, g=g, c=c, el=el):
                            e.matmul(pas[h % 2][:, :],
                                     vTall[:, g * S + c * 128: g * S + c * 128 + 128],
                                     Eb[:, el, :], start=(c == 0), stop=(c == NCH - 1))
                            return e.matmul(pl0[0:1, :], ones[:, :],
                                            Eb[:, el, :], start=(c == 0),
                                            stop=(c == NCH - 1))
                        w = [(sAx, sAx.v)]
                        if c == 0:
                            prev = (tq, h - 1) if h >= 1 else (tq - 1, 15)
                            if prev in epmark:
                                w.append((sEp, epmark[prev]))
                        p.o("tensor", avl, waits=w, incs=[(sAv, 1)])
                        avmark[i] = sAv.v
                    def evh(e, h=h, h8=h8):
                        e.tensor_copy(ao[:, h8 * 512:(h8 + 1) * 512], pas[h % 2][:, :])
                        return e.tensor_copy(lrows[0:1, h * 512:(h + 1) * 512],
                                             pl0[0:1, :])
                    if NOEVH:
                        p.o("vector", lambda e: None, waits=[(sAv, None)], incs=[(sEp, 1)])
                    else:
                        p.o("vector", evh, waits=[(sAv, None)], incs=[(sEp, 1)])
                    epmark[(tq, h)] = sEp.v
                if not NOBOUNCE:
                    p.o("sync", lambda e, hf=hf: e.dma_start(
                        scrh[hf * 4096: hf * 4096 + 4096],
                        lrows[0:1, hf * 4096:(hf + 1) * 4096]),
                        waits=[(sEp, None)], incs=[(d2, 16)])
                    p.o("sync", lambda e, hf=hf: e.dma_start(
                        lcols[:, hf * 32:(hf + 1) * 32],
                        AP(scrh, hf * 4096, [[1, 128], [512, 8], [128, 4]]),
                        allow_slow_non_contiguous=True),
                        waits=[(d2, d2.v), (sEs, sEs.v)], incs=[(d2, 16)])
                p.o("vector", lambda e, hf=hf: e.reciprocal(
                    lcols[:, hf * 32:(hf + 1) * 32], lcols[:, hf * 32:(hf + 1) * 32]),
                    waits=[(d2, None)], incs=[(sEr, 1)])
                er_v = sEr.v
                for s4 in range(0 if NOEPI else 4):
                    si = (tq * 2 + hf) * 4 + s4
                    ys = sems["y%d" % (si % 2)]
                    w = [(sEs, xomark[si - 2])] if si - 2 in xomark else []
                    p.o("sync", lambda e, tq=tq, s4=s4, si=si, hf=hf: e.dma_start(
                        xot[:, si % 2, :],
                        xosrc[tq * 4 + s4, :, hf * 1024:(hf + 1) * 1024]),
                        waits=w, incs=[(ys, 16)])
                    d3_v = (ys, ys.v)
                    for h8 in range(8):
                        h = hf * 8 + h8
                        i = ei
                        ei += 1
                        w = [(sEp, epmark[(tq, h)])]
                        if i >= 1:
                            w.append((sEs, sEs.v))
                        p.o("tensor", lambda e, h8=h8, s4=s4, i=i: e.transpose(
                            pt0[:, :],
                            ao[:, h8 * 512 + s4 * 128: h8 * 512 + s4 * 128 + 128],
                            IDfs[:, :]), waits=w, incs=[(sTp, 1)])
                        osm = sems["o%d" % (i % 2)]
                        wv = [(sTp, sTp.v), (sEr, er_v), d3_v]
                        if i >= 2:
                            wv.append((osm, osm.v))
                        def stt(e, h=h, s4=s4, i=i, hf=hf, h8=h8, si=si):
                            return e.scalar_tensor_tensor(
                                x1b[:, i % 2, :], pt0[:, :],
                                lcols[:, hf * 32 + h8 * 4 + s4:
                                      hf * 32 + h8 * 4 + s4 + 1],
                                xot[:, si % 2, h8 * 128:(h8 + 1) * 128],
                                ALU.mult, ALU.add)
                        p.o("vector", stt, waits=wv, incs=[(sEs, 1)])
                        p.o("sync", lambda e, tq=tq, s4=s4, h=h, i=i: e.dma_start(
                            x1h[tq * 512 + s4 * 128: tq * 512 + s4 * 128 + 128,
                                h * 128:(h + 1) * 128], x1b[:, i % 2, :]),
                            waits=[(sEs, sEs.v)], incs=[(osm, 16)])
                    xomark[si] = sEs.v
            if dbg:
                p.o("sync", lambda e, tq=tq: e.dma_start(
                    dbgl[tq:tq + 1, :], lrows[0:1, :]),
                    waits=[(sEp, None)], incs=[(sOw, 16)])
                if tq == 0:
                    p.o("sync", lambda e: e.dma_start(
                        dbga[:, :], ao[:, 5 * 512:8 * 512]),
                        waits=[(sEp, None)], incs=[(sOw, 16)])
            eptq_end[tq] = sEs.v
        p.o("sync", lambda e: None,
            waits=[(sOw, None), (sems["o0"], None), (sems["o1"], None)])
        p.flush("pa")
        if dbg:
            p.o("sync", lambda e: e.dma_start(dbgall[:, :], qall[:, :]),
                incs=[(sOw, 16)])
            p.o("sync", lambda e: None, waits=[(sOw, None)])
            p.flush("dbgpa")
    astck.close()

    if DEV_STAGE < 3:
        stck.close()
        return nc

    with (
        nc.sbuf_tensor("f1xb", [128, 2, E], F32) as xb,
        nc.sbuf_tensor("f1dm", [128, E], F32) as dm,
        nc.sbuf_tensor("f1st", [128, 8, 8], F32) as stt1,
        nc.sbuf_tensor("f1xn", [128, 2, E], BF16) as xn,
        nc.sbuf_tensor("f1xf", [128, 16 * 512], BF16) as xnf,
        nc.sbuf_tensor("f1ub", [128, 2, 512], BF16) as ub,
        nc.psum_tensor("f1tba", [128, 128], BF16) as ptba,
        nc.psum_tensor("f1tbb", [128, 128], BF16) as ptbb,
        nc.psum_tensor("f1p0", [128, 512], F32) as pu0,
        nc.psum_tensor("f1p1", [128, 512], F32) as pu1,
    ):
        pus = (pu0, pu1)
        ptbs = (ptba, ptbb)
        sN, sTp, sTv, sMM, sEv = (sems["sn"], sems["tp"], sems["tv"],
                                  sems["mm"], sems["ev"])
        d4 = sems["d4"]
        x1src = x1h[:, :].rearrange("(a p) f -> a p f", p=128)
        nmark, evm, dmam = {}, {}, {}
        tv0 = sTv.v
        for g in range(4):
            for s4 in range(4):
                ti = g * 4 + s4
                sl = ti % 2
                xs = sems["x%d" % sl]
                w = [(sN, nmark[ti - 2])] if ti - 2 in nmark else []
                p.o("sync", lambda e, ti=ti, sl=sl: e.dma_start(
                    xb[:, sl, :], x1src[ti, :, :]), waits=w, incs=[(xs, 16)])
                emit_ln(xb[:, sl, :], xn[:, sl, :], stt1[:, ti % 8, :], dm[:, :],
                        [(xs, xs.v)])
                nmark[ti] = sN.v
                for ec in range(16):
                    i = ti * 16 + ec
                    w = [(sN, nmark[ti])]
                    if i >= 2:
                        w.append((sTv, tv0 + i - 1))
                    p.o("tensor", lambda e, sl=sl, ec=ec, i=i: e.transpose(
                        ptbs[i % 2][:, :],
                        xn[:, sl, ec * 128:ec * 128 + 128], IDbs[:, :]),
                        waits=w, incs=[(sTp, 1)])
                    p.o("vector", lambda e, ec=ec, s4=s4, i=i: e.tensor_copy(
                        xnf[:, ec * 512 + s4 * 128: ec * 512 + s4 * 128 + 128],
                        ptbs[i % 2][:, :]),
                        waits=[(sTp, None)], incs=[(sTv, 1)])
            for tau in range(16):
                it = g * 16 + tau
                w = [(sTv, tv0 + (g + 1) * 64)]
                if it >= 2:
                    w.append((sEv, evm[it - 2]))
                p.o("tensor", lambda e, tau=tau, it=it: e.matmul(
                    pus[it % 2][:, :], H128s[:, :], xnf[:, tau * 512:tau * 512 + 512],
                    start=True, stop=True), waits=w, incs=[(sMM, 1)])
                us = sems["u%d" % (it % 2)]
                wv = [(sMM, sMM.v)]
                if it >= 2:
                    wv.append((us, us.v))
                p.o("scalar", lambda e, it=it: e.activation(
                    ub[:, it % 2, :], pus[it % 2][:, :], AF.Copy),
                    waits=wv, incs=[(sEv, 1)])
                evm[it] = sEv.v
                p.o("sync", lambda e, tau=tau, g=g, it=it: e.dma_start(
                    u1h[tau, :, g * 512:(g + 1) * 512], ub[:, it % 2, :]),
                    waits=[(sEv, sEv.v)], incs=[(us, 16)])
        p.o("sync", lambda e: None,
            waits=[(sems["u0"], None), (sems["u1"], None)])
        p.flush("pf1")

    fstck = contextlib.ExitStack()
    xq1 = fstck.enter_context(nc.sbuf_tensor("xq1", [128, 16 * TO], BF16))
    with (
        nc.sbuf_tensor("f2rb", [128, 16 * 512], BF16) as rb,
        nc.sbuf_tensor("f2z", [128, 16 * 512], F32) as z1,
        nc.sbuf_tensor("f2mx", [128, 512], F32) as mxj,
        nc.sbuf_tensor("f2mc", [128, 16], F32) as mc,
        nc.sbuf_tensor("f2qr", [128, 1024], F32) as qr,
        nc.sbuf_tensor("f2tq", [128, 512], F32) as tqb,
        nc.psum_tensor("f2z0", [128, 512], F32) as pz0,
        nc.psum_tensor("f2z1", [128, 512], F32) as pz1,
        nc.psum_tensor("f2t0", [128, 128], F32) as pt0,
        nc.psum_tensor("f2t1", [128, 128], F32) as pt1,
    ):
        pzs, pts = (pz0, pz1), (pt0, pt1)
        sMM, sEv, sV, sQ = sems["mm"], sems["ev"], sems["sv"], sems["f1"]
        sTp = sems["tp"]
        d4 = sems["d4"]
        for t5 in range(4):
            w = [(sMM, sMM.v)] if t5 > 0 else []
            for cc in range(8):
                p.o("sync", lambda e, t5=t5, cc=cc: e.dma_start(
                    rb[cc * 16:(cc + 1) * 16, :].rearrange("p (j t) -> p j t", j=16),
                    AP(u1h, t5 * 512 + cc * 16 * TO,
                       [[128 * TO, 16], [TO, 16], [1, 512]])),
                    waits=w, incs=[(ld, 16)])
            for jr in range(16):
                w = [(ld, None)]
                if t5 * 16 + jr >= 2:
                    w.append((sEv, sEv.v - 1))
                p.o("tensor", lambda e, jr=jr: e.matmul(
                    pzs[jr % 2][:, :], H16ps[:, :], rb[:, jr * 512:jr * 512 + 512],
                    start=True, stop=True), waits=w, incs=[(sMM, 1)])
                p.o("vector", lambda e, jr=jr: e.tensor_copy(
                    z1[:, jr * 512:jr * 512 + 512], pzs[jr % 2][:, :]),
                    waits=[(sMM, sMM.v)], incs=[(sEv, 1)])
            p.o("vector", lambda e: e.tensor_reduce(
                mxj[:, :], z1[:, :].rearrange("p (j t) -> p t j", j=16),
                AX.X, ALU.max, apply_absolute_value=True),
                waits=[(sEv, None)], incs=[(sV, 1)])
            for s4 in range(4):
                w = [(sV, sV.v)]
                p.o("tensor", lambda e, s4=s4: e.transpose(
                    pts[s4 % 2][:, :], mxj[:, s4 * 128:s4 * 128 + 128], IDfs[:, :]),
                    waits=w, incs=[(sTp, 1)])
                p.o("vector", lambda e, s4=s4: e.tensor_reduce(
                    mc[:, s4:s4 + 1], pts[s4 % 2][:, :], AX.X, ALU.max),
                    waits=[(sTp, None)], incs=[(sV, 1)])
            def smath(e):
                e.tensor_scalar(mc[:, 4:8], mc[:, 0:4], 1.0 / SQ2048, EPS_Q,
                                ALU.mult, ALU.max)
                e.drain()
                e.reciprocal(mc[:, 8:12], mc[:, 4:8])
                e.drain()
                e.tensor_scalar(mc[:, 8:12], mc[:, 8:12], 127.0 / SQ2048, None,
                                ALU.mult)
                return e.tensor_scalar(mc[:, 12:16], mc[:, 4:8], 1.0 / 127.0, None,
                                       ALU.mult)
            p.o("vector", smath, waits=[(sV, None)], incs=[(sV, 1)])
            p.o("sync", lambda e: e.dma_start(
                AP(scrh, 0, [[1, 128], [128, 8]]), mc[:, 8:16],
                allow_slow_non_contiguous=True),
                waits=[(sV, None), (d4, None)], incs=[(d4, 16)])
            p.o("sync", lambda e: e.dma_start(
                qr[:, :], AP(scrh, 0, [[0, 128], [1, 1024]])),
                waits=[(d4, d4.v), (sQ, sQ.v)], incs=[(d4, 16)])
            for jr in range(16):
                def quant(e, jr=jr, t5=t5):
                    e.tensor_tensor(tqb[:, :], z1[:, jr * 512:jr * 512 + 512],
                                    qr[:, 0:512], ALU.mult)
                    e.drain()
                    e.tensor_scalar(tqb[:, :], tqb[:, :], MAGIC, MAGIC,
                                    ALU.add, ALU.subtract)
                    e.drain()
                    return e.tensor_tensor(
                        xq1[:, jr * TO + t5 * 512: jr * TO + t5 * 512 + 512],
                        tqb[:, :], qr[:, 512:1024], ALU.mult)
                p.o("vector", quant, waits=[(d4, None)], incs=[(sQ, 1)])
        if dbg:
            p.o("sync", lambda e: e.dma_start(xq1d[:, :], xq1[:, :]),
                waits=[(sQ, None)], incs=[(d4, 16)])
        p.o("sync", lambda e: None, waits=[(d4, None), (sQ, None)])
        p.flush("pf2")

    if DEV_STAGE < 4:
        fstck.close()
        stck.close()
        return nc

    with (
        nc.sbuf_tensor("f3wf", [128, 4, 512], F32) as wf,
        nc.sbuf_tensor("f3w1", [128, 4, 512], F32) as wt1,
        nc.sbuf_tensor("f3w2", [128, 4, 512], F32) as wt2,
        nc.sbuf_tensor("f3gq", [128, 16, 512], BF16) as wgq,
        nc.sbuf_tensor("f3uq", [128, 16, 512], BF16) as wuq,
        nc.sbuf_tensor("f3sg", [128, 4, 512], F32) as sg,
        nc.sbuf_tensor("f3hb", [128, 4, 512], BF16) as hb,
        nc.sbuf_tensor("f3ub", [128, 2, 512], BF16) as ub2,
        nc.sbuf_tensor("f3df", [128, 2, 2048], F32) as dwf,
        nc.sbuf_tensor("f3d1", [128, 2, 2048], F32) as dw1,
        nc.sbuf_tensor("f3d2", [128, 2, 2048], F32) as dw2,
        nc.sbuf_tensor("f3dq", [128, 2, 2048], BF16) as dwq,
        nc.sbuf_tensor("f3dbg", [128, 512], F32) as dnfdbg,
        nc.psum_tensor("f3g0", [128, 512], F32) as pg0,
        nc.psum_tensor("f3g1", [128, 512], F32) as pg1,
        nc.psum_tensor("f3u0", [128, 512], F32) as pu0,
        nc.psum_tensor("f3u1", [128, 512], F32) as pu1,
        nc.psum_tensor("f3U0", [128, 512], F32) as pU0,
        nc.psum_tensor("f3U1", [128, 512], F32) as pU1,
    ):
        pgs, pus2, pUs = (pg0, pg1), (pu0, pu1), (pU0, pU1)
        sWl, sP1, sP2, sP3 = sems["wl"], sems["f2"], sems["f3"], sems["f4"]
        sMM, sSg, sHb, sUe = sems["mm"], sems["sa"], sems["f5"], sems["ev"]
        d4, dW, dS = sems["d4"], sems["g1"], sems["g2"]
        p3m = {}
        t1m, t2m, t3m = {}, {}, {}
        stashm = {}
        mm_ob = {}

        def emit_tern(src_ap, inv_col, dst_ap, nb, slot, kind, key, nslots,
                      extra_p3_waits):
            fbuf, b1, b2 = (wf, wt1, wt2) if nb == 512 else (dwf, dw1, dw2)
            ldc = sems[("w%d" if nb == 512 else "x%d") % slot]
            kprev = (kind, key - nslots)
            w = [(sP1, t1m[kprev])] if kprev in t1m else []
            if os.environ.get("MK_TERNSER", "0") == "1":
                w = [(sP3, sP3.v)]
            p.o("sync" if nb == 512 else "gpsimd",
                lambda e: e.dma_start(fbuf[:, slot, 0:nb], src_ap),
                waits=w, incs=[(ldc, 16)])
            w = [(ldc, ldc.v)]
            if kprev in t2m:
                w.append((sP2, t2m[kprev]))
            p.o("vector", lambda e: e.tensor_scalar(
                b1[:, slot, 0:nb], fbuf[:, slot, 0:nb], inv_col, MAGIC,
                ALU.mult, ALU.add), waits=w, incs=[(sP1, 1)])
            t1m[(kind, key)] = sP1.v
            w = [(sP1, sP1.v)]
            if kprev in t3m:
                w.append((sP3, t3m[kprev]))
            def p2p3(e, slot=slot, nb=nb, dst_ap=dst_ap):
                e.drain()
                e.tensor_scalar(b2[:, slot, 0:nb], b1[:, slot, 0:nb], MAGIC, 1.0,
                                ALU.subtract, ALU.min).then_inc(sP2.h, 1)
                e.drain()
                return e.tensor_scalar(dst_ap, b2[:, slot, 0:nb], -1.0, None,
                                       ALU.max)
            sP2.v += 1
            p.o("vector", p2p3, waits=w + extra_p3_waits, incs=[(sP3, 1)])
            t2m[(kind, key)] = sP2.v
            t3m[(kind, key)] = sP3.v
            return sP3.v

        dump_fence = [0]
        tern_fence = [0]
        def dump_tern0(v3):
            if not dbg:
                return
            p.o("sync", lambda e: e.dma_start(dbgg[:, 0:512], wf[:, 0, :]),
                waits=[(sP3, v3)], incs=[(d4, 16)])
            p.o("sync", lambda e: e.dma_start(dbgg[:, 512:1024], wt1[:, 0, :]),
                waits=[(sP3, v3)], incs=[(d4, 16)])
            p.o("gpsimd", lambda e: e.dma_start(dbgw[:, 0:512], wt2[:, 0, :]),
                waits=[(sP3, v3)], incs=[(d4, 16)])
            tern_fence[0] = d4.v
            p.o("vector", lambda e: None, waits=[(d4, d4.v)], incs=[(sP1, 1)])
            t1m[("g", 0)] = sP1.v
        def dump_tern(ob, last_tern):
            if not (dbg and ob == 0):
                return
            p.o("sync", lambda e: e.dma_start(dbgw[:, 0:512], wgq[:, 0, :]),
                waits=[(sP3, last_tern)], incs=[(d4, 16)])
            p.o("sync", lambda e: e.dma_start(dbgw[:, 512:1024], wuq[:, 0, :]),
                waits=[(sP3, last_tern)], incs=[(d4, 16)])
            dump_fence[0] = d4.v

        def dump_gu(ob, oc, tg, sl, mmv):
            if not (dbg and ob == 0 and oc == 0 and tg == 0):
                return
            p.o("scalar", lambda e, sl=sl: e.activation(
                dnfdbg[:, :], pgs[sl][:, :], AF.Copy),
                waits=[(sMM, mmv)], incs=[(sSg, 1)])
            p.o("sync", lambda e: e.dma_start(dbgg[:, 0:512], dnfdbg[:, :]),
                waits=[(sSg, sSg.v)], incs=[(d4, 16)])
            p.o("scalar", lambda e, sl=sl: e.activation(
                dnfdbg[:, :], pus2[sl][:, :], AF.Copy),
                waits=[(sSg, sSg.v), (d4, d4.v)], incs=[(sSg, 1)])
            p.o("sync", lambda e: e.dma_start(dbgg[:, 512:1024], dnfdbg[:, :]),
                waits=[(sSg, sSg.v)], incs=[(d4, 16)])

        def pf3_body():
          jjd = 0
          for ob in range(16):
            for jr in range(16):
                for wi, (wsrc, invc, dst) in enumerate(
                        ((wgh, wcol[:, WIG:WIG + 1], wgq),
                         (wuh, wcol[:, WIU:WIU + 1], wuq))):
                    key = (ob * 16 + jr) * 2 + wi
                    sap = AP(wsrc, jr * FI + ob * 512,
                             [[16 * FI, 8], [128 * FI, 16], [1, 512]])
                    ew = []
                    if ob > 0:
                        ew.append((sMM, mm_ob[ob - 1]))
                    if dbg and ob == 1 and dump_fence[0]:
                        ew.append((d4, dump_fence[0]))
                    if dbg and key == 4 and tern_fence[0]:
                        ew.append((d4, tern_fence[0]))
                    emit_tern(sap, invc, dst[:, jr, :], 512, key % 4, "g", key,
                              4, ew)
                    if key == 0:
                        dump_tern0(sems["f4"].v)
            last_tern = sems["f4"].v
            dump_tern(ob, last_tern)
            for _ in range(4):
                jj = jjd
                jjd += 1
                sap = AP(wdh, jj * E, [[64 * E, 2], [128 * E, 64], [1, 2048]])
                ew = [(dS, stashm[jj - 2])] if jj >= 2 else []
                v = emit_tern(sap, wcol[:, WID:WID + 1], dwq[:, jj % 2, :], 2048,
                              jj % 2, "d", jj, 2, ew)
                p.o("gpsimd", lambda e, jj=jj: e.dma_start(
                    AP(wdqh, jj * 16384, [[128, 128], [16384 * 64, 16], [1, 128]]),
                    dwq[:, jj % 2, :].rearrange("p (a b) -> p a b", a=16)),
                    waits=[(sP3, v)], incs=[(dS, 16)])
                stashm[jj] = dS.v
            for oc in range(4):
                tau = ob * 4 + oc
                for tg in range(4):
                    it = tau * 4 + tg
                    sl = it % 2
                    fl = it % 4
                    def gmm(e, oc=oc, tg=tg, sl=sl, wq=wgq, ps=pgs):
                        ins = None
                        for jr in range(16):
                            ins = e.matmul(
                                ps[sl][:, :], wq[:, jr, oc * 128:oc * 128 + 128],
                                xq1[:, jr * TO + tg * 512: jr * TO + tg * 512 + 512],
                                start=(jr == 0), stop=(jr == 15))
                        return ins
                    w = [(sP3, last_tern)]
                    if it >= 2:
                        w.append((sSg, p3m[("sg", it - 2)]))
                    p.o("tensor", lambda e, gmm=gmm: gmm(e), waits=w, incs=[(sMM, 1)])
                    wu2 = []
                    if it >= 2:
                        wu2.append((sHb, p3m[("hb", it - 2)]))
                    p.o("tensor", lambda e, oc=oc, tg=tg, sl=sl: gmm(
                        e, oc=oc, tg=tg, sl=sl, wq=wuq, ps=pus2),
                        waits=wu2, incs=[(sMM, 1)])
                    mmv = sMM.v
                    wsg2 = [(sMM, mmv)]
                    if it >= 4:
                        wsg2.append((sHb, p3m[("hb", it - 4)]))
                    dump_gu(ob, oc, tg, sl, mmv)
                    p.o("scalar", lambda e, sl=sl, fl=fl: e.activation(
                        sg[:, fl, :], pgs[sl][:, :], AF.Silu,
                        scale=wcol[:, WSG:WSG + 1]), waits=wsg2, incs=[(sSg, 1)])
                    p3m[("sg", it)] = sSg.v
                    whb = [(sSg, sSg.v)]
                    if it >= 4:
                        whb.append((sMM, p3m[("ua", it - 4)]))
                    p.o("vector", lambda e, sl=sl, fl=fl: e.scalar_tensor_tensor(
                        hb[:, fl, :], pus2[sl][:, :], wcol[:, WSU:WSU + 1],
                        sg[:, fl, :], ALU.mult, ALU.mult), waits=whb, incs=[(sHb, 1)])
                    p3m[("hb", it)] = sHb.v
                    w = [(sHb, sHb.v)]
                    if it >= 2:
                        w.append((sUe, p3m[("ue", it - 2)]))
                    p.o("tensor", lambda e, sl=sl, fl=fl: e.matmul(
                        pUs[sl][:, :], H128s[:, :], hb[:, fl, :],
                        start=True, stop=True), waits=w, incs=[(sMM, 1)])
                    p3m[("ua", it)] = sMM.v
                    us = sems["u%d" % sl]
                    wue = [(sMM, sMM.v)]
                    if it >= 2:
                        wue.append((us, us.v))
                    p.o("scalar", lambda e, sl=sl: e.activation(
                        ub2[:, sl, :], pUs[sl][:, :], AF.Copy),
                        waits=wue, incs=[(sUe, 1)])
                    p3m[("ue", it)] = sUe.v
                    p.o("sync", lambda e, tau=tau, tg=tg, sl=sl: e.dma_start(
                        AP(u2h, tau * 64 * TO + tg * 512,
                           [[64 * 64 * TO, 2], [TO, 64], [1, 512]]),
                        ub2[:, sl, :].rearrange("p (g j) -> p g j", g=2)),
                        waits=[(sUe, sUe.v)], incs=[(us, 16)])
            mm_ob[ob] = sMM.v
        pf3_body()
        p.o("sync", lambda e: None,
            waits=[(sems["u0"], None), (sems["u1"], None), (dS, None),
                   (d4, None)])
        p.flush("pf3")
    fstck.close()

    if DEV_STAGE < 5:
        stck.close()
        return nc

    with (
        nc.sbuf_tensor("pdrb", [128, 64 * 256], BF16) as rb2,
        nc.sbuf_tensor("pdmx", [128, 1024], F32) as mx2,
        nc.sbuf_tensor("pdmc", [128, 32], F32) as mc2,
        nc.sbuf_tensor("pdqr", [128, 1024], F32) as qr2,
        nc.sbuf_tensor("pdtq", [128, 512], F32) as qtb,
        nc.sbuf_tensor("pdxq", [128, 64 * 1024], BF16) as xq2,
        nc.sbuf_tensor("pdwd", [128, 64 * 128], BF16) as wdt,
        nc.sbuf_tensor("pddn", [128, 2, 512], F32) as dnf,
        nc.sbuf_tensor("pdx1", [128, 2, 128], F32) as x1t,
        nc.sbuf_tensor("pdob", [128, 2, 128], F32) as obf,
        nc.psum_tensor("pdz0", [128, 512], F32) as qz0,
        nc.psum_tensor("pdz1", [128, 512], F32) as qz1,
        nc.psum_tensor("pdt0", [128, 128], F32) as qt0,
        nc.psum_tensor("pdt1", [128, 128], F32) as qt1,
        nc.psum_tensor("pdd0", [128, 512], F32) as pd0,
        nc.psum_tensor("pdd1", [128, 512], F32) as pd1,
    ):
        qzs, qts, pds = (qz0, qz1), (qt0, qt1), (pd0, pd1)
        sMM, sEv, sV, sQ2 = sems["mm"], sems["ev"], sems["sv"], sems["f1"]
        sEs, sOw, sTp = sems["es"], sems["ow"], sems["tp"]
        d4, d5, dL = sems["d4"], sems["d2"], sems["g1"]
        tpm = {}
        for half in range(2):
            hbase = half * 1024
            for t4 in range(4):
                t0 = hbase + t4 * 256
                w = [(sMM, sMM.v)]
                p.o("sync", lambda e, t0=t0: e.dma_start(
                    rb2[:, :].rearrange("p (j t) -> p j t", j=64),
                    AP(u2h, t0, [[64 * TO, 128], [TO, 64], [1, 256]])),
                    waits=w, incs=[(ld, 16)])
                for ch in range(32):
                    w = [(ld, None)]
                    if t4 * 32 + ch >= 2:
                        w.append((sV, sV.v - 1))
                    if t4 == 0 and ch < 2:
                        w.append((sQ2, sQ2.v))
                    def mmz(e, ch=ch):
                        rhs = AP(rb2, ch * 8, [[64 * 256, 128], [1, 8], [256, 64]])
                        return e.matmul(qzs[ch % 2][:, :].rearrange(
                            "p (t j) -> p t j", t=8), H64ps[:, :], rhs,
                            start=True, stop=True)
                    p.o("tensor", mmz, waits=w, incs=[(sMM, 1)])
                    p.o("vector", lambda e, ch=ch, t4=t4: e.tensor_reduce(
                        mx2[:, t4 * 256 + ch * 8: t4 * 256 + ch * 8 + 8],
                        qzs[ch % 2][:, :].rearrange("p (t j) -> p t j", t=8),
                        AX.X, ALU.max, apply_absolute_value=True),
                        waits=[(sMM, sMM.v)], incs=[(sV, 1)])
            for s8 in range(8):
                p.o("tensor", lambda e, s8=s8: e.transpose(
                    qts[s8 % 2][:, :], mx2[:, s8 * 128:s8 * 128 + 128], IDfs[:, :]),
                    waits=[(sV, sV.v)], incs=[(sTp, 1)])
                p.o("vector", lambda e, s8=s8: e.tensor_reduce(
                    mc2[:, s8:s8 + 1], qts[s8 % 2][:, :], AX.X, ALU.max),
                    waits=[(sTp, None)], incs=[(sV, 1)])
            def smath2(e):
                e.tensor_scalar(mc2[:, 8:16], mc2[:, 0:8], 1.0 / SQ8192, EPS_Q,
                                ALU.mult, ALU.max)
                e.drain()
                e.reciprocal(mc2[:, 16:24], mc2[:, 8:16])
                e.drain()
                e.tensor_scalar(mc2[:, 16:24], mc2[:, 16:24], 127.0 / SQ8192, None,
                                ALU.mult)
                e.tensor_scalar(mc2[:, 24:32], mc2[:, 8:16], 1.0 / 127.0, None,
                                ALU.mult)
                e.drain()
                return e.tensor_tensor(
                    mc2[:, 24:32], mc2[:, 24:32],
                    AP(wcol, WSD, [[6, 128], [0, 8]]), ALU.mult)
            p.o("vector", smath2, waits=[(sV, None)], incs=[(sV, 1)])
            p.o("sync", lambda e: e.dma_start(
                AP(scrh, 0, [[1, 128], [128, 8]]), mc2[:, 16:24],
                allow_slow_non_contiguous=True),
                waits=[(sV, None), (d4, None)], incs=[(d4, 16)])
            p.o("sync", lambda e: e.dma_start(
                qr2[:, :], AP(scrh, 0, [[0, 128], [1, 1024]])),
                waits=[(d4, d4.v), (sQ2, sQ2.v)], incs=[(d4, 16)])
            for t4 in range(4):
                t0 = hbase + t4 * 256
                p.o("sync", lambda e, t0=t0: e.dma_start(
                    rb2[:, :].rearrange("p (j t) -> p j t", j=64),
                    AP(u2h, t0, [[64 * TO, 128], [TO, 64], [1, 256]])),
                    waits=[(sMM, sMM.v)], incs=[(ld, 16)])
                for ch in range(32):
                    w = [(ld, None), (d4, None)]
                    if t4 * 32 + ch >= 2:
                        w.append((sQ2, sQ2.v - 1))
                    def mmz2(e, ch=ch):
                        rhs = AP(rb2, ch * 8, [[64 * 256, 128], [1, 8], [256, 64]])
                        return e.matmul(qzs[ch % 2][:, :].rearrange(
                            "p (t j) -> p t j", t=8), H64ps[:, :], rhs,
                            start=True, stop=True)
                    p.o("tensor", mmz2, waits=w, incs=[(sMM, 1)])
                    def qnt(e, ch=ch, t4=t4):
                        tloc = t4 * 256 + ch * 8
                        qsl = AP(qr2, tloc, [[1024, 128], [1, 8], [0, 64]])
                        e.tensor_tensor(
                            qtb[:, :].rearrange("p (t j) -> p t j", t=8),
                            qzs[ch % 2][:, :].rearrange("p (t j) -> p t j", t=8),
                            qsl, ALU.mult)
                        e.drain()
                        return e.tensor_scalar(
                            AP(xq2, tloc, [[64 * 1024, 128], [1, 8], [1024, 64]]),
                            qtb[:, :].rearrange("p (t j) -> p t j", t=8),
                            MAGIC, MAGIC, ALU.add, ALU.subtract)
                    p.o("vector", qnt, waits=[(sMM, sMM.v)], incs=[(sQ2, 1)])
            for o2c in range(16):
                sl = 0
                w = [(sQ2, sQ2.v), (sMM, sMM.v)]
                p.o("sync", lambda e, o2c=o2c, sl=sl: e.dma_start(
                    wdt[:, :], AP(wdqh, o2c * 64 * 16384,
                                      [[128, 128], [16384, 64], [1, 128]])),
                    waits=w, incs=[(dL, 16)])
                for t5 in range(2):
                    it = (half * 16 + o2c) * 2 + t5
                    dsl = it % 2
                    def dmm(e, sl=sl, t5=t5, dsl=dsl):
                        ins = None
                        for j in range(64):
                            ins = e.matmul(
                                pds[dsl][:, :], wdt[:, j * 128:j * 128 + 128],
                                xq2[:, j * 1024 + t5 * 512: j * 1024 + t5 * 512 + 512],
                                start=(j == 0), stop=(j == 63))
                        return ins
                    w = [(dL, None)]
                    if it >= 2:
                        w.append((sEv, sEv.v - 1))
                    p.o("tensor", dmm, waits=w, incs=[(sMM, 1)])
                    wev = [(sMM, sMM.v)]
                    if it >= 2:
                        wev.append((sTp, tpm[it - 2]))
                    p.o("vector", lambda e, dsl=dsl: e.tensor_copy(
                        dnf[:, dsl, :], pds[dsl][:, :]), waits=wev, incs=[(sEv, 1)])
                    evv = sEv.v
                    for s4 in range(4):
                        tb = t5 * 4 + s4
                        i2 = it * 4 + s4
                        trow = hbase + tb * 128
                        ys = sems["y%d" % (i2 % 2)]
                        wx = [(sEs, sEs.v - 1)] if i2 >= 2 else []
                        p.o("sync", lambda e, trow=trow, o2c=o2c, i2=i2: e.dma_start(
                            x1t[:, i2 % 2, :],
                            x1h[trow:trow + 128, o2c * 128:o2c * 128 + 128]),
                            waits=wx, incs=[(ys, 16)])
                        wt = [(sEv, evv)]
                        if i2 >= 2:
                            wt.append((sEs, sEs.v - 1))
                        p.o("tensor", lambda e, dsl=dsl, s4=s4, i2=i2: e.transpose(
                            qts[i2 % 2][:, :],
                            dnf[:, dsl, s4 * 128:s4 * 128 + 128], IDfs[:, :]),
                            waits=wt, incs=[(sTp, 1)])
                        if s4 == 3:
                            tpm[it] = sTp.v
                        osm = sems["o%d" % (i2 % 2)]
                        ws = [(sTp, sTp.v), (ys, ys.v)]
                        if i2 >= 2:
                            ws.append((osm, osm.v))
                        def sttd(e, i2=i2, tb=tb):
                            return e.scalar_tensor_tensor(
                                obf[:, i2 % 2, :], qts[i2 % 2][:, :],
                                mc2[:, 24 + tb:24 + tb + 1],
                                x1t[:, i2 % 2, :], ALU.mult, ALU.add)
                        p.o("vector", sttd, waits=ws, incs=[(sEs, 1)])
                        p.o("sync", lambda e, trow=trow, o2c=o2c, i2=i2: e.dma_start(
                            out_h[trow:trow + 128, o2c * 128:o2c * 128 + 128],
                            obf[:, i2 % 2, :]), waits=[(sEs, sEs.v)],
                            incs=[(osm, 16)])
        p.o("sync", lambda e: None,
            waits=[(sems["o0"], None), (sems["o1"], None)])
        p.flush("pd")

    stck.close()
    return nc


_CACHE = {}


def _mk_inputs(inputs):
    bf = ml_dtypes.bfloat16
    x = np.asarray(inputs["L0_attn_x"], dtype=np.float32)
    cons = {
        "cH128": _hadamard(128).astype(bf),
        "cH16p": np.kron(np.eye(8, dtype=np.float32), _hadamard(16)).astype(bf),
        "cH64p": np.kron(np.eye(2, dtype=np.float32), _hadamard(64)).astype(bf),
        "cIDf": np.eye(128, dtype=np.float32),
        "cIDb": np.eye(128, dtype=np.float32).astype(bf),
        "conesb": np.ones((128, 1), dtype=np.float32).astype(bf),
        "Wq": np.ascontiguousarray(np.asarray(inputs["L0_attn_Wq"], dtype=np.float32)),
        "Wk": np.ascontiguousarray(np.asarray(inputs["L0_attn_Wk"], dtype=np.float32)),
        "Wv": np.ascontiguousarray(np.asarray(inputs["L0_attn_Wv"], dtype=np.float32)),
        "bqT": np.ascontiguousarray(np.asarray(inputs["L0_attn_bq"], dtype=np.float32).T),
        "bkT": np.ascontiguousarray(np.asarray(inputs["L0_attn_bk"], dtype=np.float32).T),
        "bvT": np.ascontiguousarray(np.asarray(inputs["L0_attn_bv"], dtype=np.float32).T),
        "wgT": np.ascontiguousarray(np.asarray(inputs["L0_ffn_w_gate"], dtype=np.float32).T),
        "wuT": np.ascontiguousarray(np.asarray(inputs["L0_ffn_w_up"], dtype=np.float32).T),
        "wdT": np.ascontiguousarray(np.asarray(inputs["L0_ffn_w_down"], dtype=np.float32).T),
    }
    masks = {}
    for qb in (0, TO):
        cc = np.arange(MASK_W)[None, :] - MASK_Z + qb
        ii = np.arange(128)[:, None]
        masks[qb] = (cc >= ii).astype(np.float32).astype(bf)
    in_maps = []
    for c in range(8):
        b, half = c // 2, c % 2
        m = dict(cons)
        m["x_full"] = np.ascontiguousarray(x[b])
        m["x_own"] = np.ascontiguousarray(x[b, half * TO:(half + 1) * TO])
        m["maskC"] = masks[half * TO]
        in_maps.append(m)
    return in_maps


def _get_nc():
    if "nc" not in _CACHE:
        nc = bass.Bass("TRN2", target_bir_lowering=False)
        build(nc)
        mybir.codegen_inst_isa_subclasses(nc)
        _CACHE["nc"] = nc
    return _CACHE["nc"]


def kernel(**inputs):
    nc = _get_nc()
    in_maps = _mk_inputs(inputs)
    res = run_bass_kernel_spmd(nc, in_maps, core_ids=list(range(8)))
    out = np.empty((B, S, E), dtype=np.float32)
    for c in range(8):
        b, half = c // 2, c % 2
        out[b, half * TO:(half + 1) * TO] = res.results[c]["out"]
    return out


# version 14
# speedup vs baseline: 1.3274x; 1.0071x over previous
import os
import numpy as np
import ml_dtypes
import concourse.bass as bass
import concourse.mybir as mybir
from concourse.bass_utils import run_bass_kernel_spmd

F32 = mybir.dt.float32
BF16 = mybir.dt.bfloat16
AF = mybir.ActivationFunctionType
ALU = mybir.AluOpType
AX = mybir.AxisListType
AP = bass.AP

MAGIC = 12582912.0
EPS_LN = 1e-5
EPS_Q = 1e-5
ISQ128 = 0.08838834764831845
SQ2048 = 45.254833995939045
SQ8192 = 90.50966799187809

B, S, E = 4, 4096, 2048
H, D, G = 16, 128, 4
TO = 2048
FI = 8192
MASK_Z = 3968
MASK_W = MASK_Z + 2048

DEV_STAGE = int(os.environ.get("MK_DEV_STAGE", "99"))
NOEPI = os.environ.get("MK_NOEPI", "0") == "1"
NOBOUNCE = os.environ.get("MK_NOBOUNCE", "0") == "1"
NOMASK = os.environ.get("MK_NOMASK", "0") == "1"
NOEXP = os.environ.get("MK_NOEXP", "0") == "1"
NOEVH = os.environ.get("MK_NOEVH", "0") == "1"
ENGS = ("sync", "scalar", "vector", "tensor", "gpsimd")


def _hadamard(n):
    h = np.array([[1.0]], dtype=np.float32)
    while h.shape[0] < n:
        h = np.block([[h, h], [h, -h]])
    return h


class Ctr:
    def __init__(self, nc, name):
        self.h = nc.alloc_semaphore(name=name)
        self.v = 0


class Prog:

    def __init__(self, nc):
        self.nc = nc
        self.q = {k: [] for k in ENGS}

    def o(self, eng, build, waits=(), incs=()):
        wl = [(c.h, (c.v if v is None else v)) for (c, v) in waits]
        il = []
        for (c, n) in incs:
            c.v += n
            il.append((c.h, n))

        def thunk(e, wl=wl, il=il, build=build):
            for (h, v) in wl:
                if v > 0:
                    e.wait_ge(h, v)
            ins = build(e)
            for (h, n) in il:
                if ins is None:
                    ins = e.engine_nop() if hasattr(e, "engine_nop") else e.nop()
                ins.then_inc(h, n)
        self.q[eng].append(thunk)

    def flush(self, name):
        nc = self.nc
        with nc.Block(name) as blk:
            for eng in ENGS:
                th = self.q[eng]
                if th:
                    getattr(blk, eng)(lambda e, th=th: [t(e) for t in th])
        self.q = {k: [] for k in ENGS}


def build(nc):
    dbg = DEV_STAGE < 99
    xf = nc.dram_tensor("x_full", [S, E], F32, kind="ExternalInput")
    xo = nc.dram_tensor("x_own", [TO, E], F32, kind="ExternalInput")
    Wqh = nc.dram_tensor("Wq", [H, E, D], F32, kind="ExternalInput")
    Wkh = nc.dram_tensor("Wk", [G, E, D], F32, kind="ExternalInput")
    Wvh = nc.dram_tensor("Wv", [G, E, D], F32, kind="ExternalInput")
    bqh = nc.dram_tensor("bqT", [D, H], F32, kind="ExternalInput")
    bkh = nc.dram_tensor("bkT", [D, G], F32, kind="ExternalInput")
    bvh = nc.dram_tensor("bvT", [D, G], F32, kind="ExternalInput")
    wgh = nc.dram_tensor("wgT", [E, FI], F32, kind="ExternalInput")
    wuh = nc.dram_tensor("wuT", [E, FI], F32, kind="ExternalInput")
    wdh = nc.dram_tensor("wdT", [FI, E], F32, kind="ExternalInput")
    maskh = nc.dram_tensor("maskC", [128, MASK_W], BF16, kind="ExternalInput")
    cH128 = nc.dram_tensor("cH128", [128, 128], BF16, kind="ExternalInput")
    cH16p = nc.dram_tensor("cH16p", [128, 128], BF16, kind="ExternalInput")
    cH64p = nc.dram_tensor("cH64p", [128, 128], BF16, kind="ExternalInput")
    cIDf = nc.dram_tensor("cIDf", [128, 128], F32, kind="ExternalInput")
    cIDb = nc.dram_tensor("cIDb", [128, 128], BF16, kind="ExternalInput")
    conesb = nc.dram_tensor("conesb", [128, 1], BF16, kind="ExternalInput")

    out_h = nc.dram_tensor("out", [TO, E], F32, kind="ExternalOutput")
    ik = "ExternalOutput" if dbg else "Internal"
    x1h = nc.dram_tensor("x1h", [TO, E], F32, kind=ik)
    u1h = nc.dram_tensor("u1h", [16, 128, TO], BF16, kind="Internal")
    u2h = nc.dram_tensor("u2h", [128, 64, TO], BF16, kind=ik)
    wdqh = nc.dram_tensor("wdqh", [16, 64, 128, 128], BF16, kind="Internal")
    scrh = nc.dram_tensor("scrh", [8192], F32, kind="Internal")
    xq1d = nc.dram_tensor("xq1d", [128, 16 * TO], BF16, kind=ik)
    dbgc = nc.dram_tensor("dbgc", [128, 8], F32, kind=ik)
    dbgl = nc.dram_tensor("dbgl", [4, 16 * 512], F32, kind=ik)
    dbga = nc.dram_tensor("dbga", [128, 3 * 512], F32, kind=ik)
    dbgs = nc.dram_tensor("dbgs", [128, 2 * 512], BF16, kind=ik)
    dbgq = nc.dram_tensor("dbgq", [128, 4 * 512], BF16, kind=ik)
    dbgq2 = nc.dram_tensor("dbgq2", [128, 2 * 512], BF16, kind=ik)
    dbgw = nc.dram_tensor("dbgw", [128, 2 * 512], BF16, kind=ik)
    dbgg = nc.dram_tensor("dbgg", [128, 2 * 512], F32, kind=ik)
    dbgall = nc.dram_tensor("dbgall", [128, H * TO], BF16, kind=ik)

    import contextlib
    stck = contextlib.ExitStack()
    sb = lambda n, s, d: stck.enter_context(nc.sbuf_tensor(n, s, d))

    H128s = sb("H128s", [128, 128], BF16)
    H16ps = sb("H16ps", [128, 128], BF16)
    H64ps = sb("H64ps", [128, 128], BF16)
    IDfs = sb("IDfs", [128, 128], F32)
    IDbs = sb("IDbs", [128, 128], BF16)
    ones = sb("ones", [128, 1], BF16)
    bqs = sb("bqs", [128, H], F32)
    bks = sb("bks", [128, G], F32)
    bvs = sb("bvs", [128, G], F32)
    wcol = sb("wcol", [128, 6], F32)
    negB = sb("negB", [128, 1], F32)
    magicc = sb("magicc", [128, 1], F32)

    sems = {n: Ctr(nc, n) for n in
            ["ld", "wl", "sa", "sv", "sn", "tp", "tv", "mm", "ev", "qw",
             "at", "ae", "ax", "av", "ep", "er", "es", "ow", "d1", "d2",
             "d3", "d4", "f1", "f2", "f3", "f4", "f5", "f6", "g1", "g2",
             "w0", "w1", "w2", "w3", "x0", "x1", "y0", "y1", "o0", "o1",
             "u0", "u1"]}
    p = Prog(nc)
    ld = sems["ld"]

    for dst, src in ((H128s, cH128), (H16ps, cH16p), (H64ps, cH64p),
                     (IDfs, cIDf), (IDbs, cIDb), (ones, conesb),
                     (bqs, bqh), (bks, bkh), (bvs, bvh)):
        p.o("sync", lambda e, d=dst, s=src: e.dma_start(d[:, :], s[:, :]),
            incs=[(ld, 16)])
    p.o("vector", lambda e: e.memset(magicc[:, :], MAGIC))
    p.o("sync", lambda e: None, waits=[(ld, None)])
    p.flush("p0")

    with (
        nc.sbuf_tensor("p1buf", [128, 2, FI], F32) as p1buf,
        nc.sbuf_tensor("p1dmp", [128, FI], F32) as p1dmp,
        nc.sbuf_tensor("p1col", [128, 48], F32) as p1col,
        nc.sbuf_tensor("p1red", [128, 4], F32) as p1red,
        nc.sbuf_tensor("p1row", [128, 128], F32) as p1row,
        nc.sbuf_tensor("p1sc", [128, 8], F32) as p1sc,
        nc.psum_tensor("p1ps", [128, 128], F32) as p1ps,
    ):
        sA, sB = sems["sa"], sems["sv"]
        slabs = [(w, si, wi * 16 + si) for wi, w in enumerate((wgh, wuh, wdh))
                 for si in range(16)]
        for i, (wt, si, ci) in enumerate(slabs):
            r = AP(wt, si * 128 * FI, [[FI, 128], [1, FI]])
            xs = sems["x%d" % (i % 2)]
            w = [(sA, sA.v - 1)] if i >= 2 else []
            p.o("sync", lambda e, r=r, i=i: e.dma_start(
                p1buf[:, i % 2, :], r), waits=w, incs=[(xs, 16)])
            p.o("scalar", lambda e, i=i, ci=ci: e.activation(
                p1dmp[:, :], p1buf[:, i % 2, :], AF.Abs,
                accum_out=p1col[:, ci:ci + 1]), waits=[(xs, xs.v)], incs=[(sA, 1)])
        for wi in range(3):
            p.o("vector", lambda e, wi=wi: e.tensor_reduce(
                p1red[:, wi:wi + 1], p1col[:, wi * 16:(wi + 1) * 16], AX.X, ALU.add),
                waits=[(sA, None)], incs=[(sB, 1)])
        p.o("tensor", lambda e: e.transpose(p1ps[0:4, :], p1red[:, :], IDfs[:, :]),
            waits=[(sB, None)], incs=[(sB, 1)])
        def p1fin(e):
            e.tensor_copy(p1row[0:4, :], p1ps[0:4, :])
            e.drain()
            e.tensor_reduce(p1sc[0:3, 0:1], p1row[0:3, :], AX.X, ALU.add)
            e.drain()
            e.tensor_scalar(p1sc[0:3, 1:2], p1sc[0:3, 0:1], 1.0 / (E * FI), EPS_Q,
                            ALU.mult, ALU.max)
            e.drain()
            return e.reciprocal(p1sc[0:3, 2:3], p1sc[0:3, 1:2])
        p.o("vector", p1fin, waits=[(sB, None)], incs=[(sB, 1)])
        d1 = sems["d1"]
        p.o("sync", lambda e: e.dma_start(scrh[0:6], p1sc[0:3, 1:3]),
            waits=[(sB, None)], incs=[(d1, 16)])
        for j in range(6):
            p.o("sync", lambda e, j=j: e.dma_start(
                wcol[:, j:j + 1], AP(scrh, j, [[0, 128], [1, 1]])),
                waits=[(d1, 16)], incs=[(d1, 16)])
        if dbg:
            p.o("sync", lambda e: e.dma_start(dbgc[:, 0:6], wcol[:, :]),
                waits=[(d1, None)], incs=[(d1, 16)])
        p.o("sync", lambda e: None, waits=[(d1, None)])
        p.flush("p1")
    WSG, WIG, WSU, WIU, WSD, WID = range(6)

    if DEV_STAGE < 1:
        stck.close()
        return nc

    def emit_ln(xb_ap, xn_ap, stc, dump_ap, wait_ld, width=E):
        sA, sV, sN = sems["sa"], sems["sv"], sems["sn"]
        p.o("scalar", lambda e: e.activation(dump_ap, xb_ap, AF.Square,
                                             accum_out=stc[:, 1:2]), waits=wait_ld)
        p.o("scalar", lambda e: e.activation(dump_ap, xb_ap, AF.Copy,
                                             accum_out=stc[:, 0:1]), incs=[(sA, 1)])
        def stats(e):
            e.tensor_scalar(stc[:, 2:3], stc[:, 0:1], 1.0 / width, None, ALU.mult)
            e.tensor_scalar(stc[:, 3:4], stc[:, 1:2], 1.0 / width, None, ALU.mult)
            e.drain()
            e.tensor_tensor(stc[:, 4:5], stc[:, 2:3], stc[:, 2:3], ALU.mult)
            e.drain()
            e.tensor_tensor(stc[:, 5:6], stc[:, 3:4], stc[:, 4:5], ALU.subtract)
            e.drain()
            return e.tensor_scalar(stc[:, 5:6], stc[:, 5:6], EPS_LN, None, ALU.add)
        p.o("vector", stats, waits=[(sA, None)], incs=[(sV, 1)])
        p.o("scalar", lambda e: e.activation(stc[:, 6:7], stc[:, 5:6], AF.Sqrt),
            waits=[(sV, None)], incs=[(sA, 1)])
        def fin(e):
            e.reciprocal(stc[:, 7:8], stc[:, 6:7])
            e.drain()
            e.tensor_tensor(stc[:, 6:7], stc[:, 2:3], stc[:, 7:8], ALU.mult)
            e.drain()
            return e.tensor_scalar(stc[:, 6:7], stc[:, 6:7], -1.0, None, ALU.mult)
        p.o("vector", fin, waits=[(sA, None)], incs=[(sV, 1)])
        p.o("scalar", lambda e: e.activation(xn_ap, xb_ap, AF.Identity,
                                             bias=stc[:, 6:7], scale=stc[:, 7:8]),
            waits=[(sV, None)], incs=[(sN, 1)])

    astck = contextlib.ExitStack()
    asb = lambda n, s, d: astck.enter_context(nc.sbuf_tensor(n, s, d))
    qall = asb("qall", [128, H * TO], BF16)
    kall = asb("kall", [128, G * S], BF16)
    vTall = asb("vTall", [128, G * S], BF16)

    def emit_proj(name, src, n_tok, q_path):
        sN, sTp, sTv, sMM, sEv = (sems["sn"], sems["tp"], sems["tv"],
                                  sems["mm"], sems["ev"])
        sWl, sVt = sems["wl"], sems["qw"]
        ngrp = n_tok // 512
        nw = H if q_path else G
        with (
            nc.sbuf_tensor(name + "xb", [128, 2, E], F32) as xb,
            nc.sbuf_tensor(name + "dm", [128, E], F32) as dm,
            nc.sbuf_tensor(name + "st", [128, 8, 8], F32) as stt,
            nc.sbuf_tensor(name + "xn", [128, 2, E], BF16) as xn,
            nc.sbuf_tensor(name + "xf", [128, 16 * 512], BF16) as xnf,
            nc.sbuf_tensor(name + "wb", [128, 2, E], BF16) as wb,
            nc.sbuf_tensor(name + "vt", [128, 2, 512], BF16) as vtb,
            nc.psum_tensor(name + "tba", [128, 128], BF16) as ptba,
            nc.psum_tensor(name + "tbb", [128, 128], BF16) as ptbb,
            nc.psum_tensor(name + "t2a", [128, 128], BF16) as ptb2a,
            nc.psum_tensor(name + "t2b", [128, 128], BF16) as ptb2b,
            nc.psum_tensor(name + "pm", [128, 512], F32) as pm0,
            nc.psum_tensor(name + "pn", [128, 512], F32) as pm1,
            nc.psum_tensor(name + "pv", [128, 512], F32) as pv0,
            nc.psum_tensor(name + "pw", [128, 512], F32) as pv1,
        ):
            pms = (pm0, pm1)
            pvs = (pv0, pv1)
            ptbs = (ptba, ptbb)
            ptb2s = (ptb2a, ptb2b)
            nmark, tvmark, evmark, vtmark, wmark = {}, {}, {}, {}, {}
            tp0, tv0, vt0 = sTp.v, sTv.v, sVt.v
            xsrc = src[:, :].rearrange("(a p) f -> a p f", p=128)
            for g in range(ngrp):
                for s4 in range(4):
                    ti = g * 4 + s4
                    sl = ti % 2
                    xs = sems["x%d" % sl]
                    w = [(sN, nmark[ti - 2])] if ti - 2 in nmark else []
                    p.o("sync", lambda e, ti=ti, sl=sl: e.dma_start(
                        xb[:, sl, :], xsrc[ti, :, :]), waits=w, incs=[(xs, 16)])
                    emit_ln(xb[:, sl, :], xn[:, sl, :], stt[:, ti % 8, :],
                            dm[:, :], [(xs, xs.v)])
                    nmark[ti] = sems["sn"].v
                    for ec in range(16):
                        i = ti * 16 + ec
                        w = [(sems["sn"], nmark[ti])]
                        if i >= 2:
                            w.append((sTv, tv0 + i - 1))
                        p.o("tensor", lambda e, sl=sl, ec=ec, i=i: e.transpose(
                            ptbs[i % 2][:, :],
                            xn[:, sl, ec * 128:ec * 128 + 128], IDbs[:, :]),
                            waits=w, incs=[(sTp, 1)])
                        p.o("vector", lambda e, ec=ec, s4=s4, i=i: e.tensor_copy(
                            xnf[:, ec * 512 + s4 * 128: ec * 512 + s4 * 128 + 128],
                            ptbs[i % 2][:, :]),
                            waits=[(sTp, None)], incs=[(sTv, 1)])
                for hh in range(nw):
                    sl = hh % 2
                    pk = (g, hh - 2) if hh >= 2 else (g - 1, nw - 2 + hh)
                    ws = sems["w%d" % sl]
                    w = [(sMM, wmark[pk])] if pk in wmark else []
                    WH = Wqh if q_path else Wkh
                    swp = WH[:, :, :].rearrange("h (a p) d -> h a p d", p=128)[hh]
                    p.o("gpsimd", lambda e, sl=sl, sw=swp: e.dma_start(
                        wb[:, sl, :].rearrange("p (a d) -> p a d", a=16),
                        sw.transpose([1, 0, 2])), waits=w, incs=[(ws, 16)])
                    def mk_mms(ps_pair, slot, buf):
                        def mms(e):
                            ins = None
                            for ec in range(16):
                                ins = e.matmul(ps_pair[slot][:, :],
                                               buf[:, slot, ec * 128:ec * 128 + 128],
                                               xnf[:, ec * 512:ec * 512 + 512],
                                               start=(ec == 0), stop=(ec == 15))
                            return ins
                        return mms
                    prevkey = (g, hh - 2) if hh >= 2 else (g - 1, nw - 2 + hh)
                    w = [(ws, ws.v), (sTv, tv0 + (g + 1) * 64)]
                    if prevkey in evmark:
                        w.append((sEv, evmark[prevkey]))
                    p.o("tensor", mk_mms(pms, sl, wb), waits=w, incs=[(sMM, 1)])
                    kq_mm = sMM.v
                    if q_path:
                        wmark[(g, hh)] = kq_mm
                        p.o("scalar", lambda e, g=g, hh=hh, sl=sl: e.activation(
                            qall[:, hh * TO + g * 512: hh * TO + g * 512 + 512],
                            pms[sl][:, :], AF.Identity, bias=bqs[:, hh:hh + 1]),
                            waits=[(sMM, kq_mm)], incs=[(sEv, 1)])
                        evmark[(g, hh)] = sEv.v
                    else:
                        p.o("scalar", lambda e, g=g, hh=hh, sl=sl: e.activation(
                            kall[:, hh * S + g * 512: hh * S + g * 512 + 512],
                            pms[sl][:, :], AF.Identity, bias=bks[:, hh:hh + 1]),
                            waits=[(sMM, kq_mm)], incs=[(sEv, 1)])
                        svp = Wvh[:, :, :].rearrange("h (a p) d -> h a p d", p=128)[hh]
                        p.o("gpsimd", lambda e, sl=sl, sw=svp: e.dma_start(
                            wb[:, sl, :].rearrange("p (a d) -> p a d", a=16),
                            sw.transpose([1, 0, 2])),
                            waits=[(sMM, kq_mm)], incs=[(ws, 16)])
                        wv = [(ws, ws.v)]
                        if prevkey in evmark:
                            wv.append((sEv, evmark[prevkey]))
                        p.o("tensor", mk_mms(pvs, sl, wb), waits=wv, incs=[(sMM, 1)])
                        v_mm = sMM.v
                        wmark[(g, hh)] = v_mm
                        p.o("scalar", lambda e, hh=hh, sl=sl: e.activation(
                            vtb[:, sl, :], pvs[sl][:, :], AF.Identity,
                            bias=bvs[:, hh:hh + 1]),
                            waits=[(sMM, v_mm)], incs=[(sEv, 1)])
                        ev_v = sEv.v
                        evmark[(g, hh)] = ev_v
                        for s4 in range(4):
                            iv = (g * nw + hh) * 4 + s4
                            wv2 = [(sEv, ev_v)]
                            if iv >= 2:
                                wv2.append((sVt, vt0 + iv - 1))
                            p.o("tensor", lambda e, sl=sl, s4=s4, iv=iv: e.transpose(
                                ptb2s[iv % 2][:, :],
                                vtb[:, sl, s4 * 128:s4 * 128 + 128], IDbs[:, :]),
                                waits=wv2, incs=[(sTp, 1)])
                            p.o("vector", lambda e, hh=hh, g=g, s4=s4, iv=iv: e.tensor_copy(
                                vTall[:, hh * S + (g * 4 + s4) * 128:
                                      hh * S + (g * 4 + s4) * 128 + 128],
                                ptb2s[iv % 2][:, :]),
                                waits=[(sTp, None)], incs=[(sVt, 1)])
            p.o("sync", lambda e: None,
                waits=[(sEv, None), (sVt, None), (sTv, None)])
            p.flush(name)

    emit_proj("pq", xo, TO, True)
    if DEV_STAGE < 99:
        p.o("sync", lambda e: e.dma_start(dbgq2[:, 0:512],
                                          qall[:, 14 * TO:14 * TO + 512]),
            incs=[(sems["ow"], 16)])
        p.o("sync", lambda e: e.dma_start(dbgq2[:, 512:1024],
                                          qall[:, 13 * TO:13 * TO + 512]),
            incs=[(sems["ow"], 16)])
        p.o("sync", lambda e: None, waits=[(sems["ow"], None)])
        p.flush("dbgq2")
    if DEV_STAGE < 2:
        astck.close()
        stck.close()
        return nc
    emit_proj("pkv", xf, S, False)

    if os.environ.get("MK_STOP_PKV", "0") == "1":
        astck.close()
        stck.close()
        return nc
    with (
        nc.sbuf_tensor("pbsq", [128, 2, S], BF16) as sqb,
        nc.sbuf_tensor("pbn", [1, 64], F32) as pbn,
        nc.psum_tensor("pbps", [128, 512], F32) as bps,
    ):
        sA, sMM, sV = sems["sa"], sems["mm"], sems["sv"]
        NOPB = os.environ.get("MK_NOPB", "0") == "1"
        p.o("vector", lambda e: e.memset(pbn[:, :], 0.0), incs=[(sV, 1)])
        if NOPB:
            p.o("vector", lambda e: e.memset(pbn[0:1, 26:27], -18.6))
        for which, (buf, wid, nh) in enumerate(
                () if NOPB else ((qall, TO, H), (kall, S, G))):
            for hh in range(nh):
                col = hh if which == 0 else 16 + hh
                w = [(sMM, None)] if (which, hh) > (0, 1) else []
                p.o("scalar", lambda e, hh=hh, buf=buf, wid=wid: e.activation(
                    sqb[:, hh % 2, 0:wid], buf[:, hh * wid:(hh + 1) * wid], AF.Square),
                    waits=w, incs=[(sA, 1)])
                for sub in range(wid // 512):
                    p.o("tensor", lambda e, hh=hh, sub=sub: e.matmul(
                        bps[0:1, :], ones[:, :], sqb[:, hh % 2, sub * 512:sub * 512 + 512],
                        start=True, stop=True), waits=[(sA, None), (sV, None)],
                        incs=[(sMM, 1)])
                    def bmax(e, col=col):
                        e.tensor_reduce(pbn[0:1, 20:21], bps[0:1, :], AX.X, ALU.max)
                        e.drain()
                        return e.tensor_tensor(pbn[0:1, col:col + 1],
                                               pbn[0:1, col:col + 1],
                                               pbn[0:1, 20:21], ALU.max)
                    p.o("vector", bmax, waits=[(sMM, None)], incs=[(sV, 1)])
        if not NOPB:
            def bfin1(e):
                e.tensor_reduce(pbn[0:1, 21:22], pbn[0:1, 0:16], AX.X, ALU.max)
                return e.tensor_reduce(pbn[0:1, 22:23], pbn[0:1, 16:20], AX.X, ALU.max)
            p.o("vector", bfin1, waits=[(sV, None)], incs=[(sV, 1)])
            p.o("scalar", lambda e: e.activation(pbn[0:1, 23:25], pbn[0:1, 21:23], AF.Sqrt),
                waits=[(sV, None)], incs=[(sA, 1)])
            def bfin2(e):
                e.tensor_tensor(pbn[0:1, 25:26], pbn[0:1, 23:24], pbn[0:1, 24:25], ALU.mult)
                e.drain()
                return e.tensor_scalar(pbn[0:1, 26:27], pbn[0:1, 25:26],
                                       -ISQ128 * 1.05, None, ALU.mult)
            p.o("vector", bfin2, waits=[(sA, None)], incs=[(sV, 1)])
        d1 = sems["d1"]
        p.o("sync", lambda e: e.dma_start(scrh[8:9], pbn[0:1, 26:27]),
            waits=[(sV, None)], incs=[(d1, 16)])
        p.o("sync", lambda e: e.dma_start(negB[:, :], AP(scrh, 8, [[0, 128], [1, 1]])),
            waits=[(d1, None)], incs=[(d1, 16)])
        p.o("sync", lambda e: None, waits=[(d1, None)])
        p.flush("pb")

    if os.environ.get("MK_STOP_PB", "0") == "1":
        astck.close()
        stck.close()
        return nc
    NCH = 32
    with (
        nc.sbuf_tensor("masks", [128, MASK_W], BF16) as masks,
        nc.sbuf_tensor("paE", [128, 4, 512], BF16) as Eb,
        nc.sbuf_tensor("paao", [128, 8 * 512], F32) as ao,
        nc.sbuf_tensor("palr", [1, 16 * 512], F32) as lrows,
        nc.sbuf_tensor("palc", [128, 64], F32) as lcols,
        nc.sbuf_tensor("paxo", [128, 2, 1024], F32) as xot,
        nc.sbuf_tensor("pax1", [128, 2, 128], F32) as x1b,
        nc.psum_tensor("paS0", [128, 512], F32) as ps0,
        nc.psum_tensor("paS1", [128, 512], F32) as ps1,
        nc.psum_tensor("paS2", [128, 512], F32) as ps2,
        nc.psum_tensor("paS3", [128, 512], F32) as ps3,
        nc.psum_tensor("paA0", [128, 512], F32) as pa0,
        nc.psum_tensor("paA1", [128, 512], F32) as pa1,
        nc.psum_tensor("paL0", [128, 512], F32) as pl0,
        nc.psum_tensor("paT0", [128, 128], F32) as pt0,
    ):
        pss, pas = (ps0, ps1, ps2, ps3), (pa0, pa1)
        sAt, sAe, sAx, sAv = sems["at"], sems["ae"], sems["ax"], sems["av"]
        sEp, sEr, sEs, sOw = sems["ep"], sems["er"], sems["es"], sems["ow"]
        sTp = sems["tp"]
        d2, d3 = sems["d2"], sems["d3"]
        p.o("sync", lambda e: e.dma_start(masks[:, :], maskh[:, :]), incs=[(ld, 16)])
        if dbg:
            p.o("sync", lambda e: e.dma_start(
                dbgq[:, 1024:1536], kall[:, 3 * S:3 * S + 512]), incs=[(sOw, 16)])
            p.o("sync", lambda e: e.dma_start(
                dbgq[:, 1536:2048], vTall[:, 3 * S:3 * S + 512]), incs=[(sOw, 16)])
        p.o("vector", lambda e: None, waits=[(ld, None)])
        p.o("sync", lambda e: None, waits=[(ld, None)])
        expmark, avmark, epmark, satm = {}, {}, {}, {}
        eptq_end = {}
        xosrc = xo[:, :].rearrange("(a p) f -> a p f", p=128)
        ci = 0
        ei = 0
        xomark = {}
        for tq in range(4):
            for hf in range(2):
                for h8 in range(8):
                    h = hf * 8 + h8
                    g = h // 4
                    def emit_smm(c, i, g=g, tq=tq, h=h):
                        sl = i % 4
                        w = [(sAe, expmark[i - 4])] if i >= 4 else []
                        if os.environ.get("MK_NOSMM", "0") == "1":
                            p.o("tensor", lambda e: None, waits=w, incs=[(sAt, 1)])
                        else:
                            p.o("tensor", lambda e, g=g, c=c, tq=tq, h=h, sl=sl: e.matmul(
                                pss[sl][:, :],
                                kall[:, g * S + c * 128: g * S + c * 128 + 128],
                                qall[:, h * TO + tq * 512: h * TO + tq * 512 + 512],
                                start=True, stop=True), waits=w, incs=[(sAt, 1)])
                        satm[i] = sAt.v
                    emit_smm(0, ci)
                    emit_smm(1, ci + 1)
                    emit_smm(2, ci + 2)
                    for c in range(NCH):
                        i = ci
                        ci += 1
                        sl = i % 4
                        el = i % 4
                        w = [(sAt, satm[i])]
                        if i >= 2:
                            w.append((sAv, avmark[i - 2]))
                        p.o("scalar", lambda e, sl=sl, el=el: e.activation(
                            Eb[:, el, :], pss[sl][:, :], AF.Exp,
                            bias=negB[:, 0:1], scale=ISQ128), waits=w, incs=[(sAe, 1)])
                        expmark[i] = sAe.v
                        off = tq * 512 - c * 128 + MASK_Z
                        p.o("vector", lambda e, el=el, off=off: e.tensor_tensor(
                            Eb[:, el, :], Eb[:, el, :], masks[:, off:off + 512],
                            ALU.mult), waits=[(sAe, sAe.v)], incs=[(sAx, 1)])
                        if c + 3 < NCH:
                            emit_smm(c + 3, i + 3)
                        def avl(e, h=h, g=g, c=c, el=el):
                            e.matmul(pas[h % 2][:, :],
                                     vTall[:, g * S + c * 128: g * S + c * 128 + 128],
                                     Eb[:, el, :], start=(c == 0), stop=(c == NCH - 1))
                            return e.matmul(pl0[0:1, :], ones[:, :],
                                            Eb[:, el, :], start=(c == 0),
                                            stop=(c == NCH - 1))
                        w = [(sAx, sAx.v)]
                        if c == 0:
                            prev = (tq, h - 1) if h >= 1 else (tq - 1, 15)
                            if prev in epmark:
                                w.append((sEp, epmark[prev]))
                        p.o("tensor", avl, waits=w, incs=[(sAv, 1)])
                        avmark[i] = sAv.v
                    def evh(e, h=h, h8=h8):
                        e.tensor_copy(ao[:, h8 * 512:(h8 + 1) * 512], pas[h % 2][:, :])
                        return e.tensor_copy(lrows[0:1, h * 512:(h + 1) * 512],
                                             pl0[0:1, :])
                    if NOEVH:
                        p.o("vector", lambda e: None, waits=[(sAv, None)], incs=[(sEp, 1)])
                    else:
                        p.o("vector", evh, waits=[(sAv, None)], incs=[(sEp, 1)])
                    epmark[(tq, h)] = sEp.v
                if not NOBOUNCE:
                    p.o("sync", lambda e, hf=hf: e.dma_start(
                        scrh[hf * 4096: hf * 4096 + 4096],
                        lrows[0:1, hf * 4096:(hf + 1) * 4096]),
                        waits=[(sEp, None)], incs=[(d2, 16)])
                    p.o("sync", lambda e, hf=hf: e.dma_start(
                        lcols[:, hf * 32:(hf + 1) * 32],
                        AP(scrh, hf * 4096, [[1, 128], [512, 8], [128, 4]]),
                        allow_slow_non_contiguous=True),
                        waits=[(d2, d2.v), (sEs, sEs.v)], incs=[(d2, 16)])
                p.o("vector", lambda e, hf=hf: e.reciprocal(
                    lcols[:, hf * 32:(hf + 1) * 32], lcols[:, hf * 32:(hf + 1) * 32]),
                    waits=[(d2, None)], incs=[(sEr, 1)])
                er_v = sEr.v
                for s4 in range(0 if NOEPI else 4):
                    si = (tq * 2 + hf) * 4 + s4
                    ys = sems["y%d" % (si % 2)]
                    w = [(sEs, xomark[si - 2])] if si - 2 in xomark else []
                    p.o("sync", lambda e, tq=tq, s4=s4, si=si, hf=hf: e.dma_start(
                        xot[:, si % 2, :],
                        xosrc[tq * 4 + s4, :, hf * 1024:(hf + 1) * 1024]),
                        waits=w, incs=[(ys, 16)])
                    d3_v = (ys, ys.v)
                    for h8 in range(8):
                        h = hf * 8 + h8
                        i = ei
                        ei += 1
                        w = [(sEp, epmark[(tq, h)])]
                        if i >= 1:
                            w.append((sEs, sEs.v))
                        p.o("tensor", lambda e, h8=h8, s4=s4, i=i: e.transpose(
                            pt0[:, :],
                            ao[:, h8 * 512 + s4 * 128: h8 * 512 + s4 * 128 + 128],
                            IDfs[:, :]), waits=w, incs=[(sTp, 1)])
                        osm = sems["o%d" % (i % 2)]
                        wv = [(sTp, sTp.v), (sEr, er_v), d3_v]
                        if i >= 2:
                            wv.append((osm, osm.v))
                        def stt(e, h=h, s4=s4, i=i, hf=hf, h8=h8, si=si):
                            return e.scalar_tensor_tensor(
                                x1b[:, i % 2, :], pt0[:, :],
                                lcols[:, hf * 32 + h8 * 4 + s4:
                                      hf * 32 + h8 * 4 + s4 + 1],
                                xot[:, si % 2, h8 * 128:(h8 + 1) * 128],
                                ALU.mult, ALU.add)
                        p.o("vector", stt, waits=wv, incs=[(sEs, 1)])
                        p.o("sync", lambda e, tq=tq, s4=s4, h=h, i=i: e.dma_start(
                            x1h[tq * 512 + s4 * 128: tq * 512 + s4 * 128 + 128,
                                h * 128:(h + 1) * 128], x1b[:, i % 2, :]),
                            waits=[(sEs, sEs.v)], incs=[(osm, 16)])
                    xomark[si] = sEs.v
            if dbg:
                p.o("sync", lambda e, tq=tq: e.dma_start(
                    dbgl[tq:tq + 1, :], lrows[0:1, :]),
                    waits=[(sEp, None)], incs=[(sOw, 16)])
                if tq == 0:
                    p.o("sync", lambda e: e.dma_start(
                        dbga[:, :], ao[:, 5 * 512:8 * 512]),
                        waits=[(sEp, None)], incs=[(sOw, 16)])
            eptq_end[tq] = sEs.v
        p.o("sync", lambda e: None,
            waits=[(sOw, None), (sems["o0"], None), (sems["o1"], None)])
        p.flush("pa")
        if dbg:
            p.o("sync", lambda e: e.dma_start(dbgall[:, :], qall[:, :]),
                incs=[(sOw, 16)])
            p.o("sync", lambda e: None, waits=[(sOw, None)])
            p.flush("dbgpa")
    astck.close()

    if DEV_STAGE < 3:
        stck.close()
        return nc

    with (
        nc.sbuf_tensor("f1xb", [128, 2, E], F32) as xb,
        nc.sbuf_tensor("f1dm", [128, E], F32) as dm,
        nc.sbuf_tensor("f1st", [128, 8, 8], F32) as stt1,
        nc.sbuf_tensor("f1xn", [128, 2, E], BF16) as xn,
        nc.sbuf_tensor("f1xf", [128, 16 * 512], BF16) as xnf,
        nc.sbuf_tensor("f1ub", [128, 2, 512], BF16) as ub,
        nc.psum_tensor("f1tba", [128, 128], BF16) as ptba,
        nc.psum_tensor("f1tbb", [128, 128], BF16) as ptbb,
        nc.psum_tensor("f1p0", [128, 512], F32) as pu0,
        nc.psum_tensor("f1p1", [128, 512], F32) as pu1,
    ):
        pus = (pu0, pu1)
        ptbs = (ptba, ptbb)
        sN, sTp, sTv, sMM, sEv = (sems["sn"], sems["tp"], sems["tv"],
                                  sems["mm"], sems["ev"])
        d4 = sems["d4"]
        x1src = x1h[:, :].rearrange("(a p) f -> a p f", p=128)
        nmark, evm, dmam = {}, {}, {}
        tv0 = sTv.v
        for g in range(4):
            for s4 in range(4):
                ti = g * 4 + s4
                sl = ti % 2
                xs = sems["x%d" % sl]
                w = [(sN, nmark[ti - 2])] if ti - 2 in nmark else []
                p.o("sync", lambda e, ti=ti, sl=sl: e.dma_start(
                    xb[:, sl, :], x1src[ti, :, :]), waits=w, incs=[(xs, 16)])
                emit_ln(xb[:, sl, :], xn[:, sl, :], stt1[:, ti % 8, :], dm[:, :],
                        [(xs, xs.v)])
                nmark[ti] = sN.v
                for ec in range(16):
                    i = ti * 16 + ec
                    w = [(sN, nmark[ti])]
                    if i >= 2:
                        w.append((sTv, tv0 + i - 1))
                    p.o("tensor", lambda e, sl=sl, ec=ec, i=i: e.transpose(
                        ptbs[i % 2][:, :],
                        xn[:, sl, ec * 128:ec * 128 + 128], IDbs[:, :]),
                        waits=w, incs=[(sTp, 1)])
                    p.o("vector", lambda e, ec=ec, s4=s4, i=i: e.tensor_copy(
                        xnf[:, ec * 512 + s4 * 128: ec * 512 + s4 * 128 + 128],
                        ptbs[i % 2][:, :]),
                        waits=[(sTp, None)], incs=[(sTv, 1)])
            for tau in range(16):
                it = g * 16 + tau
                w = [(sTv, tv0 + (g + 1) * 64)]
                if it >= 2:
                    w.append((sEv, evm[it - 2]))
                p.o("tensor", lambda e, tau=tau, it=it: e.matmul(
                    pus[it % 2][:, :], H128s[:, :], xnf[:, tau * 512:tau * 512 + 512],
                    start=True, stop=True), waits=w, incs=[(sMM, 1)])
                us = sems["u%d" % (it % 2)]
                wv = [(sMM, sMM.v)]
                if it >= 2:
                    wv.append((us, us.v))
                p.o("scalar", lambda e, it=it: e.activation(
                    ub[:, it % 2, :], pus[it % 2][:, :], AF.Copy),
                    waits=wv, incs=[(sEv, 1)])
                evm[it] = sEv.v
                p.o("sync", lambda e, tau=tau, g=g, it=it: e.dma_start(
                    u1h[tau, :, g * 512:(g + 1) * 512], ub[:, it % 2, :]),
                    waits=[(sEv, sEv.v)], incs=[(us, 16)])
        p.o("sync", lambda e: None,
            waits=[(sems["u0"], None), (sems["u1"], None)])
        p.flush("pf1")

    fstck = contextlib.ExitStack()
    xq1 = fstck.enter_context(nc.sbuf_tensor("xq1", [128, 16 * TO], BF16))
    with (
        nc.sbuf_tensor("f2rb", [128, 16 * 512], BF16) as rb,
        nc.sbuf_tensor("f2z", [128, 16 * 512], F32) as z1,
        nc.sbuf_tensor("f2mx", [128, 512], F32) as mxj,
        nc.sbuf_tensor("f2mc", [128, 16], F32) as mc,
        nc.sbuf_tensor("f2qr", [128, 1024], F32) as qr,
        nc.sbuf_tensor("f2tq", [128, 512], F32) as tqb,
        nc.psum_tensor("f2z0", [128, 512], F32) as pz0,
        nc.psum_tensor("f2z1", [128, 512], F32) as pz1,
        nc.psum_tensor("f2t0", [128, 128], F32) as pt0,
        nc.psum_tensor("f2t1", [128, 128], F32) as pt1,
    ):
        pzs, pts = (pz0, pz1), (pt0, pt1)
        sMM, sEv, sV, sQ = sems["mm"], sems["ev"], sems["sv"], sems["f1"]
        sTp = sems["tp"]
        d4 = sems["d4"]
        for t5 in range(4):
            w = [(sMM, sMM.v)] if t5 > 0 else []
            for cc in range(8):
                p.o("sync", lambda e, t5=t5, cc=cc: e.dma_start(
                    rb[cc * 16:(cc + 1) * 16, :].rearrange("p (j t) -> p j t", j=16),
                    AP(u1h, t5 * 512 + cc * 16 * TO,
                       [[128 * TO, 16], [TO, 16], [1, 512]])),
                    waits=w, incs=[(ld, 16)])
            for jr in range(16):
                w = [(ld, None)]
                if t5 * 16 + jr >= 2:
                    w.append((sEv, sEv.v - 1))
                p.o("tensor", lambda e, jr=jr: e.matmul(
                    pzs[jr % 2][:, :], H16ps[:, :], rb[:, jr * 512:jr * 512 + 512],
                    start=True, stop=True), waits=w, incs=[(sMM, 1)])
                p.o("vector", lambda e, jr=jr: e.tensor_copy(
                    z1[:, jr * 512:jr * 512 + 512], pzs[jr % 2][:, :]),
                    waits=[(sMM, sMM.v)], incs=[(sEv, 1)])
            p.o("vector", lambda e: e.tensor_reduce(
                mxj[:, :], z1[:, :].rearrange("p (j t) -> p t j", j=16),
                AX.X, ALU.max, apply_absolute_value=True),
                waits=[(sEv, None)], incs=[(sV, 1)])
            for s4 in range(4):
                w = [(sV, sV.v)]
                p.o("tensor", lambda e, s4=s4: e.transpose(
                    pts[s4 % 2][:, :], mxj[:, s4 * 128:s4 * 128 + 128], IDfs[:, :]),
                    waits=w, incs=[(sTp, 1)])
                p.o("vector", lambda e, s4=s4: e.tensor_reduce(
                    mc[:, s4:s4 + 1], pts[s4 % 2][:, :], AX.X, ALU.max),
                    waits=[(sTp, None)], incs=[(sV, 1)])
            def smath(e):
                e.tensor_scalar(mc[:, 4:8], mc[:, 0:4], 1.0 / SQ2048, EPS_Q,
                                ALU.mult, ALU.max)
                e.drain()
                e.reciprocal(mc[:, 8:12], mc[:, 4:8])
                e.drain()
                e.tensor_scalar(mc[:, 8:12], mc[:, 8:12], 127.0 / SQ2048, None,
                                ALU.mult)
                return e.tensor_scalar(mc[:, 12:16], mc[:, 4:8], 1.0 / 127.0, None,
                                       ALU.mult)
            p.o("vector", smath, waits=[(sV, None)], incs=[(sV, 1)])
            p.o("sync", lambda e: e.dma_start(
                AP(scrh, 0, [[1, 128], [128, 8]]), mc[:, 8:16],
                allow_slow_non_contiguous=True),
                waits=[(sV, None), (d4, None)], incs=[(d4, 16)])
            p.o("sync", lambda e: e.dma_start(
                qr[:, :], AP(scrh, 0, [[0, 128], [1, 1024]])),
                waits=[(d4, d4.v), (sQ, sQ.v)], incs=[(d4, 16)])
            for jr in range(16):
                def quant(e, jr=jr, t5=t5):
                    e.tensor_tensor(tqb[:, :], z1[:, jr * 512:jr * 512 + 512],
                                    qr[:, 0:512], ALU.mult)
                    e.drain()
                    e.tensor_scalar(tqb[:, :], tqb[:, :], MAGIC, MAGIC,
                                    ALU.add, ALU.subtract)
                    e.drain()
                    return e.tensor_tensor(
                        xq1[:, jr * TO + t5 * 512: jr * TO + t5 * 512 + 512],
                        tqb[:, :], qr[:, 512:1024], ALU.mult)
                p.o("vector", quant, waits=[(d4, None)], incs=[(sQ, 1)])
        if dbg:
            p.o("sync", lambda e: e.dma_start(xq1d[:, :], xq1[:, :]),
                waits=[(sQ, None)], incs=[(d4, 16)])
        p.o("sync", lambda e: None, waits=[(d4, None), (sQ, None)])
        p.flush("pf2")

    if DEV_STAGE < 4:
        fstck.close()
        stck.close()
        return nc

    with (
        nc.sbuf_tensor("f3wf", [128, 4, 512], F32) as wf,
        nc.sbuf_tensor("f3w1", [128, 4, 512], F32) as wt1,
        nc.sbuf_tensor("f3w2", [128, 4, 512], F32) as wt2,
        nc.sbuf_tensor("f3gq", [128, 16, 512], BF16) as wgq,
        nc.sbuf_tensor("f3uq", [128, 16, 512], BF16) as wuq,
        nc.sbuf_tensor("f3sg", [128, 4, 512], F32) as sg,
        nc.sbuf_tensor("f3hb", [128, 4, 512], BF16) as hb,
        nc.sbuf_tensor("f3ub", [128, 2, 512], BF16) as ub2,
        nc.sbuf_tensor("f3df", [128, 2, 2048], F32) as dwf,
        nc.sbuf_tensor("f3d1", [128, 2, 2048], F32) as dw1,
        nc.sbuf_tensor("f3d2", [128, 2, 2048], F32) as dw2,
        nc.sbuf_tensor("f3dq", [128, 2, 2048], BF16) as dwq,
        nc.sbuf_tensor("f3dbg", [128, 512], F32) as dnfdbg,
        nc.psum_tensor("f3g0", [128, 512], F32) as pg0,
        nc.psum_tensor("f3g1", [128, 512], F32) as pg1,
        nc.psum_tensor("f3u0", [128, 512], F32) as pu0,
        nc.psum_tensor("f3u1", [128, 512], F32) as pu1,
        nc.psum_tensor("f3U0", [128, 512], F32) as pU0,
        nc.psum_tensor("f3U1", [128, 512], F32) as pU1,
    ):
        pgs, pus2, pUs = (pg0, pg1), (pu0, pu1), (pU0, pU1)
        sWl, sP1, sP2, sP3 = sems["wl"], sems["f2"], sems["f3"], sems["f4"]
        sMM, sSg, sHb, sUe = sems["mm"], sems["sa"], sems["f5"], sems["ev"]
        d4, dW, dS = sems["d4"], sems["g1"], sems["g2"]
        p3m = {}
        t1m, t2m, t3m = {}, {}, {}
        stashm = {}
        mm_ob = {}

        def emit_tern(src_ap, inv_col, dst_ap, nb, slot, kind, key, nslots,
                      extra_p3_waits):
            fbuf, b1, b2 = (wf, wt1, wt2) if nb == 512 else (dwf, dw1, dw2)
            ldc = sems[("w%d" if nb == 512 else "x%d") % slot]
            kprev = (kind, key - nslots)
            w = [(sP1, t1m[kprev])] if kprev in t1m else []
            if os.environ.get("MK_TERNSER", "0") == "1":
                w = [(sP3, sP3.v)]
            p.o("sync" if nb == 512 else "gpsimd",
                lambda e: e.dma_start(fbuf[:, slot, 0:nb], src_ap),
                waits=w, incs=[(ldc, 16)])
            w = [(ldc, ldc.v)]
            if kprev in t2m:
                w.append((sP2, t2m[kprev]))
            p.o("vector", lambda e: e.tensor_scalar(
                b1[:, slot, 0:nb], fbuf[:, slot, 0:nb], inv_col, MAGIC,
                ALU.mult, ALU.add), waits=w, incs=[(sP1, 1)])
            t1m[(kind, key)] = sP1.v
            w = [(sP1, sP1.v)]
            if kprev in t3m:
                w.append((sP3, t3m[kprev]))
            def p2p3(e, slot=slot, nb=nb, dst_ap=dst_ap):
                e.drain()
                e.tensor_scalar(b2[:, slot, 0:nb], b1[:, slot, 0:nb], MAGIC, 1.0,
                                ALU.subtract, ALU.min).then_inc(sP2.h, 1)
                e.drain()
                return e.tensor_scalar(dst_ap, b2[:, slot, 0:nb], -1.0, None,
                                       ALU.max)
            sP2.v += 1
            p.o("vector", p2p3, waits=w + extra_p3_waits, incs=[(sP3, 1)])
            t2m[(kind, key)] = sP2.v
            t3m[(kind, key)] = sP3.v
            return sP3.v

        dump_fence = [0]
        tern_fence = [0]
        def dump_tern0(v3):
            if not dbg:
                return
            p.o("sync", lambda e: e.dma_start(dbgg[:, 0:512], wf[:, 0, :]),
                waits=[(sP3, v3)], incs=[(d4, 16)])
            p.o("sync", lambda e: e.dma_start(dbgg[:, 512:1024], wt1[:, 0, :]),
                waits=[(sP3, v3)], incs=[(d4, 16)])
            p.o("gpsimd", lambda e: e.dma_start(dbgw[:, 0:512], wt2[:, 0, :]),
                waits=[(sP3, v3)], incs=[(d4, 16)])
            tern_fence[0] = d4.v
            p.o("vector", lambda e: None, waits=[(d4, d4.v)], incs=[(sP1, 1)])
            t1m[("g", 0)] = sP1.v
        def dump_tern(ob, last_tern):
            if not (dbg and ob == 0):
                return
            p.o("sync", lambda e: e.dma_start(dbgw[:, 0:512], wgq[:, 0, :]),
                waits=[(sP3, last_tern)], incs=[(d4, 16)])
            p.o("sync", lambda e: e.dma_start(dbgw[:, 512:1024], wuq[:, 0, :]),
                waits=[(sP3, last_tern)], incs=[(d4, 16)])
            dump_fence[0] = d4.v

        def dump_gu(ob, oc, tg, sl, mmv):
            if not (dbg and ob == 0 and oc == 0 and tg == 0):
                return
            p.o("scalar", lambda e, sl=sl: e.activation(
                dnfdbg[:, :], pgs[sl][:, :], AF.Copy),
                waits=[(sMM, mmv)], incs=[(sSg, 1)])
            p.o("sync", lambda e: e.dma_start(dbgg[:, 0:512], dnfdbg[:, :]),
                waits=[(sSg, sSg.v)], incs=[(d4, 16)])
            p.o("scalar", lambda e, sl=sl: e.activation(
                dnfdbg[:, :], pus2[sl][:, :], AF.Copy),
                waits=[(sSg, sSg.v), (d4, d4.v)], incs=[(sSg, 1)])
            p.o("sync", lambda e: e.dma_start(dbgg[:, 512:1024], dnfdbg[:, :]),
                waits=[(sSg, sSg.v)], incs=[(d4, 16)])

        def pf3_body():
          jjd = 0
          for ob in range(16):
            for jr in range(16):
                for wi, (wsrc, invc, dst) in enumerate(
                        ((wgh, wcol[:, WIG:WIG + 1], wgq),
                         (wuh, wcol[:, WIU:WIU + 1], wuq))):
                    key = (ob * 16 + jr) * 2 + wi
                    sap = AP(wsrc, jr * FI + ob * 512,
                             [[16 * FI, 8], [128 * FI, 16], [1, 512]])
                    ew = []
                    if ob > 0:
                        ew.append((sMM, mm_ob[ob - 1]))
                    if dbg and ob == 1 and dump_fence[0]:
                        ew.append((d4, dump_fence[0]))
                    if dbg and key == 4 and tern_fence[0]:
                        ew.append((d4, tern_fence[0]))
                    emit_tern(sap, invc, dst[:, jr, :], 512, key % 4, "g", key,
                              4, ew)
                    if key == 0:
                        dump_tern0(sems["f4"].v)
            last_tern = sems["f4"].v
            dump_tern(ob, last_tern)
            for _ in range(4):
                jj = jjd
                jjd += 1
                sap = AP(wdh, jj * E, [[64 * E, 2], [128 * E, 64], [1, 2048]])
                ew = [(dS, stashm[jj - 2])] if jj >= 2 else []
                v = emit_tern(sap, wcol[:, WID:WID + 1], dwq[:, jj % 2, :], 2048,
                              jj % 2, "d", jj, 2, ew)
                p.o("gpsimd", lambda e, jj=jj: e.dma_start(
                    AP(wdqh, jj * 16384, [[128, 128], [16384 * 64, 16], [1, 128]]),
                    dwq[:, jj % 2, :].rearrange("p (a b) -> p a b", a=16)),
                    waits=[(sP3, v)], incs=[(dS, 16)])
                stashm[jj] = dS.v
            for oc in range(4):
                tau = ob * 4 + oc
                for tg in range(4):
                    it = tau * 4 + tg
                    sl = it % 2
                    fl = it % 4
                    def gmm(e, oc=oc, tg=tg, sl=sl, wq=wgq, ps=pgs):
                        ins = None
                        for jr in range(16):
                            ins = e.matmul(
                                ps[sl][:, :], wq[:, jr, oc * 128:oc * 128 + 128],
                                xq1[:, jr * TO + tg * 512: jr * TO + tg * 512 + 512],
                                start=(jr == 0), stop=(jr == 15))
                        return ins
                    w = [(sP3, last_tern)]
                    if it >= 2:
                        w.append((sSg, p3m[("sg", it - 2)]))
                    p.o("tensor", lambda e, gmm=gmm: gmm(e), waits=w, incs=[(sMM, 1)])
                    wu2 = []
                    if it >= 2:
                        wu2.append((sHb, p3m[("hb", it - 2)]))
                    p.o("tensor", lambda e, oc=oc, tg=tg, sl=sl: gmm(
                        e, oc=oc, tg=tg, sl=sl, wq=wuq, ps=pus2),
                        waits=wu2, incs=[(sMM, 1)])
                    mmv = sMM.v
                    wsg2 = [(sMM, mmv)]
                    if it >= 4:
                        wsg2.append((sHb, p3m[("hb", it - 4)]))
                    dump_gu(ob, oc, tg, sl, mmv)
                    p.o("scalar", lambda e, sl=sl, fl=fl: e.activation(
                        sg[:, fl, :], pgs[sl][:, :], AF.Silu,
                        scale=wcol[:, WSG:WSG + 1]), waits=wsg2, incs=[(sSg, 1)])
                    p3m[("sg", it)] = sSg.v
                    whb = [(sSg, sSg.v)]
                    if it >= 4:
                        whb.append((sMM, p3m[("ua", it - 4)]))
                    p.o("vector", lambda e, sl=sl, fl=fl: e.scalar_tensor_tensor(
                        hb[:, fl, :], pus2[sl][:, :], wcol[:, WSU:WSU + 1],
                        sg[:, fl, :], ALU.mult, ALU.mult), waits=whb, incs=[(sHb, 1)])
                    p3m[("hb", it)] = sHb.v
                    w = [(sHb, sHb.v)]
                    if it >= 2:
                        w.append((sUe, p3m[("ue", it - 2)]))
                    p.o("tensor", lambda e, sl=sl, fl=fl: e.matmul(
                        pUs[sl][:, :], H128s[:, :], hb[:, fl, :],
                        start=True, stop=True), waits=w, incs=[(sMM, 1)])
                    p3m[("ua", it)] = sMM.v
                    us = sems["u%d" % sl]
                    wue = [(sMM, sMM.v)]
                    if it >= 2:
                        wue.append((us, us.v))
                    p.o("scalar", lambda e, sl=sl: e.activation(
                        ub2[:, sl, :], pUs[sl][:, :], AF.Copy),
                        waits=wue, incs=[(sUe, 1)])
                    p3m[("ue", it)] = sUe.v
                    p.o("sync", lambda e, tau=tau, tg=tg, sl=sl: e.dma_start(
                        AP(u2h, tau * 64 * TO + tg * 512,
                           [[64 * 64 * TO, 2], [TO, 64], [1, 512]]),
                        ub2[:, sl, :].rearrange("p (g j) -> p g j", g=2)),
                        waits=[(sUe, sUe.v)], incs=[(us, 16)])
            mm_ob[ob] = sMM.v
        pf3_body()
        p.o("sync", lambda e: None,
            waits=[(sems["u0"], None), (sems["u1"], None), (dS, None),
                   (d4, None)])
        p.flush("pf3")
    fstck.close()

    if DEV_STAGE < 5:
        stck.close()
        return nc

    with (
        nc.sbuf_tensor("pdrb", [128, 64 * 256], BF16) as rb2,
        nc.sbuf_tensor("pdmx", [128, 1024], F32) as mx2,
        nc.sbuf_tensor("pdmc", [128, 32], F32) as mc2,
        nc.sbuf_tensor("pdqr", [128, 1024], F32) as qr2,
        nc.sbuf_tensor("pdtq", [128, 512], F32) as qtb,
        nc.sbuf_tensor("pdxq", [128, 64 * 1024], BF16) as xq2,
        nc.sbuf_tensor("pdwd", [128, 64 * 128], BF16) as wdt,
        nc.sbuf_tensor("pddn", [128, 2, 512], F32) as dnf,
        nc.sbuf_tensor("pdx1", [128, 2, 128], F32) as x1t,
        nc.sbuf_tensor("pdob", [128, 2, 128], F32) as obf,
        nc.psum_tensor("pdz0", [128, 512], F32) as qz0,
        nc.psum_tensor("pdz1", [128, 512], F32) as qz1,
        nc.psum_tensor("pdt0", [128, 128], F32) as qt0,
        nc.psum_tensor("pdt1", [128, 128], F32) as qt1,
        nc.psum_tensor("pdd0", [128, 512], F32) as pd0,
        nc.psum_tensor("pdd1", [128, 512], F32) as pd1,
    ):
        qzs, qts, pds = (qz0, qz1), (qt0, qt1), (pd0, pd1)
        sMM, sEv, sV, sQ2 = sems["mm"], sems["ev"], sems["sv"], sems["f1"]
        sEs, sOw, sTp = sems["es"], sems["ow"], sems["tp"]
        d4, d5, dL = sems["d4"], sems["d2"], sems["g1"]
        tpm = {}
        for half in range(2):
            hbase = half * 1024
            for t4 in range(4):
                t0 = hbase + t4 * 256
                w = [(sMM, sMM.v)]
                p.o("sync", lambda e, t0=t0: e.dma_start(
                    rb2[:, :].rearrange("p (j t) -> p j t", j=64),
                    AP(u2h, t0, [[64 * TO, 128], [TO, 64], [1, 256]])),
                    waits=w, incs=[(ld, 16)])
                for ch in range(32):
                    w = [(ld, None)]
                    if t4 * 32 + ch >= 2:
                        w.append((sV, sV.v - 1))
                    if t4 == 0 and ch < 2:
                        w.append((sQ2, sQ2.v))
                    def mmz(e, ch=ch):
                        rhs = AP(rb2, ch * 8, [[64 * 256, 128], [1, 8], [256, 64]])
                        return e.matmul(qzs[ch % 2][:, :].rearrange(
                            "p (t j) -> p t j", t=8), H64ps[:, :], rhs,
                            start=True, stop=True)
                    p.o("tensor", mmz, waits=w, incs=[(sMM, 1)])
                    p.o("vector", lambda e, ch=ch, t4=t4: e.tensor_reduce(
                        mx2[:, t4 * 256 + ch * 8: t4 * 256 + ch * 8 + 8],
                        qzs[ch % 2][:, :].rearrange("p (t j) -> p t j", t=8),
                        AX.X, ALU.max, apply_absolute_value=True),
                        waits=[(sMM, sMM.v)], incs=[(sV, 1)])
            for s8 in range(8):
                p.o("tensor", lambda e, s8=s8: e.transpose(
                    qts[s8 % 2][:, :], mx2[:, s8 * 128:s8 * 128 + 128], IDfs[:, :]),
                    waits=[(sV, sV.v)], incs=[(sTp, 1)])
                p.o("vector", lambda e, s8=s8: e.tensor_reduce(
                    mc2[:, s8:s8 + 1], qts[s8 % 2][:, :], AX.X, ALU.max),
                    waits=[(sTp, None)], incs=[(sV, 1)])
            def smath2(e):
                e.tensor_scalar(mc2[:, 8:16], mc2[:, 0:8], 1.0 / SQ8192, EPS_Q,
                                ALU.mult, ALU.max)
                e.drain()
                e.reciprocal(mc2[:, 16:24], mc2[:, 8:16])
                e.drain()
                e.tensor_scalar(mc2[:, 16:24], mc2[:, 16:24], 127.0 / SQ8192, None,
                                ALU.mult)
                e.tensor_scalar(mc2[:, 24:32], mc2[:, 8:16], 1.0 / 127.0, None,
                                ALU.mult)
                e.drain()
                return e.tensor_tensor(
                    mc2[:, 24:32], mc2[:, 24:32],
                    AP(wcol, WSD, [[6, 128], [0, 8]]), ALU.mult)
            p.o("vector", smath2, waits=[(sV, None)], incs=[(sV, 1)])
            p.o("sync", lambda e: e.dma_start(
                AP(scrh, 0, [[1, 128], [128, 8]]), mc2[:, 16:24],
                allow_slow_non_contiguous=True),
                waits=[(sV, None), (d4, None)], incs=[(d4, 16)])
            p.o("sync", lambda e: e.dma_start(
                qr2[:, :], AP(scrh, 0, [[0, 128], [1, 1024]])),
                waits=[(d4, d4.v), (sQ2, sQ2.v)], incs=[(d4, 16)])
            for t4 in range(4):
                t0 = hbase + t4 * 256
                p.o("sync", lambda e, t0=t0: e.dma_start(
                    rb2[:, :].rearrange("p (j t) -> p j t", j=64),
                    AP(u2h, t0, [[64 * TO, 128], [TO, 64], [1, 256]])),
                    waits=[(sMM, sMM.v)], incs=[(ld, 16)])
                for ch in range(32):
                    w = [(ld, None), (d4, None)]
                    if t4 * 32 + ch >= 2:
                        w.append((sQ2, sQ2.v - 1))
                    def mmz2(e, ch=ch):
                        rhs = AP(rb2, ch * 8, [[64 * 256, 128], [1, 8], [256, 64]])
                        return e.matmul(qzs[ch % 2][:, :].rearrange(
                            "p (t j) -> p t j", t=8), H64ps[:, :], rhs,
                            start=True, stop=True)
                    p.o("tensor", mmz2, waits=w, incs=[(sMM, 1)])
                    def qnt(e, ch=ch, t4=t4):
                        tloc = t4 * 256 + ch * 8
                        qsl = AP(qr2, tloc, [[1024, 128], [1, 8], [0, 64]])
                        e.tensor_tensor(
                            qtb[:, :].rearrange("p (t j) -> p t j", t=8),
                            qzs[ch % 2][:, :].rearrange("p (t j) -> p t j", t=8),
                            qsl, ALU.mult)
                        e.drain()
                        return e.tensor_scalar(
                            AP(xq2, tloc, [[64 * 1024, 128], [1, 8], [1024, 64]]),
                            qtb[:, :].rearrange("p (t j) -> p t j", t=8),
                            MAGIC, MAGIC, ALU.add, ALU.subtract)
                    p.o("vector", qnt, waits=[(sMM, sMM.v)], incs=[(sQ2, 1)])
            for o2c in range(16):
                sl = 0
                w = [(sQ2, sQ2.v), (sMM, sMM.v)]
                p.o("sync", lambda e, o2c=o2c, sl=sl: e.dma_start(
                    wdt[:, :], AP(wdqh, o2c * 64 * 16384,
                                      [[128, 128], [16384, 64], [1, 128]])),
                    waits=w, incs=[(dL, 16)])
                for t5 in range(2):
                    it = (half * 16 + o2c) * 2 + t5
                    dsl = it % 2
                    def dmm(e, sl=sl, t5=t5, dsl=dsl):
                        ins = None
                        for j in range(64):
                            ins = e.matmul(
                                pds[dsl][:, :], wdt[:, j * 128:j * 128 + 128],
                                xq2[:, j * 1024 + t5 * 512: j * 1024 + t5 * 512 + 512],
                                start=(j == 0), stop=(j == 63))
                        return ins
                    w = [(dL, None)]
                    if it >= 2:
                        w.append((sEv, sEv.v - 1))
                    p.o("tensor", dmm, waits=w, incs=[(sMM, 1)])
                    wev = [(sMM, sMM.v)]
                    if it >= 2:
                        wev.append((sTp, tpm[it - 2]))
                    p.o("vector", lambda e, dsl=dsl: e.tensor_copy(
                        dnf[:, dsl, :], pds[dsl][:, :]), waits=wev, incs=[(sEv, 1)])
                    evv = sEv.v
                    for s4 in range(4):
                        tb = t5 * 4 + s4
                        i2 = it * 4 + s4
                        trow = hbase + tb * 128
                        ys = sems["y%d" % (i2 % 2)]
                        wx = [(sEs, sEs.v - 1)] if i2 >= 2 else []
                        p.o("sync", lambda e, trow=trow, o2c=o2c, i2=i2: e.dma_start(
                            x1t[:, i2 % 2, :],
                            x1h[trow:trow + 128, o2c * 128:o2c * 128 + 128]),
                            waits=wx, incs=[(ys, 16)])
                        wt = [(sEv, evv)]
                        if i2 >= 2:
                            wt.append((sEs, sEs.v - 1))
                        p.o("tensor", lambda e, dsl=dsl, s4=s4, i2=i2: e.transpose(
                            qts[i2 % 2][:, :],
                            dnf[:, dsl, s4 * 128:s4 * 128 + 128], IDfs[:, :]),
                            waits=wt, incs=[(sTp, 1)])
                        if s4 == 3:
                            tpm[it] = sTp.v
                        osm = sems["o%d" % (i2 % 2)]
                        ws = [(sTp, sTp.v), (ys, ys.v)]
                        if i2 >= 2:
                            ws.append((osm, osm.v))
                        def sttd(e, i2=i2, tb=tb):
                            return e.scalar_tensor_tensor(
                                obf[:, i2 % 2, :], qts[i2 % 2][:, :],
                                mc2[:, 24 + tb:24 + tb + 1],
                                x1t[:, i2 % 2, :], ALU.mult, ALU.add)
                        p.o("vector", sttd, waits=ws, incs=[(sEs, 1)])
                        p.o("sync", lambda e, trow=trow, o2c=o2c, i2=i2: e.dma_start(
                            out_h[trow:trow + 128, o2c * 128:o2c * 128 + 128],
                            obf[:, i2 % 2, :]), waits=[(sEs, sEs.v)],
                            incs=[(osm, 16)])
        p.o("sync", lambda e: None,
            waits=[(sems["o0"], None), (sems["o1"], None)])
        p.flush("pd")

    stck.close()
    return nc


_CACHE = {}


def _mk_inputs(inputs):
    bf = ml_dtypes.bfloat16
    x = np.asarray(inputs["L0_attn_x"], dtype=np.float32)
    cons = {
        "cH128": _hadamard(128).astype(bf),
        "cH16p": np.kron(np.eye(8, dtype=np.float32), _hadamard(16)).astype(bf),
        "cH64p": np.kron(np.eye(2, dtype=np.float32), _hadamard(64)).astype(bf),
        "cIDf": np.eye(128, dtype=np.float32),
        "cIDb": np.eye(128, dtype=np.float32).astype(bf),
        "conesb": np.ones((128, 1), dtype=np.float32).astype(bf),
        "Wq": np.ascontiguousarray(np.asarray(inputs["L0_attn_Wq"], dtype=np.float32)),
        "Wk": np.ascontiguousarray(np.asarray(inputs["L0_attn_Wk"], dtype=np.float32)),
        "Wv": np.ascontiguousarray(np.asarray(inputs["L0_attn_Wv"], dtype=np.float32)),
        "bqT": np.ascontiguousarray(np.asarray(inputs["L0_attn_bq"], dtype=np.float32).T),
        "bkT": np.ascontiguousarray(np.asarray(inputs["L0_attn_bk"], dtype=np.float32).T),
        "bvT": np.ascontiguousarray(np.asarray(inputs["L0_attn_bv"], dtype=np.float32).T),
        "wgT": np.ascontiguousarray(np.asarray(inputs["L0_ffn_w_gate"], dtype=np.float32).T),
        "wuT": np.ascontiguousarray(np.asarray(inputs["L0_ffn_w_up"], dtype=np.float32).T),
        "wdT": np.ascontiguousarray(np.asarray(inputs["L0_ffn_w_down"], dtype=np.float32).T),
    }
    masks = {}
    for qb in (0, TO):
        cc = np.arange(MASK_W)[None, :] - MASK_Z + qb
        ii = np.arange(128)[:, None]
        masks[qb] = (cc >= ii).astype(np.float32).astype(bf)
    in_maps = []
    for c in range(8):
        b, half = c // 2, c % 2
        m = dict(cons)
        m["x_full"] = np.ascontiguousarray(x[b])
        m["x_own"] = np.ascontiguousarray(x[b, half * TO:(half + 1) * TO])
        m["maskC"] = masks[half * TO]
        in_maps.append(m)
    return in_maps


def _get_nc():
    if "nc" not in _CACHE:
        nc = bass.Bass("TRN2", target_bir_lowering=False)
        build(nc)
        mybir.codegen_inst_isa_subclasses(nc)
        _CACHE["nc"] = nc
    return _CACHE["nc"]


def kernel(**inputs):
    nc = _get_nc()
    in_maps = _mk_inputs(inputs)
    res = run_bass_kernel_spmd(nc, in_maps, core_ids=list(range(8)))
    out = np.empty((B, S, E), dtype=np.float32)
    for c in range(8):
        b, half = c // 2, c % 2
        out[b, half * TO:(half + 1) * TO] = res.results[c]["out"]
    return out
